# version 04
# baseline (speedup 1.0000x reference)
import contextlib
import numpy as np
import ml_dtypes
import concourse.bass as bass
import concourse.mybir as mybir
from concourse.bass_utils import run_bass_kernel_spmd

F32 = mybir.dt.float32
BF16 = mybir.dt.bfloat16
AF = mybir.ActivationFunctionType
ALU = mybir.AluOpType
AX = mybir.AxisListType

NCORES = 8
B = 2
T = 2048
D = 1024
DK = D // 128
LN_EPS = 1e-5

ENGINES = ("sync", "vector", "scalar", "tensor", "gpsimd")
_PAGE = 8192



class Op:
    __slots__ = ("idx", "eng", "fn", "reads", "writes", "is_dma",
                 "needs_inc", "inc_prefix", "waits", "tag", "pre_wait", "slow_wb",
                 "slow_rd")

    def __init__(self, idx, eng, fn, reads, writes, is_dma, tag=""):
        self.idx = idx
        self.eng = eng
        self.fn = fn
        self.reads = reads
        self.writes = writes
        self.is_dma = is_dma
        self.needs_inc = is_dma
        self.inc_prefix = 0
        self.waits = []
        self.tag = tag
        self.pre_wait = None
        self.slow_wb = False
        self.slow_rd = False


def _ovl(a, b):
    return (a[1] < b[2] and b[1] < a[2] and a[3] < b[4] and b[3] < a[4])


class Em:
    def __init__(self, nc):
        self.nc = nc
        self.ops = []
        self._w = {}
        self._r = {}
        self._edges = []
        self._qcount = {e: 0 for e in ENGINES}
        self._last_verify = {}

    @staticmethod
    def _pages(reg):
        return range(reg[3] // _PAGE, max(reg[4] - 1, reg[3]) // _PAGE + 1)

    def _append(self, o):
        o.idx = len(self.ops)
        self.ops.append(o)
        return o.idx

    def _mk_verify(self, q):
        cum = self._qcount[q]
        o = Op(0, q, (lambda e: e.nop()), [], [], False, tag=f"verify_{q}_{cum}")
        o.pre_wait = (("dma", q), 16 * cum)
        idx = self._append(o)
        self._edges.append(set())
        self._last_verify[q] = (cum, idx)
        return idx

    def op(self, eng, fn, reads=(), writes=(), is_dma=False, tag="", slow=False,
           slow_rd=False):
        o = Op(0, eng, fn, list(reads), list(writes), is_dma, tag)
        o.slow_wb = slow
        o.slow_rd = slow_rd
        deps = set()
        for r in o.reads:
            sp = self._w.get(r[0])
            if sp:
                for pg in self._pages(r):
                    for key, widx in sp.get(pg, {}).items():
                        if _ovl(r, (r[0],) + key):
                            deps.add(widx)
        for w in o.writes:
            sp = self._w.get(w[0])
            if sp:
                for pg in self._pages(w):
                    for key, widx in sp.get(pg, {}).items():
                        if _ovl(w, (w[0],) + key):
                            deps.add(widx)
            sp = self._r.get(w[0])
            if sp:
                for pg in self._pages(w):
                    for key, ridx in sp.get(pg, {}).items():
                        if _ovl(w, (w[0],) + key[:4]):
                            deps.add(ridx)
        ndeps = set()
        for j in deps:
            p = self.ops[j]
            if p.is_dma:
                q = p.eng
                if is_dma and q == eng:
                    continue
                lv = self._last_verify.get(q)
                if lv is None or lv[0] < p.inc_prefix:
                    ndeps.add(self._mk_verify(q))
                else:
                    ndeps.add(lv[1])
            else:
                ndeps.add(j)
        if is_dma:
            self._qcount[eng] += 1
            o.inc_prefix = self._qcount[eng]
        idx = self._append(o)
        ndeps.discard(idx)
        self._edges.append(ndeps)
        for w in o.writes:
            sp = self._w.setdefault(w[0], {})
            spr = self._r.setdefault(w[0], {})
            for pg in self._pages(w):
                d = sp.get(pg)
                if d:
                    for k in [k for k in d if _ovl(w, (w[0],) + k)]:
                        del d[k]
                dr = spr.get(pg)
                if dr:
                    for k in [k for k in dr if _ovl(w, (w[0],) + k[:4])]:
                        del dr[k]
                sp.setdefault(pg, {})[(w[1], w[2], w[3], w[4])] = idx
        for r in o.reads:
            sp = self._r.setdefault(r[0], {})
            for pg in self._pages(r):
                sp.setdefault(pg, {})[(r[1], r[2], r[3], r[4], eng)] = idx
        return o

    def finalize(self):
        nc = self.nc
        ops = self.ops

        def _skip(p, o):
            return (p.eng == o.eng and not p.is_dma
                    and p.eng in ("tensor", "gpsimd")
                    and not p.slow_wb and not o.slow_rd)

        for o in ops:
            for j in self._edges[o.idx]:
                p = ops[j]
                if _skip(p, o):
                    continue
                p.needs_inc = True
        dma_count = dict(self._qcount)
        inc_count = {e: 0 for e in ENGINES}
        eng_ops = {e: [] for e in ENGINES}
        for o in ops:
            eng_ops[o.eng].append(o)
            if not o.is_dma and o.needs_inc:
                inc_count[o.eng] += 1
                o.inc_prefix = inc_count[o.eng]
        guaranteed = {}
        for o in ops:
            ws = []
            if getattr(o, "pre_wait", None) is not None:
                ws.append(o.pre_wait)
            vals = []
            for j in self._edges[o.idx]:
                p = ops[j]
                if _skip(p, o):
                    continue
                assert not p.is_dma
                vals.append((("eng", p.eng), p.inc_prefix))
            for key, val in sorted(vals, key=lambda kv: -kv[1]):
                gk = (o.eng, key)
                if guaranteed.get(gk, 0) >= val:
                    continue
                guaranteed[gk] = val
                ws.append((key, val))
            o.waits = ws
        self._stack = contextlib.ExitStack()
        sems = {}
        for e in ENGINES:
            if inc_count[e]:
                sems[("eng", e)] = self._stack.enter_context(nc.semaphore(f"sp_{e}"))
            if dma_count[e]:
                sems[("dma", e)] = self._stack.enter_context(nc.semaphore(f"sq_{e}"))

        with nc.Block() as blk:
            def body(eng_obj, elist, ename):
                for o in elist:
                    for key, val in o.waits:
                        eng_obj.wait_ge(sems[key], val)
                    ins = o.fn(eng_obj)
                    if o.is_dma:
                        ins.then_inc(sems[("dma", o.eng)], 16)
                    elif o.needs_inc:
                        assert ins is not None, f"op {o.tag} returned None"
                        ins.then_inc(sems[("eng", o.eng)], 1)
                if ename == "sync":
                    for e in ENGINES:
                        if dma_count[e]:
                            eng_obj.wait_ge(sems[("dma", e)], 16 * dma_count[e])

            blk.sync(lambda e: body(e, eng_ops["sync"], "sync"))
            if eng_ops["vector"]:
                blk.vector(lambda e: body(e, eng_ops["vector"], "vector"))
            if eng_ops["scalar"]:
                blk.scalar(lambda e: body(e, eng_ops["scalar"], "scalar"))
            if eng_ops["tensor"]:
                blk.tensor(lambda e: body(e, eng_ops["tensor"], "tensor"))
            if eng_ops["gpsimd"]:
                blk.gpsimd(lambda e: body(e, eng_ops["gpsimd"], "gpsimd"))



_SB_UID = [0]


class SB:
    def __init__(self, nc, stack, name, cols, dtype, parts=128):
        _SB_UID[0] += 1
        self.h = stack.enter_context(
            nc.sbuf_tensor(f"{name}_{_SB_UID[0]}", [parts, cols], dtype))
        self.cols = cols
        self.parts = parts
        self.esz = 2 if dtype == BF16 else 4
        self.base = nc.lookup_mloc(self.h).addr

    def ap(self, p0=0, p1=None, c0=0, c1=None):
        p1 = self.parts if p1 is None else p1
        c1 = self.cols if c1 is None else c1
        return self.h[p0:p1, c0:c1]

    def reg(self, p0=0, p1=None, c0=0, c1=None):
        p1 = self.parts if p1 is None else p1
        c1 = self.cols if c1 is None else c1
        return ("SB", p0, p1, self.base + c0 * self.esz, self.base + c1 * self.esz)

    def at(self, p0=0, p1=None, c0=0, c1=None):
        return self.ap(p0, p1, c0, c1), self.reg(p0, p1, c0, c1)


class PSx:
    def __init__(self, nc, stack):
        self.h = stack.enter_context(nc.psum_tensor("PSALL", [128, 8 * 512], F32))

    def bank(self, b, c0=0, c1=512, p0=0, p1=128):
        return (self.h[p0:p1, b * 512 + c0: b * 512 + c1],
                ("PS", p0, p1, b, b + 1))

    def preg(self, b):
        return ("PS", 0, 128, b, b + 1)


class DR:
    def __init__(self, nc, name, shape, dtype, kind="Internal"):
        self.t = nc.dram_tensor(name, list(shape), dtype, kind=kind)
        self.name = name
        self.shape = tuple(shape)
        st = []
        s = 1
        for d in reversed(shape):
            st.insert(0, s)
            s *= d
        self.strides = st
        self.size = s

    def slc(self, *ranges):
        offset = 0
        dims = []
        norm = []
        for r in ranges:
            if isinstance(r, int):
                norm.append((r, r + 1, False))
            else:
                norm.append((r[0], r[1], True))
        assert len(norm) == len(self.shape), (self.name, ranges)
        for i, (a, b2, keep) in enumerate(norm):
            offset += a * self.strides[i]
            if keep:
                dims.append([self.strides[i], b2 - a])
        lo = offset
        hi = offset + 1
        for i, (a, b2, _) in enumerate(norm):
            hi += (b2 - a - 1) * self.strides[i]
        if not dims:
            dims = [[1, 1]]
        return bass.AP(self.t, offset, dims), ("D" + self.name, 0, 1, lo, hi)

    def wslice(self, lidx, din, c0, c1, nd=None):
        nd = (din // 128) if nd is None else nd
        dout = self.shape[-1]
        off = c0 + (lidx * self.strides[0] if lidx is not None else 0)
        dims = [[dout, 128], [128 * dout, nd], [1, c1 - c0]]
        lo = off
        hi = off + (nd * 128 - 1) * dout + (c1 - c0)
        return bass.AP(self.t, off, dims), ("D" + self.name, 0, 1, lo, hi)

    def bcast(self, parts, *ranges):
        ap, reg = self.slc(*ranges)
        dims = [list(d) for d in ap.ap if d[1] > 1]
        if not dims:
            dims = [[1, 1]]
        ap2 = bass.AP(self.t, ap.offset, [[0, parts]] + dims)
        return ap2, reg



class Kb:
    def __init__(self, nc, stack, em):
        self.nc = nc
        self.stack = stack
        self.em = em
        self.ps = PSx(nc, stack)
        self.ident = SB(nc, stack, "c_ident", 128, BF16)
        self.ones_col = SB(nc, stack, "c_ones", 1, BF16)
        self.ones_row = SB(nc, stack, "c_onesr", 128, BF16, parts=1)
        self.pcol = SB(nc, stack, "c_pcol", 1, F32)
        self._scr = SB(nc, stack, "c_scr", 128, F32)
        self.smal = SB(nc, stack, "c_smal", 2048, F32, parts=1)
        self.smal_bf = SB(nc, stack, "c_smalb", 2048, BF16, parts=1)
        self.c_sd = SB(nc, stack, "c_sd", 512, F32)
        self.c_rsb = SB(nc, stack, "c_rsb", 512, F32)
        self.ident_f = SB(nc, stack, "c_identf", 128, F32)
        em_ = em
        scr, ident, pcol = self._scr, self.ident, self.pcol
        em_.op("gpsimd", lambda e: e.iota(scr.ap(), [[1, 128]], channel_multiplier=0,
                                          allow_small_or_imprecise_dtypes=True),
               writes=[scr.reg()], tag="iota_j")
        em_.op("gpsimd", lambda e: e.iota(pcol.ap(), [[0, 1]], channel_multiplier=1,
                                          allow_small_or_imprecise_dtypes=True),
               writes=[pcol.reg()], tag="iota_p")
        em_.op("vector", lambda e: e.tensor_scalar(ident.ap(), scr.ap(), pcol.ap(),
                                                   None, ALU.is_equal),
               reads=[scr.reg(), pcol.reg()], writes=[ident.reg()], tag="ident")
        em_.op("vector", lambda e: e.tensor_scalar(self.ident_f.ap(), scr.ap(),
                                                   pcol.ap(), None, ALU.is_equal),
               reads=[scr.reg(), pcol.reg()], writes=[self.ident_f.reg()],
               tag="identf")
        em_.op("vector", lambda e: e.memset(self.ones_col.ap(), 1.0),
               writes=[self.ones_col.reg()], tag="ones_c")
        em_.op("vector", lambda e: e.memset(self.ones_row.ap(), 1.0),
               writes=[self.ones_row.reg()], tag="ones_r")
        self.constcol(1.0)
        self.constcol(LN_EPS)

    def load(self, queue, sbuf_tv, dram_tv, tag="ld"):
        (dap, dreg) = dram_tv
        (sap, sreg) = sbuf_tv

        def fn(eng):
            return eng.dma_start(out=sap, in_=dap)
        self.em.op(queue, fn, reads=[dreg], writes=[sreg], is_dma=True, tag=tag)

    def store(self, queue, dram_tv, sbuf_tv, tag="st", accum=False):
        (dap, dreg) = dram_tv
        (sap, sreg) = sbuf_tv
        kw = {"accum_op": ALU.add} if accum else {}

        def fn(eng):
            return eng.dma_start(out=dap, in_=sap, **kw)
        self.em.op(queue, fn, reads=[sreg] + ([dreg] if accum else []),
                   writes=[dreg], is_dma=True, tag=tag)

    def mm(self, out_tv, lhsT_tv, rhs_tv, start, stop, tag="mm"):
        (oap, oreg) = out_tv
        (lap, lreg) = lhsT_tv
        (rap, rreg) = rhs_tv

        def fn(eng):
            return eng.matmul(oap, lap, rap, start=start, stop=stop)
        self.em.op("tensor", fn, reads=[lreg, rreg], writes=[oreg], tag=tag)

    def mm_group(self, out_tv, pairs, tag="mmg"):
        (oap, oreg) = out_tv
        reads = []
        for (lap, lreg), (rap, rreg) in pairs:
            reads.extend([lreg, rreg])
        n = len(pairs)

        def fn(eng):
            ins = None
            for i, ((lap, _), (rap, _)) in enumerate(pairs):
                ins = eng.matmul(oap, lap, rap, start=(i == 0), stop=(i == n - 1))
            return ins
        self.em.op("tensor", fn, reads=reads, writes=[oreg], tag=tag)

    def constcol(self, val):
        key = float(val)
        if not hasattr(self, "_ccols"):
            self._ccols = {}
            self._cpool = SB(self.nc, self.stack, "c_cpool", 16, F32)
        if key not in self._ccols:
            i = len(self._ccols)
            assert i < 16
            self.memset("vector", self._cpool.at(0, 128, i, i + 1), key, tag=f"cset{i}")
            self._ccols[key] = i
        i = self._ccols[key]
        return self._cpool, i

    def act(self, func, out_tv, in_tv, bias=0.0, scale=1.0, extra=(), tag="act",
            flush=False):
        (oap, oreg) = out_tv
        (iap, ireg) = in_tv
        extra = list(extra)
        if isinstance(bias, float) and bias != 0.0:
            pool, i = self.constcol(bias)
            np_ = out_tv[1][2] - out_tv[1][1]
            bias = pool.ap(0, np_, i, i + 1)
            extra.append(pool.reg(0, 128, i, i + 1))

        def fn(eng):
            ins = eng.activation(oap, iap, func, bias=bias, scale=scale)
            if flush:
                ins = eng.drain()
            return ins
        self.em.op("scalar", fn, reads=[ireg] + extra, writes=[oreg], tag=tag)

    def tt(self, eng, op, out_tv, a_tv, b_tv, tag="tt"):
        (oap, oreg) = out_tv
        (aap, areg) = a_tv
        (bap, breg) = b_tv

        def fn(e):
            return e.tensor_tensor(oap, aap, bap, op=op)
        self.em.op(eng, fn, reads=[areg, breg], writes=[oreg], tag=tag)

    def ts(self, out_tv, in_tv, s1, op0, s2=None, op1=None, extra=(), tag="ts"):
        (oap, oreg) = out_tv
        (iap, ireg) = in_tv

        def fn(e):
            if s2 is None:
                return e.tensor_single_scalar(oap, iap, s1, op=op0)
            return e.tensor_scalar(oap, iap, s1, s2, op0, op1)
        self.em.op("vector", fn, reads=[ireg] + list(extra), writes=[oreg], tag=tag)

    def stt(self, out_tv, in0_tv, scalar, in1_tv, op0, op1, extra=(), tag="stt"):
        (oap, oreg) = out_tv
        (aap, areg) = in0_tv
        (bap, breg) = in1_tv

        def fn(e):
            return e.scalar_tensor_tensor(oap, aap, scalar, bap, op0, op1)
        self.em.op("vector", fn, reads=[areg, breg] + list(extra),
                   writes=[oreg], tag=tag)

    def copy(self, eng, out_tv, in_tv, tag="cp"):
        (oap, oreg) = out_tv
        (iap, ireg) = in_tv

        def fn(e):
            return e.tensor_copy(oap, iap)
        self.em.op(eng, fn, reads=[ireg], writes=[oreg], tag=tag)

    def memset(self, eng, tv, val, tag="ms"):
        (ap, reg) = tv

        def fn(e):
            return e.memset(ap, val)
        self.em.op(eng, fn, writes=[reg], tag=tag)

    def scan(self, out_tv, a_tv, b_tv, initial, extra=(), tag="scan"):
        (oap, oreg) = out_tv
        (aap, areg) = a_tv
        (bap, breg) = b_tv

        def fn(e):
            return e.tensor_tensor_scan(oap, aap, bap, initial, ALU.mult, ALU.add)
        self.em.op("vector", fn, reads=[areg, breg] + list(extra),
                   writes=[oreg], tag=tag, slow=True)

    def reduce(self, op, out_tv, in_tv, negate=False, tag="red"):
        (oap, oreg) = out_tv
        (iap, ireg) = in_tv

        def fn(e):
            return e.tensor_reduce(oap, iap, axis=AX.X, op=op, negate=negate)
        self.em.op("vector", fn, reads=[ireg], writes=[oreg], tag=tag, slow=True,
                   slow_rd=True)

    def recip(self, out_tv, in_tv, tag="recip"):
        (oap, oreg) = out_tv
        (iap, ireg) = in_tv

        def fn(e):
            return e.reciprocal(oap, iap)
        self.em.op("vector", fn, reads=[ireg], writes=[oreg], tag=tag, slow=True)

    def ln_stats(self, xf_tiles, Tc, banks, stage, tag="ln", n_feat=None):
        nd = len(xf_tiles)
        n = n_feat if n_feat is not None else 128 * nd
        sx = self.ps.bank(banks[0], 0, Tc, 0, 1)
        sxx = self.ps.bank(banks[0] + 1, 0, Tc, 0, 1)
        px, pxx = [], []
        ones = (self.ones_col.ap(), self.ones_col.reg())
        for i, xt in enumerate(xf_tiles):
            xbf = stage.at(0, 128, 2 * i * Tc, (2 * i + 1) * Tc)
            xsq = stage.at(0, 128, (2 * i + 1) * Tc, (2 * i + 2) * Tc)
            self.copy("vector", xbf, xt, tag=f"{tag}cv{i}")
            self.tt("gpsimd", ALU.mult, xsq, xt, xt, tag=f"{tag}sq{i}")
            px.append((ones, xbf))
            pxx.append((ones, xsq))
        self.mm_group(sx, px, tag=f"{tag}SX")
        self.mm_group(sxx, pxx, tag=f"{tag}SXX")
        sm = self.smal
        negmu = sm.at(0, 1, 0, Tc)
        mu2 = sm.at(0, 1, Tc, 2 * Tc)
        ex2 = sm.at(0, 1, 2 * Tc, 3 * Tc)
        var = sm.at(0, 1, 3 * Tc, 4 * Tc)
        self.ts(negmu, sx, -1.0 / n, ALU.mult, tag=f"{tag}mu")
        self.tt("vector", ALU.mult, mu2, negmu, negmu, tag=f"{tag}mu2")
        self.ts(ex2, sxx, 1.0 / n, ALU.mult, tag=f"{tag}ex2")
        self.tt("vector", ALU.subtract, var, ex2, mu2, tag=f"{tag}var")
        nb = self.smal_bf
        negmu_b0 = nb.at(0, 1, 0, Tc)
        var_b0 = nb.at(0, 1, Tc, 2 * Tc)
        self.copy("vector", negmu_b0, negmu, tag=f"{tag}mub")
        self.copy("vector", var_b0, var, tag=f"{tag}vrb")
        negmu_b = nb.at(0, 1, 2 * Tc, 3 * Tc)
        var_b = nb.at(0, 1, 3 * Tc, 4 * Tc)
        self.copy("gpsimd", negmu_b, negmu_b0, tag=f"{tag}mubr")
        self.copy("gpsimd", var_b, var_b0, tag=f"{tag}vrbr")
        mub = self.ps.bank(banks[1], 0, Tc)
        vbp = self.ps.bank(banks[1] + 1, 0, Tc)
        lhs1 = (self.ones_row.ap(0, 1, 0, 128), self.ones_row.reg())
        self.mm_group(mub, [(lhs1, negmu_b)], tag=f"{tag}BCmu")
        self.mm_group(vbp, [(lhs1, var_b)], tag=f"{tag}BCvar")
        sdw = self.c_sd.at(0, 128, 0, Tc)
        rsb = self.c_rsb.at(0, 128, 0, Tc)
        self.act(AF.Sqrt, sdw, vbp, bias=LN_EPS, tag=f"{tag}sd", flush=True)
        self.recip(rsb, sdw, tag=f"{tag}rs")
        if DBG.get("ln_tap") == tag:
            lt = DR(self.nc, "tap_ln", [8, Tc], F32, kind="ExternalOutput")
            st_t = DR(self.nc, "tap_stage", [128, 2 * len(xf_tiles) * Tc], BF16,
                      kind="ExternalOutput")
            row = self._lntap_row
            self.copy("vector", row.at(0, 1, 0, Tc), sx, tag="lt1")
            self.store("gpsimd", lt.slc((0, 1), (0, Tc)), row.at(0, 1, 0, Tc), tag="lt1s")
            self.copy("vector", row.at(0, 1, 0, Tc), sxx, tag="lt2")
            self.store("gpsimd", lt.slc((1, 2), (0, Tc)), row.at(0, 1, 0, Tc), tag="lt2s")
            self.store("gpsimd", lt.slc((2, 3), (0, Tc)), negmu, tag="lt3s")
            self.store("gpsimd", lt.slc((3, 4), (0, Tc)),
                       (rsb[0][0:1, 0:Tc], rsb[1]), tag="lt4s")
            self.store("gpsimd", st_t.slc((0, 128), (0, 2 * len(xf_tiles) * Tc)),
                       stage.at(0, 128, 0, 2 * len(xf_tiles) * Tc), tag="lt5s")
        return mub, rsb



def _f32(a):
    return np.ascontiguousarray(a, dtype=np.float32)


def host_prep_l0a(inp):
    out = {}
    out["l0a_inwT"] = _f32(np.asarray(inp["L0_attn_in_w"]).T)
    cw = np.asarray(inp["L0_attn_conv_w"])[:, 0, :]
    out["l0a_convw"] = _f32(cw.reshape(16, 128, 4).transpose(1, 0, 2).reshape(128, 64))
    out["l0a_convb"] = _f32(np.asarray(inp["L0_attn_conv_b"]).reshape(16, 128).T)
    xpT = np.asarray(inp["L0_attn_xproj_w"]).T
    pad = np.zeros((2048, 128), dtype=np.float32)
    pad[:, 0:16] = xpT[:, 64:80]
    pad[:, 32:48] = xpT[:, 80:96]
    pad[:, 64:128] = xpT[:, 0:64]
    out["l0a_xprojT"] = _f32(pad)
    out["l0a_dtwT"] = _f32(np.asarray(inp["L0_attn_dt_w"]).T)
    out["l0a_dtb"] = _f32(np.asarray(inp["L0_attn_dt_b"]).reshape(16, 128).T)
    al = np.asarray(inp["L0_attn_A_log"])
    out["l0a_Alog"] = _f32(al.reshape(16, 128, 16).transpose(1, 0, 2).reshape(128, 256))
    Dv = np.asarray(inp["L0_attn_D"], dtype=np.float32)
    dd = np.zeros((16, 128, 128), dtype=np.float32)
    i = np.arange(128)
    for dc in range(16):
        dd[dc, i, i] = Dv[dc * 128 + i]
    out["l0a_diagD"] = _f32(dd.transpose(1, 0, 2).reshape(128, 16 * 128))
    out["l0a_outwT"] = _f32(np.asarray(inp["L0_attn_out_w"]).T)
    out["l0a_lng"] = _f32(np.asarray(inp["L0_attn_norm_g"]).reshape(8, 128).T)
    out["l0a_lnb"] = _f32(np.asarray(inp["L0_attn_norm_b"]).reshape(8, 128).T)
    return out



def emit_l0a(kb, x_in, x_out, W):
    nc, st, em = kb.nc, kb.stack, kb.em
    ps = kb.ps
    DI = 2048
    NDC = 16
    NST = 16

    xs_sp = DR(nc, "l0a_xs", [B, D, T], F32)
    z_sp = DR(nc, "l0a_z", [B, DI, T], BF16)
    u32_sp = DR(nc, "l0a_u32", [B, DI, T], F32)
    ubf_sp = DR(nc, "l0a_ubf", [B, DI, T], BF16)
    dl_sp = DR(nc, "l0a_dl", [B, DI, T], F32)
    B_sp = DR(nc, "l0a_B", [B, NST, T], F32)
    C_sp = DR(nc, "l0a_C", [B, NST, T], F32)

    convw = SB(nc, st, "s0a_cw", 64, F32)
    convb = SB(nc, st, "s0a_cb", 16, F32)
    dtb = SB(nc, st, "sb0_dtb", 16, F32)
    lng = SB(nc, st, "s0a_g", 8, F32)
    lnb = SB(nc, st, "s0a_b", 8, F32)
    Alog = SB(nc, st, "s0a_Al", 256, F32)
    Acoef = SB(nc, st, "s0a_A", 256, F32)
    diagD_bf = SB(nc, st, "s0a_dD", NDC * 128, BF16)
    kb.load("sync", convw.at(), W["l0a_convw"].slc((0, 128), (0, 64)), tag="cw")
    kb.load("sync", convb.at(), W["l0a_convb"].slc((0, 128), (0, 16)), tag="cb")
    kb.load("sync", dtb.at(), W["l0a_dtb"].slc((0, 128), (0, 16)), tag="dtb")
    kb.load("sync", lng.at(), W["l0a_lng"].slc((0, 128), (0, 8)), tag="lng")
    kb.load("sync", lnb.at(), W["l0a_lnb"].slc((0, 128), (0, 8)), tag="lnb")
    kb.load("sync", Alog.at(), W["l0a_Alog"].slc((0, 128), (0, 256)), tag="alog")
    kb.act(AF.Exp, Acoef.at(), Alog.at(), tag="aexp")
    kb.ts(Acoef.at(), Acoef.at(), -1.0, ALU.mult, tag="aneg")

    TA = 512
    NCH = T // TA
    with contextlib.ExitStack() as sW:
        inw_bf = SB(nc, sW, "s0a_inw", DK * 4096, BF16)
        xproj_bf = SB(nc, sW, "s0a_xp", NDC * 128, BF16)
        dtw_bf = SB(nc, sW, "s0a_dtw", 2048, BF16)
        with contextlib.ExitStack() as s2:
            wst = SB(nc, s2, "l0a_wst", 4096, F32)
            for dk in range(DK):
                kb.load("sync", wst.at(),
                        W["l0a_inwT"].slc((dk * 128, (dk + 1) * 128), (0, 4096)),
                        tag=f"inw{dk}")
                kb.copy("vector", inw_bf.at(0, 128, dk * 4096, (dk + 1) * 4096),
                        wst.at(), tag=f"inwc{dk}")
            for dc in range(NDC):
                kb.load("sync", wst.at(0, 128, 0, 128),
                        W["l0a_xprojT"].slc((dc * 128, (dc + 1) * 128), (0, 128)),
                        tag=f"xp{dc}")
                kb.copy("vector", xproj_bf.at(0, 128, dc * 128, (dc + 1) * 128),
                        wst.at(0, 128, 0, 128), tag=f"xpc{dc}")
            kb.load("sync", wst.at(64, 128, 0, 2048),
                    W["l0a_dtwT"].slc((0, 64), (0, 2048)), tag="dtw")
            kb.copy("vector", dtw_bf.at(64, 128, 0, 2048), wst.at(64, 128, 0, 2048),
                    tag="dtwc")
            kb.load("sync", wst.at(0, 128, 0, NDC * 128),
                    W["l0a_diagD"].slc((0, 128), (0, NDC * 128)), tag="dD")
            kb.copy("vector", diagD_bf.at(), wst.at(0, 128, 0, NDC * 128), tag="dDc")

        with contextlib.ExitStack() as sA:
            x_sb = SB(nc, sA, "pa_x", DK * TA, F32)
            stage = SB(nc, sA, "pa_stage", 2 * DK * TA, BF16)
            xn_bf = SB(nc, sA, "pa_xn", DK * TA, BF16)
            xcv = SB(nc, sA, "pa_xcv", NDC * 520, F32)
            tail = SB(nc, sA, "pa_tail", NDC * 4, F32)
            ubf_sb = SB(nc, sA, "pa_ubf", NDC * TA, BF16)
            u32st = SB(nc, sA, "pa_u32", TA, F32)
            zst = SB(nc, sA, "pa_z", TA, BF16)
            ts1 = SB(nc, sA, "pa_t1", TA, F32)
            ts2 = SB(nc, sA, "pa_t2", TA, F32)
            draw_bf = SB(nc, sA, "pa_draw", TA, BF16)
            bc_st = SB(nc, sA, "pa_bc", 2 * TA, F32, parts=48)
            dsoft = SB(nc, sA, "pa_ds", 2 * TA, F32)
            dst = SB(nc, sA, "pa_dst", TA, F32)

            for b in range(B):
                for dc in range(NDC):
                    kb.memset("gpsimd", xcv.at(0, 128, dc * 520, dc * 520 + 3), 0.0,
                              tag=f"h0_{b}_{dc}")
                for c in range(NCH):
                    t0 = c * TA
                    for dk in range(DK):
                        kb.load("sync", x_sb.at(0, 128, dk * TA, (dk + 1) * TA),
                                x_in.slc(b, (dk * 128, (dk + 1) * 128), (t0, t0 + TA)),
                                tag=f"Ax{b}_{c}_{dk}")
                    xt = [x_sb.at(0, 128, dk * TA, (dk + 1) * TA) for dk in range(DK)]
                    mub, rsb = kb.ln_stats(xt, TA, (0, 2), stage, tag=f"Aln{b}{c}")
                    for dk in range(DK):
                        xsd = x_sb.at(0, 128, dk * TA, (dk + 1) * TA)
                        kb.tt("vector", ALU.add, ts1.at(), xt[dk], mub, tag=f"Aa{b}{c}{dk}")
                        kb.tt("vector", ALU.mult, xsd, ts1.at(), rsb, tag=f"Am{b}{c}{dk}")
                        kb.store("gpsimd",
                                 xs_sp.slc(b, (dk * 128, (dk + 1) * 128), (t0, t0 + TA)),
                                 xsd, tag=f"Axs{b}{c}{dk}")
                    xst = [x_sb.at(0, 128, dk * TA, (dk + 1) * TA) for dk in range(DK)]
                    mub2, rsb2 = kb.ln_stats(xst, TA, (0, 2), stage, tag=f"Al2{b}{c}")
                    for dk in range(DK):
                        kb.tt("vector", ALU.add, ts1.at(), xst[dk], mub2, tag=f"A2a{b}{c}{dk}")
                        kb.tt("vector", ALU.mult, ts2.at(), ts1.at(), rsb2, tag=f"A2m{b}{c}{dk}")
                        kb.ts(xn_bf.at(0, 128, dk * TA, (dk + 1) * TA), ts2.at(),
                              lng.ap(0, 128, dk, dk + 1), ALU.mult,
                              lnb.ap(0, 128, dk, dk + 1), ALU.add,
                              extra=[lng.reg(), lnb.reg()], tag=f"A2g{b}{c}{dk}")
                    for oc in range(32):
                        pt = ps.bank(4 + (oc % 4), 0, TA)
                        pairs = [((inw_bf.ap(0, 128, dk * 4096 + oc * 128,
                                             dk * 4096 + (oc + 1) * 128),
                                   inw_bf.reg(0, 128, dk * 4096 + oc * 128,
                                              dk * 4096 + (oc + 1) * 128)),
                                  xn_bf.at(0, 128, dk * TA, (dk + 1) * TA))
                                 for dk in range(DK)]
                        kb.mm_group(pt, pairs, tag=f"Aip{b}{c}{oc}")
                        if oc < 16:
                            dc = oc
                            kb.copy("vector", xcv.at(0, 128, dc * 520 + 3, dc * 520 + 515),
                                    pt, tag=f"Axc{b}{c}{oc}")
                        else:
                            zi = oc - 16
                            kb.copy("vector", zst.at(0, 128, 0, TA),
                                    pt, tag=f"Azc{b}{c}{oc}")
                            kb.store("gpsimd",
                                     z_sp.slc(b, (zi * 128, (zi + 1) * 128), (t0, t0 + TA)),
                                     zst.at(0, 128, 0, TA),
                                     tag=f"Azs{b}{c}{oc}")
                    for dc in range(NDC):
                        base = dc * 520
                        xk = [xcv.at(0, 128, base + k, base + k + 512) for k in range(4)]
                        wcol = [convw.ap(0, 128, dc * 4 + k, dc * 4 + k + 1) for k in range(4)]
                        wr = [convw.reg()]
                        kb.ts(ts1.at(), xk[0], wcol[0], ALU.mult, extra=wr, tag=f"Ac0{b}{c}{dc}")
                        kb.stt(ts2.at(), xk[1], wcol[1], ts1.at(), ALU.mult, ALU.add,
                               extra=wr, tag=f"Ac1{b}{c}{dc}")
                        kb.stt(ts1.at(), xk[2], wcol[2], ts2.at(), ALU.mult, ALU.add,
                               extra=wr, tag=f"Ac2{b}{c}{dc}")
                        kb.stt(ts2.at(), xk[3], wcol[3], ts1.at(), ALU.mult, ALU.add,
                               extra=wr, tag=f"Ac3{b}{c}{dc}")
                        kb.copy("gpsimd", tail.at(0, 128, dc * 4, dc * 4 + 3),
                                xcv.at(0, 128, base + 512, base + 515), tag=f"Atl{b}{c}{dc}")
                        kb.copy("gpsimd", xcv.at(0, 128, base, base + 3),
                                tail.at(0, 128, dc * 4, dc * 4 + 3), tag=f"Ath{b}{c}{dc}")
                        u32 = u32st.at(0, 128, 0, TA)
                        kb.act(AF.Silu, u32, ts2.at(), bias=convb.ap(0, 128, dc, dc + 1),
                               extra=[convb.reg()], tag=f"Asl{b}{c}{dc}")
                        kb.copy("vector", ubf_sb.at(0, 128, dc * TA, (dc + 1) * TA), u32,
                                tag=f"Aub{b}{c}{dc}")
                        kb.store("gpsimd",
                                 u32_sp.slc(b, (dc * 128, (dc + 1) * 128), (t0, t0 + TA)),
                                 u32, tag=f"Au3{b}{c}{dc}")
                        kb.store("gpsimd",
                                 ubf_sp.slc(b, (dc * 128, (dc + 1) * 128), (t0, t0 + TA)),
                                 ubf_sb.at(0, 128, dc * TA, (dc + 1) * TA),
                                 tag=f"Aus{b}{c}{dc}")
                    pt = ps.bank(0, 0, TA)
                    pairs = [((xproj_bf.ap(0, 128, dc * 128, (dc + 1) * 128),
                               xproj_bf.reg(0, 128, dc * 128, (dc + 1) * 128)),
                              ubf_sb.at(0, 128, dc * TA, (dc + 1) * TA))
                             for dc in range(NDC)]
                    kb.mm_group(pt, pairs, tag=f"Axp{b}{c}")
                    kb.copy("vector", bc_st.at(0, 16, 0, TA),
                            (ps.h[0:16, 0:TA], ps.preg(0)), tag=f"AB{b}{c}")
                    kb.copy("vector", bc_st.at(32, 48, TA, 2 * TA),
                            (ps.h[32:48, 0:TA], ps.preg(0)), tag=f"AC{b}{c}")
                    kb.copy("vector", draw_bf.at(64, 128, 0, TA),
                            (ps.h[64:128, 0:TA], ps.preg(0)), tag=f"AR{b}{c}")
                    kb.store("gpsimd", B_sp.slc(b, (0, 16), (t0, t0 + TA)),
                             bc_st.at(0, 16, 0, TA), tag=f"ABs{b}{c}")
                    kb.store("gpsimd", C_sp.slc(b, (0, 16), (t0, t0 + TA)),
                             bc_st.at(32, 48, TA, 2 * TA), tag=f"ACs{b}{c}")
                    for di in range(NDC):
                        pt = ps.bank(1 + (di % 2), 0, TA)
                        kb.mm_group(pt, [((dtw_bf.ap(64, 128, di * 128, (di + 1) * 128),
                                           dtw_bf.reg(64, 128, di * 128, (di + 1) * 128)),
                                          draw_bf.at(64, 128, 0, TA))],
                                    tag=f"Adt{b}{c}{di}")
                        dbcol = dtb.ap(0, 128, di, di + 1)
                        dbr = [dtb.reg()]
                        a1 = dsoft.at(0, 128, 0, TA)
                        a2 = dsoft.at(0, 128, TA, 2 * TA)
                        kb.act(AF.Abs, a1, pt, bias=dbcol, extra=dbr, tag=f"Aab{b}{c}{di}")
                        kb.act(AF.Exp, a2, a1, scale=-1.0, tag=f"Aex{b}{c}{di}")
                        kb.act(AF.Ln, a1, a2, bias=1.0, tag=f"Al1{b}{c}{di}")
                        kb.act(AF.Relu, a2, pt, bias=dbcol, extra=dbr, tag=f"Arl{b}{c}{di}")
                        dd = dst.at(0, 128, 0, TA)
                        kb.tt("vector", ALU.add, dd, a1, a2, tag=f"Add{b}{c}{di}")
                        kb.store("gpsimd",
                                 dl_sp.slc(b, (di * 128, (di + 1) * 128), (t0, t0 + TA)),
                                 dd, tag=f"Ads{b}{c}{di}")

    outw_bf = SB(nc, st, "s0a_ow", NDC * 1024, BF16)
    with contextlib.ExitStack() as s2:
        wst = SB(nc, s2, "l0a_wst4", 1024, F32)
        for dc in range(NDC):
            kb.load("sync", wst.at(),
                    W["l0a_outwT"].slc((dc * 128, (dc + 1) * 128), (0, 1024)),
                    tag=f"ow{dc}")
            kb.copy("vector", outw_bf.at(0, 128, dc * 1024, (dc + 1) * 1024), wst.at(),
                    tag=f"owc{dc}")

    TB = 512
    NTC = T // TB
    with contextlib.ExitStack() as sB:
        bbc = SB(nc, sB, "pb_B", NST * TB, BF16)
        cbc = SB(nc, sB, "pb_C", NST * TB, BF16)
        bcst = SB(nc, sB, "pb_bst", 2 * TB, F32)
        dl_sb = SB(nc, sB, "pb_dl", 2 * TB, F32)
        u32_sb = SB(nc, sB, "pb_u32", 2 * TB, F32)
        ubf_sb2 = SB(nc, sB, "pb_ubf", 2 * TB, BF16)
        zbf_sb = SB(nc, sB, "pb_z", 2 * TB, BF16)
        du_sb = SB(nc, sB, "pb_du", 2 * TB, F32)
        sz_sb = SB(nc, sB, "pb_sz", 2 * TB, F32)
        dA_sb = SB(nc, sB, "pb_dA", 2 * TB, F32)
        dBu_sb = SB(nc, sB, "pb_dBu", 2 * TB, F32)
        h_sb = SB(nc, sB, "pb_h", 2 * TB, F32)
        yt_sb = SB(nc, sB, "pb_yt", 2 * TB, BF16)
        hcar = SB(nc, sB, "pb_hc", NDC * NST, F32)
        ygat = SB(nc, sB, "pb_yg", NDC * TB, BF16)
        o_sb = SB(nc, sB, "pc_o", DK * TB, F32)
        xr_sb = SB(nc, sB, "pc_x", DK * TB, F32)
        xsr_sb = SB(nc, sB, "pc_xs", TB, F32)
        stage2 = SB(nc, sB, "pc_stage", 2 * DK * TB, BF16)
        tc1 = SB(nc, sB, "pc_t1", TB, F32)
        tc2 = SB(nc, sB, "pc_t2", TB, F32)

        for b in range(B):
            for tc in range(NTC):
                t0 = tc * TB
                for n in range(NST):
                    np_ = n % 2
                    kb.load("scalar", bcst.at(0, 128, np_ * TB, (np_ + 1) * TB),
                            B_sp.bcast(128, b, (n, n + 1), (t0, t0 + TB)),
                            tag=f"BB{b}{tc}{n}")
                    kb.copy("vector", bbc.at(0, 128, n * TB, (n + 1) * TB),
                            bcst.at(0, 128, np_ * TB, (np_ + 1) * TB), tag=f"BBc{b}{tc}{n}")
                    kb.load("scalar", bcst.at(0, 128, np_ * TB, (np_ + 1) * TB),
                            C_sp.bcast(128, b, (n, n + 1), (t0, t0 + TB)),
                            tag=f"BC{b}{tc}{n}")
                    kb.copy("vector", cbc.at(0, 128, n * TB, (n + 1) * TB),
                            bcst.at(0, 128, np_ * TB, (np_ + 1) * TB), tag=f"BCc{b}{tc}{n}")
                for dc in range(NDC):
                    dp = dc % 2
                    dcs = (dc * 128, (dc + 1) * 128)
                    dls = dl_sb.at(0, 128, dp * TB, (dp + 1) * TB)
                    u32s = u32_sb.at(0, 128, dp * TB, (dp + 1) * TB)
                    ubfs = ubf_sb2.at(0, 128, dp * TB, (dp + 1) * TB)
                    zbfs = zbf_sb.at(0, 128, dp * TB, (dp + 1) * TB)
                    dus = du_sb.at(0, 128, dp * TB, (dp + 1) * TB)
                    szs = sz_sb.at(0, 128, dp * TB, (dp + 1) * TB)
                    kb.load("sync", dls, dl_sp.slc(b, dcs, (t0, t0 + TB)), tag=f"Bd{b}{tc}{dc}")
                    kb.load("sync", u32s, u32_sp.slc(b, dcs, (t0, t0 + TB)), tag=f"Bu{b}{tc}{dc}")
                    kb.load("sync", ubfs, ubf_sp.slc(b, dcs, (t0, t0 + TB)), tag=f"Bv{b}{tc}{dc}")
                    kb.load("sync", zbfs, z_sp.slc(b, dcs, (t0, t0 + TB)), tag=f"Bz{b}{tc}{dc}")
                    kb.tt("gpsimd", ALU.mult, dus, dls, u32s, tag=f"Bdu{b}{tc}{dc}")
                    kb.act(AF.Silu, szs, zbfs, tag=f"Bsz{b}{tc}{dc}")
                    pt = ps.bank(dc % 2, 0, TB)
                    for n in range(NST):
                        pp = n % 2
                        dA = dA_sb.at(0, 128, pp * TB, (pp + 1) * TB)
                        dBu = dBu_sb.at(0, 128, pp * TB, (pp + 1) * TB)
                        hh = h_sb.at(0, 128, pp * TB, (pp + 1) * TB)
                        yt = yt_sb.at(0, 128, pp * TB, (pp + 1) * TB)
                        kb.act(AF.Exp, dA, dls,
                               scale=Acoef.ap(0, 128, dc * 16 + n, dc * 16 + n + 1),
                               extra=[Acoef.reg()], tag=f"BdA{b}{tc}{dc}_{n}")
                        e1 = "gpsimd" if (n % 4 != 3) else "vector"
                        kb.tt(e1, ALU.mult, dBu, dus,
                              bbc.at(0, 128, n * TB, (n + 1) * TB), tag=f"BdB{b}{tc}{dc}_{n}")
                        hc = (hcar.ap(0, 128, dc * NST + n, dc * NST + n + 1),
                              hcar.reg(0, 128, dc * NST + n, dc * NST + n + 1))
                        if tc == 0:
                            kb.scan(hh, dA, dBu, 0.0, tag=f"Bsc{b}{tc}{dc}_{n}")
                        else:
                            kb.scan(hh, dA, dBu, hc[0], extra=[hc[1]],
                                    tag=f"Bsc{b}{tc}{dc}_{n}")
                        if tc < NTC - 1:
                            kb.copy("gpsimd", hc,
                                    (h_sb.ap(0, 128, (pp + 1) * TB - 1, (pp + 1) * TB),
                                     h_sb.reg(0, 128, (pp + 1) * TB - 1, (pp + 1) * TB)),
                                    tag=f"Bhc{b}{tc}{dc}_{n}")
                        e2 = "vector" if (n % 4 != 1) else "gpsimd"
                        kb.tt(e2, ALU.mult, yt, hh,
                              cbc.at(0, 128, n * TB, (n + 1) * TB), tag=f"Byt{b}{tc}{dc}_{n}")
                        kb.mm(pt, (kb.ident.ap(), kb.ident.reg()), yt,
                              start=(n == 0), stop=False, tag=f"Bya{b}{tc}{dc}_{n}")
                    kb.mm(pt, (diagD_bf.ap(0, 128, dc * 128, (dc + 1) * 128),
                               diagD_bf.reg(0, 128, dc * 128, (dc + 1) * 128)),
                          ubfs, start=False, stop=True, tag=f"BDu{b}{tc}{dc}")
                    kb.tt("vector", ALU.mult, ygat.at(0, 128, dc * TB, (dc + 1) * TB),
                          pt, szs, tag=f"Byg{b}{tc}{dc}")

                for dk in range(DK):
                    kb.load("sync", xr_sb.at(0, 128, dk * TB, (dk + 1) * TB),
                            x_in.slc(b, (dk * 128, (dk + 1) * 128), (t0, t0 + TB)),
                            tag=f"Cx{b}{tc}{dk}")
                for oc in range(DK):
                    pt = ps.bank(2 + oc % 2, 0, TB)
                    pairs = [((outw_bf.ap(0, 128, dc * 1024 + oc * 128,
                                          dc * 1024 + (oc + 1) * 128),
                               outw_bf.reg(0, 128, dc * 1024 + oc * 128,
                                           dc * 1024 + (oc + 1) * 128)),
                              ygat.at(0, 128, dc * TB, (dc + 1) * TB))
                             for dc in range(NDC)]
                    kb.mm_group(pt, pairs, tag=f"Cop{b}{tc}{oc}")
                    kb.load("scalar", xsr_sb.at(),
                            xs_sp.slc(b, (oc * 128, (oc + 1) * 128), (t0, t0 + TB)),
                            tag=f"Cxs{b}{tc}{oc}")
                    kb.tt("vector", ALU.add, o_sb.at(0, 128, oc * TB, (oc + 1) * TB),
                          pt, xsr_sb.at(), tag=f"Co{b}{tc}{oc}")
                ot = [o_sb.at(0, 128, dk * TB, (dk + 1) * TB) for dk in range(DK)]
                mub, rsb = kb.ln_stats(ot, TB, (4, 6), stage2, tag=f"Cln{b}{tc}")
                for dk in range(DK):
                    kb.tt("vector", ALU.add, tc1.at(), ot[dk], mub, tag=f"Ca{b}{tc}{dk}")
                    kb.tt("vector", ALU.mult, tc2.at(), tc1.at(), rsb, tag=f"Cm{b}{tc}{dk}")
                    kb.tt("vector", ALU.add, xr_sb.at(0, 128, dk * TB, (dk + 1) * TB),
                          xr_sb.at(0, 128, dk * TB, (dk + 1) * TB), tc2.at(),
                          tag=f"Cr{b}{tc}{dk}")
                    kb.store("gpsimd",
                             x_out.slc(b, (dk * 128, (dk + 1) * 128), (t0, t0 + TB)),
                             xr_sb.at(0, 128, dk * TB, (dk + 1) * TB), tag=f"Cst{b}{tc}{dk}")




def host_prep_l0f(inp):
    out = {}
    out["l0f_wqkv"] = _f32(inp["L0_ffn_wqkv"])
    out["l0f_wo"] = _f32(inp["L0_ffn_wo"])
    qs = np.asarray(inp["L0_ffn_q_scale"])
    ks = np.asarray(inp["L0_ffn_k_scale"])
    p = np.arange(128)
    out["l0f_qs"] = _f32(qs[:, p % 64].T)
    out["l0f_ks"] = _f32(ks[:, p % 64].T)
    lg = np.asarray(inp["L0_ffn_ln_g"])
    lb = np.asarray(inp["L0_ffn_ln_b"])
    out["l0f_lng"] = _f32(np.stack([lg[l].reshape(8, 128).T for l in range(2)], -1)
                          .reshape(128, 16))
    out["l0f_lnb"] = _f32(np.stack([lb[l].reshape(8, 128).T for l in range(2)], -1)
                          .reshape(128, 16))
    HID = 2730
    HP = 2816
    w1 = np.asarray(inp["L0_ffn_w1"])
    b1 = np.asarray(inp["L0_ffn_b1"])
    w1p = np.zeros((2, 1024, 2 * HP), dtype=np.float32)
    w1p[:, :, 0:HID] = w1[:, :, 0:HID]
    w1p[:, :, HP:HP + HID] = w1[:, :, HID:2 * HID]
    out["l0f_w1p"] = _f32(w1p)
    b1p = np.zeros((2, 2 * HP), dtype=np.float32)
    b1p[:, 0:HID] = b1[:, 0:HID]
    b1p[:, HP:HP + HID] = b1[:, HID:2 * HID]
    nb = np.zeros((128, 88), dtype=np.float32)
    for l in range(2):
        for oc in range(44):
            nb[:, l * 44 + oc] = b1p[l, oc * 128:(oc + 1) * 128]
    out["l0f_b1p"] = _f32(nb)
    w2 = np.asarray(inp["L0_ffn_w2"])
    w2p = np.zeros((2, HP, 1024), dtype=np.float32)
    w2p[:, 0:HID, :] = w2
    out["l0f_w2p"] = _f32(w2p)
    b2 = np.asarray(inp["L0_ffn_b2"])
    out["l0f_b2"] = _f32(np.stack([b2[l].reshape(8, 128).T for l in range(2)], -1)
                         .reshape(128, 16))
    out["cst_e2"] = _f32((np.arange(128)[:, None] // 64 == np.arange(2)[None, :]))
    out["cst_e2b"] = _f32((np.arange(2)[:, None] == np.arange(128)[None, :] // 64))
    R = np.zeros((128, 128), dtype=np.float32)
    for blk in range(2):
        o = blk * 64
        for dd in range(32):
            R[o + dd, o + dd + 32] = -1.0
            R[o + dd + 32, o + dd] = 1.0
    out["cst_rot"] = _f32(R.T)
    DIM_HEAD = 64
    WINDOW = 512
    half = np.arange(0, DIM_HEAD, 2, dtype=np.float64)
    inv_freq = 1.0 / 10000.0 ** (half / DIM_HEAD)
    t = np.arange(T, dtype=np.float64)
    freqs = t[:, None] * inv_freq[None, :]
    freqs = np.concatenate([freqs, freqs], -1)
    base = (half + 0.4 * DIM_HEAD) / (1.4 * DIM_HEAD)
    power = (t - T // 2) / float(WINDOW // 2)
    scale = base[None, :] ** power[:, None]
    scale = np.concatenate([scale, scale], -1)
    cos = np.cos(freqs)
    sin = np.sin(freqs)
    p64 = np.arange(128) % 64
    out["l0f_qcs"] = _f32((cos * scale * 8.0).T[p64])
    out["l0f_qss"] = _f32((sin * scale * 8.0).T[p64])
    out["l0f_kcs"] = _f32((cos / scale).T[p64])
    out["l0f_kss"] = _f32((sin / scale).T[p64])
    W_ = 512
    il = np.arange(W_)
    jl = np.arange(2 * W_)
    masks = np.zeros((2, W_, 2 * W_), dtype=np.float32)
    for wv in range(2):
        w = 1 if wv else 0
        ig = w * W_ + il
        jg = (w - 1) * W_ + jl
        valid = np.ones((W_, 2 * W_), dtype=bool)
        jgb = np.broadcast_to(jg[None, :], (W_, 2 * W_))
        igb = np.broadcast_to(ig[:, None], (W_, 2 * W_))
        bad = (igb < jgb) | (igb > jgb + W_) | (jgb < 0)
        masks[wv][bad] = -1e30
    out["l0f_mask"] = _f32(masks)
    return out


DBG = {}


def emit_l0f(kb, x_in, x_out, W):
    nc, st, em = kb.nc, kb.stack, kb.em
    ps = kb.ps
    NH = 8
    NW = 4
    WS = 512
    HPK = 22
    TC = 512
    NCH = T // TC
    n_layers = DBG.get("l0f_layers", 2)
    skip_attn = DBG.get("l0f_skip_attn", False)
    skip_ffn = DBG.get("l0f_skip_ffn", False)

    ffs = DR(nc, "l0f_ffs", [B, D, T], F32,
             kind="ExternalOutput" if DBG.get("l0f_tap") else "Internal")

    e2 = SB(nc, st, "f_e2", 2, BF16)
    e2b = SB(nc, st, "f_e2b", 128, BF16, parts=2)
    rotm = SB(nc, st, "f_rot", 128, BF16)
    qscol = SB(nc, st, "f_qs", 2, F32)
    kscol = SB(nc, st, "f_ks", 2, F32)
    lngc = SB(nc, st, "f_lng", 16, F32)
    lnbc = SB(nc, st, "f_lnb", 16, F32)
    b1c = SB(nc, st, "f_b1", 88, F32)
    b2c = SB(nc, st, "f_b2", 16, F32)
    with contextlib.ExitStack() as s2:
        wst = SB(nc, s2, "f_wst0", 128, F32)
        kb.load("sync", wst.at(0, 128, 0, 2), W["cst_e2"].slc((0, 128), (0, 2)), tag="e2")
        kb.copy("vector", e2.at(), wst.at(0, 128, 0, 2), tag="e2c")
        kb.load("sync", wst.at(0, 2, 0, 128), W["cst_e2b"].slc((0, 2), (0, 128)), tag="e2b")
        kb.copy("vector", e2b.at(0, 2, 0, 128), wst.at(0, 2, 0, 128), tag="e2bc")
        kb.load("sync", wst.at(), W["cst_rot"].slc((0, 128), (0, 128)), tag="rot")
        kb.copy("vector", rotm.at(), wst.at(), tag="rotc")
    kb.load("sync", qscol.at(), W["l0f_qs"].slc((0, 128), (0, 2)), tag="qs")
    kb.load("sync", kscol.at(), W["l0f_ks"].slc((0, 128), (0, 2)), tag="ks")
    kb.load("sync", lngc.at(), W["l0f_lng"].slc((0, 128), (0, 16)), tag="flng")
    kb.load("sync", lnbc.at(), W["l0f_lnb"].slc((0, 128), (0, 16)), tag="flnb")
    kb.load("sync", b1c.at(), W["l0f_b1p"].slc((0, 128), (0, 88)), tag="fb1")
    kb.load("sync", b2c.at(), W["l0f_b2"].slc((0, 128), (0, 16)), tag="fb2")

    with contextlib.ExitStack() as s0:
        x_sb = SB(nc, s0, "f0_x", DK * TC, F32)
        stage = SB(nc, s0, "f0_st", 2 * DK * TC, BF16)
        tt1 = SB(nc, s0, "f0_t1", TC, F32)
        for b in range(B):
            for c in range(NCH):
                t0 = c * TC
                for dk in range(DK):
                    kb.load("sync", x_sb.at(0, 128, dk * TC, (dk + 1) * TC),
                            x_in.slc(b, (dk * 128, (dk + 1) * 128), (t0, t0 + TC)),
                            tag=f"F0x{b}{c}{dk}")
                xt = [x_sb.at(0, 128, dk * TC, (dk + 1) * TC) for dk in range(DK)]
                mub, rsb = kb.ln_stats(xt, TC, (0, 2), stage, tag=f"F0l{b}{c}")
                for dk in range(DK):
                    kb.tt("vector", ALU.add, tt1.at(), xt[dk], mub, tag=f"F0a{b}{c}{dk}")
                    kb.tt("vector", ALU.mult, x_sb.at(0, 128, dk * TC, (dk + 1) * TC),
                          tt1.at(), rsb, tag=f"F0m{b}{c}{dk}")
                    kb.store("gpsimd", ffs.slc(b, (dk * 128, (dk + 1) * 128), (t0, t0 + TC)),
                             x_sb.at(0, 128, dk * TC, (dk + 1) * TC), tag=f"F0s{b}{c}{dk}")

    def pe_transpose(out_tv, in_tv, tag):
        (oap, oreg) = out_tv
        (iap, ireg) = in_tv

        def fn(e):
            return e.transpose(oap, iap, kb.ident.ap())
        em.op("tensor", fn, reads=[ireg, kb.ident.reg()], writes=[oreg], tag=tag)

    for l in range(n_layers):
        for b in range(B):
            if not skip_attn:
                with contextlib.ExitStack() as sq:
                    q_f = SB(nc, sq, "fq_q", 4 * T, BF16)
                    k_f = SB(nc, sq, "fq_k", 4 * (T + WS), BF16)
                    vt = SB(nc, sq, "fa_vt", 20 * 512, BF16)
                    for rc in range(4):
                        kb.memset("gpsimd", k_f.at(0, 128, rc * (T + WS), rc * (T + WS) + WS),
                                  0.0, tag=f"kp{l}{b}{rc}")
                    kb.memset("gpsimd", vt.at(0, 128, 0, 4 * 512), 0.0, tag=f"vtp{l}{b}")
                    with contextlib.ExitStack() as sv2:
                        v_f = SB(nc, sv2, "fq_v", 4 * (T + WS), BF16)
                        xsh = SB(nc, sv2, "fq_xsh", DK * 513, F32)
                        xshb = SB(nc, sv2, "fq_xshb", DK * 513, BF16)
                        wqst = SB(nc, sv2, "fq_wst", DK * 256, F32)
                        wq_bf = SB(nc, sv2, "fq_wb", DK * 1536, BF16)
                        qcs_t = SB(nc, sv2, "fq_qcs", T, F32)
                        qss_t = SB(nc, sv2, "fq_qss", T, F32)
                        kcs_t = SB(nc, sv2, "fq_kcs", T, F32)
                        kss_t = SB(nc, sv2, "fq_kss", T, F32)
                        sqs = SB(nc, sv2, "fq_sq", TC, BF16)
                        qraw = SB(nc, sv2, "fq_qraw", TC, F32)
                        nrm = SB(nc, sv2, "fq_nrm", 3 * TC, F32, parts=2)
                        nrmb = SB(nc, sv2, "fq_nrmb", TC, BF16, parts=2)
                        rbc = SB(nc, sv2, "fq_rbc", TC, F32)
                        qn32 = SB(nc, sv2, "fq_qn", TC, F32)
                        qnb = SB(nc, sv2, "fq_qnb", TC, BF16)
                        s1 = SB(nc, sv2, "fq_s1", TC, F32)
                        s2b = SB(nc, sv2, "fq_s2", TC, F32)
                        kb.memset("gpsimd", v_f.at(0, 128, 0, 4 * (T + WS)), 0.0,
                                  tag=f"vp{l}{b}")
                        kb.load("sync", qcs_t.at(), W["l0f_qcs"].slc((0, 128), (0, T)),
                                tag=f"tq{l}{b}")
                        kb.load("sync", qss_t.at(), W["l0f_qss"].slc((0, 128), (0, T)),
                                tag=f"tq2{l}{b}")
                        kb.load("sync", kcs_t.at(), W["l0f_kcs"].slc((0, 128), (0, T)),
                                tag=f"tk{l}{b}")
                        kb.load("sync", kss_t.at(), W["l0f_kss"].slc((0, 128), (0, T)),
                                tag=f"tk2{l}{b}")
                        for sl in range(6):
                            kb.load("sync", wqst.at(),
                                    W["l0f_wqkv"].wslice(l, D, sl * 256, (sl + 1) * 256),
                                    tag=f"Qw{l}{b}{sl}")
                            for dk in range(DK):
                                kb.copy("vector",
                                        wq_bf.at(0, 128, dk * 1536 + sl * 256,
                                                 dk * 1536 + (sl + 1) * 256),
                                        wqst.at(0, 128, dk * 256, (dk + 1) * 256),
                                        tag=f"Qwc{l}{b}{sl}{dk}")
                        for c in range(NCH):
                            t0 = c * TC
                            for dk in range(DK):
                                if t0 == 0:
                                    kb.memset("gpsimd", xsh.at(0, 128, dk * 513, dk * 513 + 1),
                                              0.0, tag=f"Qh{l}{b}{c}{dk}")
                                    kb.load("sync", xsh.at(0, 128, dk * 513 + 1, dk * 513 + 513),
                                            ffs.slc(b, (dk * 128, (dk + 1) * 128), (0, TC)),
                                            tag=f"Qx{l}{b}{c}{dk}")
                                else:
                                    kb.load("sync", xsh.at(0, 128, dk * 513, dk * 513 + 513),
                                            ffs.slc(b, (dk * 128, (dk + 1) * 128),
                                                    (t0 - 1, t0 + TC)),
                                            tag=f"Qx{l}{b}{c}{dk}")
                                kb.copy("vector", xshb.at(0, 128, dk * 513, (dk + 1) * 513),
                                        xsh.at(0, 128, dk * 513, (dk + 1) * 513),
                                        tag=f"Qc{l}{b}{c}{dk}")
                            for oc in range(12):
                                pt = ps.bank(4 + oc % 3, 0, TC)
                                pairs = []
                                for dk in range(DK):
                                    off = 1 if dk < 4 else 0
                                    pairs.append(
                                        ((wq_bf.ap(0, 128, dk * 1536 + oc * 128,
                                                   dk * 1536 + (oc + 1) * 128),
                                          wq_bf.reg(0, 128, dk * 1536 + oc * 128,
                                                    dk * 1536 + (oc + 1) * 128)),
                                         xshb.at(0, 128, dk * 513 + off, dk * 513 + off + TC)))
                                kb.mm_group(pt, pairs, tag=f"Qmm{l}{b}{c}{oc}")
                                if oc >= 8:
                                    rc = oc - 8
                                    kb.copy("vector",
                                            v_f.at(0, 128, rc * (T + WS) + WS + t0,
                                                   rc * (T + WS) + WS + t0 + TC),
                                            pt, tag=f"Qv{l}{b}{c}{oc}")
                                    continue
                                isq = oc < 4
                                rc = oc % 4
                                kb.copy("vector", qraw.at(), pt, tag=f"Qqr{l}{b}{c}{oc}")
                                kb.tt("gpsimd", ALU.mult, sqs.at(), qraw.at(), qraw.at(),
                                      tag=f"Qsq{l}{b}{c}{oc}")
                                ssp = ps.bank(7, 0, TC, 0, 2)
                                kb.mm_group(ssp, [((e2.ap(), e2.reg()), sqs.at())],
                                            tag=f"Qss{l}{b}{c}{oc}")
                                kb.act(AF.Sqrt, nrm.at(0, 2, 0, TC), ssp, tag=f"Qn1{l}{b}{c}{oc}")
                                kb.ts(nrm.at(0, 2, TC, 2 * TC), nrm.at(0, 2, 0, TC),
                                      1e-12, ALU.max, tag=f"Qn2{l}{b}{c}{oc}")
                                kb.recip(nrm.at(0, 2, 2 * TC, 3 * TC),
                                         nrm.at(0, 2, TC, 2 * TC), tag=f"Qn3{l}{b}{c}{oc}")
                                kb.copy("vector", nrmb.at(0, 2, 0, TC),
                                        nrm.at(0, 2, 2 * TC, 3 * TC), tag=f"Qn4{l}{b}{c}{oc}")
                                rbp = ps.bank(7, 0, TC)
                                kb.mm_group(rbp, [((e2b.ap(0, 2, 0, 128), e2b.reg(0, 2, 0, 128)),
                                                   nrmb.at(0, 2, 0, TC))],
                                            tag=f"Qrb{l}{b}{c}{oc}")
                                kb.copy("vector", rbc.at(), rbp, tag=f"Qrc{l}{b}{c}{oc}")
                                scol = qscol if isq else kscol
                                kb.stt(qn32.at(), pt, scol.ap(0, 128, l, l + 1), rbc.at(),
                                       ALU.mult, ALU.mult, extra=[scol.reg()],
                                       tag=f"Qsc{l}{b}{c}{oc}")
                                kb.copy("vector", qnb.at(), qn32.at(), tag=f"Qnb{l}{b}{c}{oc}")
                                rp = ps.bank(7, 0, TC)
                                kb.mm_group(rp, [((rotm.ap(), rotm.reg()), qnb.at())],
                                            tag=f"Qrt{l}{b}{c}{oc}")
                                cst = qcs_t if isq else kcs_t
                                snt = qss_t if isq else kss_t
                                kb.tt("vector", ALU.mult, s1.at(), qn32.at(),
                                      cst.at(0, 128, t0, t0 + TC), tag=f"Q1{l}{b}{c}{oc}")
                                kb.tt("vector", ALU.mult, s2b.at(), rp,
                                      snt.at(0, 128, t0, t0 + TC), tag=f"Q2{l}{b}{c}{oc}")
                                if isq:
                                    dstv = q_f.at(0, 128, rc * T + t0, rc * T + t0 + TC)
                                else:
                                    dstv = k_f.at(0, 128, rc * (T + WS) + WS + t0,
                                                  rc * (T + WS) + WS + t0 + TC)
                                kb.tt("vector", ALU.add, dstv, s1.at(), s2b.at(),
                                      tag=f"Q3{l}{b}{c}{oc}")
                        for rc in range(4):
                            for jc in range(4, 20):
                                tpb = (kb.ps.h[0:128, 7 * 512:7 * 512 + 64].bitcast(BF16),
                                       kb.ps.preg(7))
                                pe_transpose(tpb,
                                             (v_f.ap(0, 128, rc * (T + WS) + jc * 128,
                                                     rc * (T + WS) + (jc + 1) * 128),
                                              v_f.reg(0, 128, rc * (T + WS) + jc * 128,
                                                      rc * (T + WS) + (jc + 1) * 128)),
                                             tag=f"vt{l}{b}{rc}{jc}")
                                kb.copy("vector",
                                        vt.at(0, 128, jc * 512 + rc * 128,
                                              jc * 512 + (rc + 1) * 128),
                                        tpb, tag=f"vte{l}{b}{rc}{jc}")
                    if DBG.get("l0f_tapqkv") and l == 0 and b == 0:
                        qt_t = DR(nc, "tap_q", [128, 4 * T], F32, kind="ExternalOutput")
                        kt_t = DR(nc, "tap_k", [128, 4 * (T + WS)], F32, kind="ExternalOutput")
                        vt_t = DR(nc, "tap_vt", [128, 20 * 512], F32, kind="ExternalOutput")
                        with contextlib.ExitStack() as sT:
                            cvt = SB(nc, sT, "tapc", 4 * (T + WS), F32)
                            kb.copy("vector", cvt.at(0, 128, 0, 4 * T), q_f.at(), tag="tapq")
                            kb.store("gpsimd", qt_t.slc((0, 128), (0, 4 * T)),
                                     cvt.at(0, 128, 0, 4 * T), tag="tapqs")
                            kb.copy("vector", cvt.at(), k_f.at(), tag="tapk")
                            kb.store("gpsimd", kt_t.slc((0, 128), (0, 4 * (T + WS))),
                                     cvt.at(), tag="tapks")
                            kb.copy("vector", cvt.at(0, 128, 0, 20 * 512), vt.at(), tag="tapv")
                            kb.store("gpsimd", vt_t.slc((0, 128), (0, 20 * 512)),
                                     cvt.at(0, 128, 0, 20 * 512), tag="tapvs")
                    with contextlib.ExitStack() as sa:
                        masks = SB(nc, sa, "fa_msk", 2 * 4 * 1024, F32)
                        att = SB(nc, sa, "fa_att", 8 * 512, BF16)
                        smf = SB(nc, sa, "fa_sm", 1024, F32)
                        pb = SB(nc, sa, "fa_p", 1024, BF16)
                        pnb = SB(nc, sa, "fa_pn", 1024, BF16)
                        stat = SB(nc, sa, "fa_stt", 4, F32)
                        o_f = SB(nc, sa, "fa_o", 8 * T, BF16, parts=64)
                        for wv in range(2):
                            for ic in range(4):
                                kb.load("sync",
                                        masks.at(0, 128, (wv * 4 + ic) * 1024,
                                                 (wv * 4 + ic + 1) * 1024),
                                        W["l0f_mask"].slc(wv, (ic * 128, (ic + 1) * 128),
                                                          (0, 1024)),
                                        tag=f"Fm{l}{b}{wv}{ic}")
                        for h in range(NH):
                            rc = h // 2
                            p0 = 64 * (h % 2)
                            for w in range(NW):
                                wv = 1 if w else 0
                                for ic in range(4):
                                    for jh in range(2):
                                        sb_ = ps.bank(jh, 0, 512)
                                        kb.mm_group(
                                            sb_,
                                            [((q_f.ap(p0, p0 + 64,
                                                      rc * T + w * WS + ic * 128,
                                                      rc * T + w * WS + (ic + 1) * 128),
                                               q_f.reg(p0, p0 + 64,
                                                       rc * T + w * WS + ic * 128,
                                                       rc * T + w * WS + (ic + 1) * 128)),
                                              (k_f.ap(p0, p0 + 64,
                                                      rc * (T + WS) + w * WS + jh * 512,
                                                      rc * (T + WS) + w * WS + (jh + 1) * 512),
                                               k_f.reg(p0, p0 + 64,
                                                       rc * (T + WS) + w * WS + jh * 512,
                                                       rc * (T + WS) + w * WS + (jh + 1) * 512)))],
                                            tag=f"SC{l}{b}{h}{w}{ic}{jh}")
                                        kb.tt("vector", ALU.add,
                                              smf.at(0, 128, jh * 512, (jh + 1) * 512), sb_,
                                              masks.at(0, 128,
                                                       (wv * 4 + ic) * 1024 + jh * 512,
                                                       (wv * 4 + ic) * 1024 + (jh + 1) * 512),
                                              tag=f"SM{l}{b}{h}{w}{ic}{jh}")
                                    kb.reduce(ALU.max, stat.at(0, 128, 0, 1), smf.at(),
                                              negate=True, tag=f"SR{l}{b}{h}{w}{ic}")
                                    kb.act(AF.Exp, pb.at(), smf.at(),
                                           bias=stat.ap(0, 128, 0, 1),
                                           extra=[stat.reg(0, 128, 0, 1)],
                                           tag=f"SE{l}{b}{h}{w}{ic}")
                                    kb.reduce(ALU.add, stat.at(0, 128, 1, 2), pb.at(),
                                              tag=f"SS{l}{b}{h}{w}{ic}")
                                    kb.recip(stat.at(0, 128, 2, 3), stat.at(0, 128, 1, 2),
                                             tag=f"SV{l}{b}{h}{w}{ic}")
                                    kb.ts(pnb.at(), pb.at(), stat.ap(0, 128, 2, 3), ALU.mult,
                                          extra=[stat.reg(0, 128, 2, 3)],
                                          tag=f"SN{l}{b}{h}{w}{ic}")
                                    if (DBG.get("l0f_tapsc") and l == 0 and b == 0
                                            and h == 0 and w == 1 and ic == 0):
                                        sm_t = DR(nc, "tap_sm", [128, 1024], F32,
                                                  kind="ExternalOutput")
                                        pn_t = DR(nc, "tap_pn", [128, 1024], F32,
                                                  kind="ExternalOutput")
                                        kb.store("gpsimd", sm_t.slc((0, 128), (0, 1024)),
                                                 smf.at(), tag="tapsm")
                                        kb.copy("vector", smf.at(), pnb.at(), tag="tappncv")
                                        kb.store("gpsimd", pn_t.slc((0, 128), (0, 1024)),
                                                 smf.at(), tag="tappn")
                                    for jc in range(8):
                                        dstp = (kb.ps.h[0:128, 7 * 512 + jc * 64:
                                                        7 * 512 + (jc + 1) * 64].bitcast(BF16),
                                                kb.ps.preg(7))
                                        pe_transpose(dstp,
                                                     pnb.at(0, 128, jc * 128, (jc + 1) * 128),
                                                     tag=f"AT{l}{b}{h}{w}{ic}{jc}")
                                    dstap = bass.AP(att.h, ic * 128,
                                                    [[att.cols, 128], [512, 8], [1, 128]])
                                    srcap = kb.ps.h[0:128, 7 * 512:8 * 512].bitcast(BF16)

                                    def evfn(e, oap=dstap, iap=srcap):
                                        return e.tensor_copy(oap, iap)
                                    em.op("vector", evfn, reads=[kb.ps.preg(7)],
                                          writes=[att.reg(0, 128, 0, 8 * 512)],
                                          tag=f"AE{l}{b}{h}{w}{ic}")
                                avp = ps.bank(2 + (w % 2), 0, 512, 0, 64)
                                pairs = []
                                for jc in range(8):
                                    jcg = w * 4 + jc
                                    pairs.append(
                                        ((vt.ap(0, 128, jcg * 512 + rc * 128 + p0,
                                                jcg * 512 + rc * 128 + p0 + 64),
                                          vt.reg(0, 128, jcg * 512 + rc * 128 + p0,
                                                 jcg * 512 + rc * 128 + p0 + 64)),
                                         att.at(0, 128, jc * 512, (jc + 1) * 512)))
                                kb.mm_group(avp, pairs, tag=f"AV{l}{b}{h}{w}")
                                kb.copy("vector",
                                        o_f.at(0, 64, h * T + w * WS, h * T + (w + 1) * WS),
                                        avp, tag=f"AO{l}{b}{h}{w}")
                        if DBG.get("l0f_tapqkv") and l == 0 and b == 0:
                            of_t = DR(nc, "tap_o", [64, 8 * T], F32, kind="ExternalOutput")
                            with contextlib.ExitStack() as sT2:
                                cvt2 = SB(nc, sT2, "tapc2", 4 * T, F32, parts=64)
                                for hf in range(2):
                                    kb.copy("vector", cvt2.at(0, 64, 0, 4 * T),
                                            o_f.at(0, 64, hf * 4 * T, (hf + 1) * 4 * T), tag=f"tapo{hf}")
                                    kb.store("gpsimd", of_t.slc((0, 64), (hf * 4 * T, (hf + 1) * 4 * T)),
                                             cvt2.at(0, 64, 0, 4 * T), tag=f"tapos{hf}")
                        with contextlib.ExitStack() as so:
                            wo_bf = SB(nc, so, "fo_wo", 8 * 1024, BF16, parts=64)
                            wst = SB(nc, so, "fo_wst", 1024, F32, parts=64)
                            xsb = SB(nc, so, "fo_x", TC, F32)
                            for h in range(NH):
                                kb.load("sync", wst.at(0, 64, 0, 1024),
                                        W["l0f_wo"].slc(l, (h * 64, (h + 1) * 64), (0, 1024)),
                                        tag=f"WO{l}{b}{h}")
                                kb.copy("vector", wo_bf.at(0, 64, h * 1024, (h + 1) * 1024),
                                        wst.at(0, 64, 0, 1024), tag=f"WOc{l}{b}{h}")
                            for c in range(NCH):
                                t0 = c * TC
                                for oc in range(DK):
                                    pt = ps.bank(4 + oc % 2, 0, TC)
                                    pairs = [((wo_bf.ap(0, 64, h * 1024 + oc * 128,
                                                        h * 1024 + (oc + 1) * 128),
                                               wo_bf.reg(0, 64, h * 1024 + oc * 128,
                                                         h * 1024 + (oc + 1) * 128)),
                                              o_f.at(0, 64, h * T + t0, h * T + t0 + TC))
                                             for h in range(NH)]
                                    kb.mm_group(pt, pairs, tag=f"WOm{l}{b}{c}{oc}")
                                    kb.load("scalar", xsb.at(),
                                            ffs.slc(b, (oc * 128, (oc + 1) * 128),
                                                    (t0, t0 + TC)),
                                            tag=f"WOx{l}{b}{c}{oc}")
                                    kb.tt("vector", ALU.add, xsb.at(), pt, xsb.at(),
                                          tag=f"WOa{l}{b}{c}{oc}")
                                    kb.store("gpsimd",
                                             ffs.slc(b, (oc * 128, (oc + 1) * 128),
                                                     (t0, t0 + TC)),
                                             xsb.at(), tag=f"WOs{l}{b}{c}{oc}")

            if skip_ffn:
                continue
            with contextlib.ExitStack() as sf:
                xsh = SB(nc, sf, "ff_xsh", DK * 513, F32)
                htail = SB(nc, sf, "ff_htl", DK, F32)
                stage = SB(nc, sf, "ff_st", 2 * DK * TC, BF16)
                h2n = SB(nc, sf, "ff_h2n", DK * TC, BF16)
                gel = SB(nc, sf, "ff_gel", HPK * TC, BF16)
                w1st = SB(nc, sf, "ff_w1s", DK * 512, F32)
                w1b = SB(nc, sf, "ff_w1b", 2 * DK * 512, BF16)
                w2b = SB(nc, sf, "ff_w2b", HPK * 1024, BF16)
                w2st = SB(nc, sf, "ff_w2s", 1024, F32)
                tt1 = SB(nc, sf, "ff_t1", TC, F32)
                tt2 = SB(nc, sf, "ff_t2", TC, F32)
                for hk in range(HPK):
                    kb.load("sync", w2st.at(),
                            W["l0f_w2p"].slc(l, (hk * 128, (hk + 1) * 128), (0, 1024)),
                            tag=f"W2{l}{b}{hk}")
                    kb.copy("vector", w2b.at(0, 128, hk * 1024, (hk + 1) * 1024),
                            w2st.at(), tag=f"W2c{l}{b}{hk}")
                for c in range(NCH):
                    t0 = c * TC
                    for dk in range(DK):
                        if t0 == 0:
                            kb.memset("gpsimd", xsh.at(0, 128, dk * 513, dk * 513 + 1),
                                      0.0, tag=f"Gh{l}{b}{c}{dk}")
                        else:
                            kb.copy("gpsimd", xsh.at(0, 128, dk * 513, dk * 513 + 1),
                                    htail.at(0, 128, dk, dk + 1), tag=f"Gh{l}{b}{c}{dk}")
                        kb.load("sync", xsh.at(0, 128, dk * 513 + 1, dk * 513 + 513),
                                ffs.slc(b, (dk * 128, (dk + 1) * 128), (t0, t0 + TC)),
                                tag=f"Gx{l}{b}{c}{dk}")
                        if c < NCH - 1:
                            kb.copy("gpsimd", htail.at(0, 128, dk, dk + 1),
                                    xsh.at(0, 128, dk * 513 + 512, dk * 513 + 513),
                                    tag=f"Gt{l}{b}{c}{dk}")
                    sh = []
                    for dk in range(DK):
                        off = 1 if dk < 4 else 0
                        sh.append(xsh.at(0, 128, dk * 513 + off, dk * 513 + off + TC))
                    mub, rsb = kb.ln_stats(sh, TC, (0, 2), stage, tag=f"Gl{l}{b}{c}")
                    for dk in range(DK):
                        kb.tt("vector", ALU.add, tt1.at(), sh[dk], mub, tag=f"Ga{l}{b}{c}{dk}")
                        kb.tt("vector", ALU.mult, tt2.at(), tt1.at(), rsb,
                              tag=f"Gm{l}{b}{c}{dk}")
                        kb.ts(h2n.at(0, 128, dk * TC, (dk + 1) * TC), tt2.at(),
                              lngc.ap(0, 128, dk * 2 + l, dk * 2 + l + 1), ALU.mult,
                              lnbc.ap(0, 128, dk * 2 + l, dk * 2 + l + 1), ALU.add,
                              extra=[lngc.reg(), lnbc.reg()], tag=f"Gg{l}{b}{c}{dk}")
                    for gfirst in (True, False):
                        for oi in range(HPK):
                            oc = oi + HPK if gfirst else oi
                            wp = oi % 2
                            kb.load("sync", w1st.at(0, 128, 0, DK * 128),
                                    W["l0f_w1p"].wslice(l, D, oc * 128, (oc + 1) * 128),
                                    tag=f"G1{l}{b}{c}{oc}")
                            kb.copy("vector",
                                    w1b.at(0, 128, wp * DK * 128, wp * DK * 128 + DK * 128),
                                    w1st.at(0, 128, 0, DK * 128), tag=f"G1c{l}{b}{c}{oc}")
                            pt = ps.bank(4 + oi % 3, 0, TC)
                            pairs = [((w1b.ap(0, 128, wp * DK * 128 + dk * 128,
                                              wp * DK * 128 + (dk + 1) * 128),
                                       w1b.reg(0, 128, wp * DK * 128 + dk * 128,
                                               wp * DK * 128 + (dk + 1) * 128)),
                                      h2n.at(0, 128, dk * TC, (dk + 1) * TC))
                                     for dk in range(DK)]
                            kb.mm_group(pt, pairs, tag=f"G1m{l}{b}{c}{oc}")
                            bcol = b1c.ap(0, 128, l * 44 + oc, l * 44 + oc + 1)
                            breg = [b1c.reg()]
                            if gfirst:
                                kb.act(AF.Gelu, gel.at(0, 128, oi * TC, (oi + 1) * TC),
                                       pt, bias=bcol, extra=breg, tag=f"G1g{l}{b}{c}{oc}")
                            else:
                                kb.stt(gel.at(0, 128, oi * TC, (oi + 1) * TC), pt, bcol,
                                       gel.at(0, 128, oi * TC, (oi + 1) * TC),
                                       ALU.add, ALU.mult, extra=breg,
                                       tag=f"G1h{l}{b}{c}{oc}")
                    for oc in range(DK):
                        pt = ps.bank(4 + oc % 2, 0, TC)
                        pairs = [((w2b.ap(0, 128, hk * 1024 + oc * 128,
                                          hk * 1024 + (oc + 1) * 128),
                                   w2b.reg(0, 128, hk * 1024 + oc * 128,
                                           hk * 1024 + (oc + 1) * 128)),
                                  gel.at(0, 128, hk * TC, (hk + 1) * TC))
                                 for hk in range(HPK)]
                        kb.mm_group(pt, pairs, tag=f"G2m{l}{b}{c}{oc}")
                        kb.stt(tt1.at(), pt,
                               b2c.ap(0, 128, oc * 2 + l, oc * 2 + l + 1),
                               xsh.at(0, 128, oc * 513 + 1, oc * 513 + 1 + TC),
                               ALU.add, ALU.add, extra=[b2c.reg()],
                               tag=f"G2a{l}{b}{c}{oc}")
                        kb.store("gpsimd",
                                 ffs.slc(b, (oc * 128, (oc + 1) * 128), (t0, t0 + TC)),
                                 tt1.at(), tag=f"G2s{l}{b}{c}{oc}")

    with contextlib.ExitStack() as s9:
        f_sb = SB(nc, s9, "f9_f", DK * TC, F32)
        xrr = SB(nc, s9, "f9_x", DK * TC, F32)
        stage = SB(nc, s9, "f9_st", 2 * DK * TC, BF16)
        tt1 = SB(nc, s9, "f9_t1", TC, F32)
        tt2 = SB(nc, s9, "f9_t2", TC, F32)
        kb._lntap_row = tt2
        for b in range(B):
            for c in range(NCH):
                t0 = c * TC
                for dk in range(DK):
                    kb.load("sync", f_sb.at(0, 128, dk * TC, (dk + 1) * TC),
                            ffs.slc(b, (dk * 128, (dk + 1) * 128), (t0, t0 + TC)),
                            tag=f"E1{b}{c}{dk}")
                    kb.load("scalar", xrr.at(0, 128, dk * TC, (dk + 1) * TC),
                            x_in.slc(b, (dk * 128, (dk + 1) * 128), (t0, t0 + TC)),
                            tag=f"E2{b}{c}{dk}")
                ft = [f_sb.at(0, 128, dk * TC, (dk + 1) * TC) for dk in range(DK)]
                mub, rsb = kb.ln_stats(ft, TC, (0, 2), stage, tag=f"El{b}{c}")
                for dk in range(DK):
                    kb.tt("vector", ALU.add, tt1.at(), ft[dk], mub, tag=f"Ea{b}{c}{dk}")
                    kb.tt("vector", ALU.mult, tt2.at(), tt1.at(), rsb, tag=f"Em{b}{c}{dk}")
                    kb.tt("vector", ALU.add, xrr.at(0, 128, dk * TC, (dk + 1) * TC),
                          xrr.at(0, 128, dk * TC, (dk + 1) * TC), tt2.at(),
                          tag=f"Er{b}{c}{dk}")
                    kb.store("gpsimd",
                             x_out.slc(b, (dk * 128, (dk + 1) * 128), (t0, t0 + TC)),
                             xrr.at(0, 128, dk * TC, (dk + 1) * TC), tag=f"Es{b}{c}{dk}")




def host_prep_l1a(inp):
    out = {}
    out["l1a_wqkvT"] = _f32(np.asarray(inp["L1_attn_w_qkv"]).T)
    wc = np.asarray(inp["L1_attn_w_conv"])
    w2 = np.zeros((8, 3, 128, 128), dtype=np.float32)
    for ch in range(8):
        for tau in range(3):
            for a in range(2):
                g = 2 * ch + a
                w2[ch, tau, 64 * a:64 * a + 64, 64 * a:64 * a + 64] = \
                    wc[g * 64:(g + 1) * 64, :, tau].T
    out["l1a_wconv2"] = _f32(w2)
    bc = np.asarray(inp["L1_attn_b_conv"])
    out["l1a_bconv"] = _f32(bc.reshape(8, 128).T)
    out["l1a_nullk"] = _f32(np.asarray(inp["L1_attn_null_k"]).reshape(1024)
                            .reshape(8, 128).T)
    out["l1a_nullv"] = _f32(np.asarray(inp["L1_attn_null_v"]).reshape(1024)
                            .reshape(8, 128).T)
    out["l1a_woutT"] = _f32(np.asarray(inp["L1_attn_w_out"]).T)
    out["l1a_bout"] = _f32(np.asarray(inp["L1_attn_b_out"]).reshape(8, 128).T)
    i = np.arange(2048)[:, None]
    jc = np.arange(683)[None, :]
    msk = np.zeros((2048, 684), dtype=np.float32)
    msk[:, 1:] = np.where(i < np.minimum(3 * jc + 1, 2047), np.float32(-1e30), 0.0)
    out["l1a_mask"] = np.ascontiguousarray(msk, dtype=ml_dtypes.bfloat16)
    return out


def emit_l1a(kb, x_in, x_out, W):
    nc, st, em = kb.nc, kb.stack, kb.em
    ps = kb.ps
    NH = 16
    J = 684
    JP = 768
    NJC = 6
    SCALE = 1.0 / 32.0

    aout = DR(nc, "l1a_out", [B, D, T], F32,
              kind="ExternalOutput" if DBG.get("l1a_tap") else "Internal")

    wconv_bf = SB(nc, st, "a1_wc", 8 * 3 * 128, BF16)
    bconv = SB(nc, st, "a1_bc", 8, F32)
    nullk = SB(nc, st, "a1_nk", 8, F32)
    nullv = SB(nc, st, "a1_nv", 8, F32)
    bout = SB(nc, st, "a1_bo", 8, F32)
    with contextlib.ExitStack() as s2:
        wst = SB(nc, s2, "a1_wst", 8 * 3 * 128, F32)
        wc2 = W["l1a_wconv2"]
        wc_ap = bass.AP(wc2.t, 0, [[128, 128], [3 * 128 * 128, 8],
                                   [128 * 128, 3], [1, 128]])
        kb.load("sync", wst.at(), (wc_ap, ("D" + wc2.name, 0, 1, 0, wc2.size)),
                tag="a1wc")
        kb.copy("vector", wconv_bf.at(), wst.at(), tag="a1wcc")
    kb.load("sync", bconv.at(), W["l1a_bconv"].slc((0, 128), (0, 8)), tag="a1bc")
    kb.load("sync", nullk.at(), W["l1a_nullk"].slc((0, 128), (0, 8)), tag="a1nk")
    kb.load("sync", nullv.at(), W["l1a_nullv"].slc((0, 128), (0, 8)), tag="a1nv")
    kb.load("sync", bout.at(), W["l1a_bout"].slc((0, 128), (0, 8)), tag="a1bo")

    def pe_transpose(out_tv, in_tv, tag):
        (oap, oreg) = out_tv
        (iap, ireg) = in_tv

        def fn(e):
            return e.transpose(oap, iap, kb.ident.ap())
        em.op("tensor", fn, reads=[ireg, kb.ident.reg()], writes=[oreg], tag=tag)

    TC = 512
    NCH = T // TC
    for b in range(B):
        with contextlib.ExitStack() as sb_:
            xn_bf = SB(nc, sb_, "a1_xn", DK * T, BF16)
            q_f = SB(nc, sb_, "a1_q", 8 * T, BF16)
            kc_f = SB(nc, sb_, "a1_kc", 8 * JP, BF16)
            vt_f = SB(nc, sb_, "a1_vt", 8 * NJC * 128, BF16)
            with contextlib.ExitStack() as sq:
                kv_f = SB(nc, sq, "a1_kv", 8 * (T + 4), BF16)
                vc_f = SB(nc, sq, "a1_vc", 8 * JP, BF16)
                x_sb = SB(nc, sq, "a1_x", DK * TC, F32)
                stage = SB(nc, sq, "a1_st", 2 * DK * TC, BF16)
                tt1 = SB(nc, sq, "a1_t1", TC, F32)
                wqst = SB(nc, sq, "a1_wq", DK * 256, F32)
                qtmp = SB(nc, sq, "a1_qt", TC, BF16)
                wq_bf = SB(nc, sq, "a1_wqb", DK * 256, BF16)
                for ch in range(8):
                    kb.memset("gpsimd", kv_f.at(0, 128, ch * (T + 4), ch * (T + 4) + 1),
                              0.0, tag=f"akp{b}{ch}")
                for c in range(NCH):
                    t0 = c * TC
                    for dk in range(DK):
                        kb.load("sync", x_sb.at(0, 128, dk * TC, (dk + 1) * TC),
                                x_in.slc(b, (dk * 128, (dk + 1) * 128), (t0, t0 + TC)),
                                tag=f"Ax{b}{c}{dk}")
                    xt = [x_sb.at(0, 128, dk * TC, (dk + 1) * TC) for dk in range(DK)]
                    mub, rsb = kb.ln_stats(xt, TC, (0, 2), stage, tag=f"Aln{b}{c}")
                    for dk in range(DK):
                        kb.tt("vector", ALU.add, tt1.at(), xt[dk], mub, tag=f"Aa{b}{c}{dk}")
                        kb.tt("vector", ALU.mult,
                              xn_bf.at(0, 128, dk * T + t0, dk * T + t0 + TC),
                              tt1.at(), rsb, tag=f"Am{b}{c}{dk}")

                def qkv_part(sls, kind, tagp):
                    for sl in sls:
                        kb.load("sync", wqst.at(),
                                W["l1a_wqkvT"].wslice(None, D, sl * 256, (sl + 1) * 256),
                                tag=f"Aw{b}{tagp}{sl}")
                        for dk in range(DK):
                            kb.copy("vector",
                                    wq_bf.at(0, 128, dk * 256, (dk + 1) * 256),
                                    wqst.at(0, 128, dk * 256, (dk + 1) * 256),
                                    tag=f"Awc{b}{tagp}{sl}{dk}")
                        for oc4 in range(2):
                            oc = sl * 2 + oc4
                            ch = oc % 8
                            for tt_ in range(4):
                                pt = ps.bank(4 + (oc4 * 4 + tt_) % 4, 0, TC)
                                pairs = [((wq_bf.ap(0, 128, dk * 256 + oc4 * 128,
                                                    dk * 256 + (oc4 + 1) * 128),
                                           wq_bf.reg(0, 128, dk * 256 + oc4 * 128,
                                                     dk * 256 + (oc4 + 1) * 128)),
                                          xn_bf.at(0, 128, dk * T + tt_ * TC,
                                                   dk * T + (tt_ + 1) * TC))
                                         for dk in range(DK)]
                                kb.mm_group(pt, pairs, tag=f"Aq{b}{tagp}{oc}{tt_}")
                                if kind == 0:
                                    kb.ts(qtmp.at(), pt, SCALE, ALU.mult,
                                          tag=f"Aqe{b}{oc}{tt_}")
                                    kb.copy("gpsimd",
                                            q_f.at(0, 128, ch * T + tt_ * TC,
                                                   ch * T + (tt_ + 1) * TC),
                                            qtmp.at(), tag=f"Aqr{b}{oc}{tt_}")
                                else:
                                    kb.copy("vector",
                                            kv_f.at(0, 128, ch * (T + 4) + 1 + tt_ * TC,
                                                    ch * (T + 4) + 1 + (tt_ + 1) * TC),
                                            pt, tag=f"Ake{b}{tagp}{oc}{tt_}")

                def compress(dst_f, nullc, which):
                    for ch in range(8):
                        for jh in range(2):
                            j0 = jh * 512
                            nj = 512 if jh == 0 else 683 - 512
                            pt = ps.bank(jh, 0, nj)
                            pairs = []
                            for tau in range(3):
                                col0 = ch * (T + 4) + 3 * j0 + tau
                                rsl = kv_f.h[0:128, col0:col0 + 3 * nj]
                                rap = bass.AP(rsl.tensor, rsl.offset,
                                              [list(rsl.ap[0]), [3, nj]])
                                pairs.append(
                                    ((wconv_bf.ap(0, 128, (ch * 3 + tau) * 128,
                                                  (ch * 3 + tau + 1) * 128),
                                      wconv_bf.reg(0, 128, (ch * 3 + tau) * 128,
                                                   (ch * 3 + tau + 1) * 128)),
                                     (rap, kv_f.reg(0, 128, ch * (T + 4),
                                                    (ch + 1) * (T + 4)))))
                            kb.mm_group(pt, pairs, tag=f"Ac{b}{which}{ch}{jh}")
                            kb.ts((dst_f.ap(0, 128, ch * JP + 1 + j0,
                                            ch * JP + 1 + j0 + nj),
                                   dst_f.reg(0, 128, ch * JP + 1 + j0,
                                             ch * JP + 1 + j0 + nj)),
                                  pt, bconv.ap(0, 128, ch, ch + 1),
                                  ALU.add, extra=[bconv.reg()],
                                  tag=f"Ace{b}{which}{ch}{jh}")
                    for ch in range(8):
                        kb.copy("vector",
                                (dst_f.ap(0, 128, ch * JP, ch * JP + 1),
                                 dst_f.reg(0, 128, ch * JP, ch * JP + 1)),
                                (nullc.ap(0, 128, ch, ch + 1), nullc.reg()),
                                tag=f"Acn{b}{which}{ch}")
                        kb.memset("gpsimd",
                                  (dst_f.ap(0, 128, ch * JP + J, (ch + 1) * JP),
                                   dst_f.reg(0, 128, ch * JP + J, (ch + 1) * JP)),
                                  0.0, tag=f"Acz{b}{which}{ch}")

                qkv_part([4, 5, 6, 7], 1, "k")
                compress(kc_f, nullk, 0)
                qkv_part([8, 9, 10, 11], 1, "v")
                compress(vc_f, nullv, 1)
                qkv_part([0, 1, 2, 3], 0, "q")
                for ch in range(8):
                    for jc in range(NJC):
                        tpb = (kb.ps.h[0:128, 7 * 512:7 * 512 + 64].bitcast(BF16),
                               kb.ps.preg(7))
                        pe_transpose(tpb,
                                     (vc_f.ap(0, 128, ch * JP + jc * 128,
                                              ch * JP + (jc + 1) * 128),
                                      vc_f.reg(0, 128, ch * JP + jc * 128,
                                               ch * JP + (jc + 1) * 128)),
                                     tag=f"avt{b}{ch}{jc}")
                        kb.copy("vector",
                                vt_f.at(0, 128, (ch * NJC + jc) * 128,
                                        (ch * NJC + jc + 1) * 128),
                                tpb, tag=f"avte{b}{ch}{jc}")
            with contextlib.ExitStack() as sa:
                masks = SB(nc, sa, "a1_msk", 16 * J, BF16)
                att = SB(nc, sa, "a1_att", NJC * 512, BF16)
                smf = SB(nc, sa, "a1_sm", J, F32)
                pnb = SB(nc, sa, "a1_pn", JP, BF16)
                pbb = SB(nc, sa, "a1_pb", JP, BF16)
                stat = SB(nc, sa, "a1_stt", 4, F32)
                o_f = SB(nc, sa, "a1_o", NH * 512, BF16, parts=64)
                wout_bf = SB(nc, sa, "a1_wo", NH * 1024, BF16, parts=64)
                wst = SB(nc, sa, "a1_ws", 1024, F32, parts=64)
                ot1 = SB(nc, sa, "a1_ot", TC, F32)
                kb.memset("gpsimd", pnb.at(), 0.0, tag=f"apz{b}")
                kb.memset("gpsimd", pbb.at(), 0.0, tag=f"apz2{b}")
                for ic in range(16):
                    kb.load("sync", masks.at(0, 128, ic * J, (ic + 1) * J),
                            W["l1a_mask"].slc((ic * 128, (ic + 1) * 128), (0, J)),
                            tag=f"aml{b}{ic}")
                for h in range(NH):
                    kb.load("sync", wst.at(0, 64, 0, 1024),
                            W["l1a_woutT"].slc((h * 64, (h + 1) * 64), (0, 1024)),
                            tag=f"awo{b}{h}")
                    kb.copy("vector", wout_bf.at(0, 64, h * 1024, (h + 1) * 1024),
                            wst.at(0, 64, 0, 1024), tag=f"awoc{b}{h}")
                for ig in range(4):
                    for h in range(NH):
                        ch = h // 2
                        p0 = 64 * (h % 2)
                        for ic4 in range(4):
                            ic = ig * 4 + ic4
                            i0 = ic * 128
                            for jh in range(2):
                                j0 = jh * 512
                                nj = 512 if jh == 0 else J - 512
                                sb2 = ps.bank(jh, 0, nj)
                                kb.mm_group(
                                    sb2,
                                    [((q_f.ap(p0, p0 + 64, ch * T + i0, ch * T + i0 + 128),
                                       q_f.reg(p0, p0 + 64, ch * T + i0, ch * T + i0 + 128)),
                                      (kc_f.ap(p0, p0 + 64, ch * JP + j0, ch * JP + j0 + nj),
                                       kc_f.reg(p0, p0 + 64, ch * JP + j0, ch * JP + j0 + nj)))],
                                    tag=f"S{b}{h}{ic}{jh}")
                                kb.tt("vector", ALU.add,
                                      smf.at(0, 128, j0, j0 + nj), sb2,
                                      masks.at(0, 128, ic * J + j0, ic * J + j0 + nj),
                                      tag=f"Sm{b}{h}{ic}{jh}")
                            kb.reduce(ALU.max, stat.at(0, 128, 0, 1), smf.at(),
                                      negate=True, tag=f"Sr{b}{h}{ic}")
                            kb.act(AF.Exp, pbb.at(0, 128, 0, J), smf.at(),
                                   bias=stat.ap(0, 128, 0, 1),
                                   extra=[stat.reg(0, 128, 0, 1)], tag=f"Se{b}{h}{ic}")
                            kb.reduce(ALU.add, stat.at(0, 128, 1, 2),
                                      pbb.at(0, 128, 0, J), tag=f"Ss{b}{h}{ic}")
                            kb.recip(stat.at(0, 128, 2, 3), stat.at(0, 128, 1, 2),
                                     tag=f"Sv{b}{h}{ic}")
                            kb.ts(pnb.at(0, 128, 0, J), pbb.at(0, 128, 0, J),
                                  stat.ap(0, 128, 2, 3), ALU.mult,
                                  extra=[stat.reg(0, 128, 2, 3)], tag=f"Sn{b}{h}{ic}")
                            if DBG.get("l1a_tap") and b == 0 and h == 0 and ic == 0:
                                smt = DR(nc, "tap_sm1", [128, J], F32,
                                         kind="ExternalOutput")
                                pnt = DR(nc, "tap_pn1", [128, JP], BF16,
                                         kind="ExternalOutput")
                                qt_ = DR(nc, "tap_q1", [64, 128], BF16,
                                         kind="ExternalOutput")
                                kct = DR(nc, "tap_kc1", [64, JP], BF16,
                                         kind="ExternalOutput")
                                kb.store("gpsimd", smt.slc((0, 128), (0, J)),
                                         smf.at(), tag="t1sm")
                                kb.store("gpsimd", pnt.slc((0, 128), (0, JP)),
                                         pnb.at(), tag="t1pn")
                                kb.store("gpsimd", qt_.slc((0, 64), (0, 128)),
                                         (q_f.ap(0, 64, 0, 128),
                                          q_f.reg(0, 64, 0, 128)), tag="t1q")
                                kb.store("gpsimd", kct.slc((0, 64), (0, JP)),
                                         (kc_f.ap(0, 64, 0, JP),
                                          kc_f.reg(0, 64, 0, JP)), tag="t1kc")
                            for jc in range(NJC):
                                dstp = (kb.ps.h[0:128, 7 * 512 + jc * 64:
                                                7 * 512 + (jc + 1) * 64].bitcast(BF16),
                                        kb.ps.preg(7))
                                pe_transpose(dstp,
                                             pnb.at(0, 128, jc * 128, (jc + 1) * 128),
                                             tag=f"St{b}{h}{ic}{jc}")
                            dstap = bass.AP(att.h, ic4 * 128,
                                            [[att.cols, 128], [512, NJC], [1, 128]])
                            srcap = kb.ps.h[0:128, 7 * 512:7 * 512 + NJC * 64].bitcast(BF16)

                            def evfn(e, oap=dstap, iap=srcap):
                                return e.tensor_copy(oap, iap)
                            em.op("vector", evfn, reads=[kb.ps.preg(7)],
                                  writes=[att.reg(0, 128, 0, NJC * 512)],
                                  tag=f"Ste{b}{h}{ic}")
                        avp = ps.bank(2 + (h % 2), 0, 512, 0, 64)
                        pairs = []
                        for jc in range(NJC):
                            pairs.append(
                                ((vt_f.ap(0, 128, (ch * NJC + jc) * 128 + p0,
                                          (ch * NJC + jc) * 128 + p0 + 64),
                                  vt_f.reg(0, 128, (ch * NJC + jc) * 128 + p0,
                                           (ch * NJC + jc) * 128 + p0 + 64)),
                                 att.at(0, 128, jc * 512, (jc + 1) * 512)))
                        kb.mm_group(avp, pairs, tag=f"AV{b}{h}{ig}")
                        kb.copy("vector", o_f.at(0, 64, h * 512, (h + 1) * 512),
                                avp, tag=f"AO{b}{h}{ig}")
                    if DBG.get("l1a_tap") and b == 0 and ig == 0:
                        oft = DR(nc, "tap_of", [64, NH * 512], BF16,
                                 kind="ExternalOutput")
                        attt = DR(nc, "tap_att", [128, NJC * 512], BF16,
                                  kind="ExternalOutput")
                        kb.store("gpsimd", oft.slc((0, 64), (0, NH * 512)),
                                 o_f.at(0, 64, 0, NH * 512), tag="t2of")
                        kb.store("gpsimd", attt.slc((0, 128), (0, NJC * 512)),
                                 att.at(0, 128, 0, NJC * 512), tag="t2att")
                    t0 = ig * 512
                    for oc in range(DK):
                        pt = ps.bank(4 + oc % 2, 0, TC)
                        pairs = [((wout_bf.ap(0, 64, h2 * 1024 + oc * 128,
                                              h2 * 1024 + (oc + 1) * 128),
                                   wout_bf.reg(0, 64, h2 * 1024 + oc * 128,
                                               h2 * 1024 + (oc + 1) * 128)),
                                  o_f.at(0, 64, h2 * 512, (h2 + 1) * 512))
                                 for h2 in range(NH)]
                        kb.mm_group(pt, pairs, tag=f"AW{b}{ig}{oc}")
                        kb.ts(ot1.at(), pt, bout.ap(0, 128, oc, oc + 1), ALU.add,
                              extra=[bout.reg()], tag=f"AWb{b}{ig}{oc}")
                        kb.store("gpsimd",
                                 aout.slc(b, (oc * 128, (oc + 1) * 128), (t0, t0 + TC)),
                                 ot1.at(), tag=f"AWs{b}{ig}{oc}")

    with contextlib.ExitStack() as s9:
        f_sb = SB(nc, s9, "a9_f", DK * TC, F32)
        xrr = SB(nc, s9, "a9_x", DK * TC, F32)
        stage = SB(nc, s9, "a9_st", 2 * DK * TC, BF16)
        tt1 = SB(nc, s9, "a9_t1", TC, F32)
        tt2 = SB(nc, s9, "a9_t2", TC, F32)
        kb._lntap_row = tt2
        for b in range(B):
            for c in range(NCH):
                t0 = c * TC
                for dk in range(DK):
                    kb.load("sync", f_sb.at(0, 128, dk * TC, (dk + 1) * TC),
                            aout.slc(b, (dk * 128, (dk + 1) * 128), (t0, t0 + TC)),
                            tag=f"Z1{b}{c}{dk}")
                    kb.load("scalar", xrr.at(0, 128, dk * TC, (dk + 1) * TC),
                            x_in.slc(b, (dk * 128, (dk + 1) * 128), (t0, t0 + TC)),
                            tag=f"Z2{b}{c}{dk}")
                ft = [f_sb.at(0, 128, dk * TC, (dk + 1) * TC) for dk in range(DK)]
                mub, rsb = kb.ln_stats(ft, TC, (0, 2), stage, tag=f"Zl{b}{c}")
                for dk in range(DK):
                    kb.tt("vector", ALU.add, tt1.at(), ft[dk], mub, tag=f"Za{b}{c}{dk}")
                    kb.tt("vector", ALU.mult, tt2.at(), tt1.at(), rsb, tag=f"Zm{b}{c}{dk}")
                    kb.tt("vector", ALU.add, xrr.at(0, 128, dk * TC, (dk + 1) * TC),
                          xrr.at(0, 128, dk * TC, (dk + 1) * TC), tt2.at(),
                          tag=f"Zr{b}{c}{dk}")
                    kb.store("gpsimd",
                             x_out.slc(b, (dk * 128, (dk + 1) * 128), (t0, t0 + TC)),
                             xrr.at(0, 128, dk * TC, (dk + 1) * TC), tag=f"Zs{b}{c}{dk}")




def host_prep_l1f(inp):
    out = {}
    out["l1f_wgT"] = _f32(np.asarray(inp["L1_ffn_wg"]).T)
    out["l1f_bg"] = _f32(np.asarray(inp["L1_ffn_bg"]).reshape(1, 8))
    out["l1f_WgT"] = _f32(np.asarray(inp["L1_ffn_Wgate"]).transpose(0, 2, 1))
    out["l1f_WuT"] = _f32(np.asarray(inp["L1_ffn_Wup"]).transpose(0, 2, 1))
    out["l1f_WdT"] = _f32(np.asarray(inp["L1_ffn_Wdown"]).transpose(0, 2, 1))
    return out


def emit_l1f(kb, x_in, x_out, W):
    nc, st, em = kb.nc, kb.stack, kb.em
    ps = kb.ps
    NE = 8
    HID = 4096
    TG = 1024
    TC = 512
    NCH = T // TC

    moe = DR(nc, "l1f_moe", [B, D, T], F32)
    wrt = DR(nc, "l1f_wrt", [B, NE, T], F32,
             kind="ExternalOutput" if DBG.get("l1f_tap") else "Internal")
    xn_sp = DR(nc, "l1f_xn", [B, D, T], BF16)

    wg_bf = SB(nc, st, "m_wg", DK * 8, BF16)
    bg_bf = SB(nc, st, "m_bg", 8, BF16, parts=1)
    with contextlib.ExitStack() as s2:
        wst = SB(nc, s2, "m_wst0", DK * 8, F32)
        kb.load("sync", wst.at(), W["l1f_wgT"].wslice(None, D, 0, 8), tag="mwg")
        kb.copy("vector", wg_bf.at(), wst.at(), tag="mwgc")
        kb.load("sync", wst.at(0, 1, 0, 8), W["l1f_bg"].slc((0, 1), (0, 8)), tag="mbg")
        kb.copy("vector", bg_bf.at(0, 1, 0, 8), wst.at(0, 1, 0, 8), tag="mbgc")

    with contextlib.ExitStack() as s1:
        x_sb = SB(nc, s1, "m_x", DK * TC, F32)
        stage = SB(nc, s1, "m_st", 2 * DK * TC, BF16)
        tt1 = SB(nc, s1, "m_t1", TC, F32)
        xnc = SB(nc, s1, "m_xn", DK * TC, BF16)
        lsb = SB(nc, s1, "m_l", 8, F32)
        rsb_ = SB(nc, s1, "m_r", 16, F32)
        wsb = SB(nc, s1, "m_w", 8, F32)
        e1 = SB(nc, s1, "m_e1", 8, F32)
        e2 = SB(nc, s1, "m_e2", 8, F32)
        e3 = SB(nc, s1, "m_e3", 8, F32)
        wrow = SB(nc, s1, "m_wr", 128, F32, parts=8)
        for b in range(B):
            for c in range(NCH):
                t0 = c * TC
                for dk in range(DK):
                    kb.load("sync", x_sb.at(0, 128, dk * TC, (dk + 1) * TC),
                            x_in.slc(b, (dk * 128, (dk + 1) * 128), (t0, t0 + TC)),
                            tag=f"Mx{b}{c}{dk}")
                xt = [x_sb.at(0, 128, dk * TC, (dk + 1) * TC) for dk in range(DK)]
                mub, rsb = kb.ln_stats(xt, TC, (0, 2), stage, tag=f"Ml{b}{c}")
                for dk in range(DK):
                    kb.tt("vector", ALU.add, tt1.at(), xt[dk], mub, tag=f"Ma{b}{c}{dk}")
                    kb.tt("vector", ALU.mult, xnc.at(0, 128, dk * TC, (dk + 1) * TC),
                          tt1.at(), rsb, tag=f"Mm{b}{c}{dk}")
                    kb.store("gpsimd",
                             xn_sp.slc(b, (dk * 128, (dk + 1) * 128), (t0, t0 + TC)),
                             xnc.at(0, 128, dk * TC, (dk + 1) * TC), tag=f"Ms{b}{c}{dk}")
                for tt_ in range(4):
                    i0 = tt_ * 128
                    lp = ps.bank(4, 0, 8)
                    pairs = [((xnc.ap(0, 128, dk * TC + i0, dk * TC + i0 + 128),
                               xnc.reg(0, 128, dk * TC + i0, dk * TC + i0 + 128)),
                              wg_bf.at(0, 128, dk * 8, (dk + 1) * 8))
                             for dk in range(DK)]
                    pairs.append(((kb.ones_row.ap(0, 1, 0, 128), kb.ones_row.reg()),
                                  bg_bf.at(0, 1, 0, 8)))
                    kb.mm_group(lp, pairs, tag=f"Mr{b}{c}{tt_}")
                    kb.copy("vector", lsb.at(), lp, tag=f"Mre{b}{c}{tt_}")
                    if DBG.get("l1f_tap") and b == 0 and c == 0 and tt_ == 0:
                        lt_ = DR(nc, "tap_lg", [128, 48], F32, kind="ExternalOutput")
                        kb.store("gpsimd", lt_.slc((0, 128), (0, 8)), lsb.at(),
                                 tag="tlg")
                    kb.reduce(ALU.max, rsb_.at(0, 128, 0, 1), lsb.at(),
                              tag=f"Mm1{b}{c}{tt_}")
                    kb.reduce(ALU.max, rsb_.at(0, 128, 1, 2), lsb.at(), negate=True,
                              tag=f"Mn1{b}{c}{tt_}")
                    kb.ts(e1.at(), lsb.at(), rsb_.ap(0, 128, 0, 1), ALU.is_equal,
                          extra=[rsb_.reg(0, 128, 0, 1)], tag=f"Meq{b}{c}{tt_}")
                    kb.stt(e2.at(), e1.at(), -1e30, lsb.at(), ALU.mult, ALU.add,
                           tag=f"Ml2{b}{c}{tt_}")
                    kb.reduce(ALU.max, rsb_.at(0, 128, 2, 3), e2.at(),
                              tag=f"Mm2{b}{c}{tt_}")
                    kb.ts(e1.at(), lsb.at(), rsb_.ap(0, 128, 2, 3), ALU.is_ge,
                          extra=[rsb_.reg(0, 128, 2, 3)], tag=f"Mkp{b}{c}{tt_}")
                    kb.act(AF.Exp, e2.at(), lsb.at(), bias=rsb_.ap(0, 128, 1, 2),
                           extra=[rsb_.reg(0, 128, 1, 2)], tag=f"Mex{b}{c}{tt_}",
                           flush=True)
                    kb.tt("vector", ALU.mult, e3.at(), e1.at(), e2.at(),
                          tag=f"Mpk{b}{c}{tt_}")
                    kb.reduce(ALU.add, rsb_.at(0, 128, 3, 4), e3.at(),
                              tag=f"Msu{b}{c}{tt_}")
                    kb.recip(rsb_.at(0, 128, 4, 5), rsb_.at(0, 128, 3, 4),
                             tag=f"Mrc{b}{c}{tt_}")
                    kb.ts(wsb.at(), e3.at(), rsb_.ap(0, 128, 4, 5), ALU.mult,
                          extra=[rsb_.reg(0, 128, 4, 5)], tag=f"Mwn{b}{c}{tt_}")
                    if DBG.get("l1f_tap") and b == 0 and c == 0 and tt_ == 0:
                        kb.store("gpsimd", lt_.slc((0, 128), (8, 16)), e1.at(), tag="te1")
                        kb.store("gpsimd", lt_.slc((0, 128), (16, 24)), e2.at(), tag="te2")
                        kb.store("gpsimd", lt_.slc((0, 128), (24, 32)), e3.at(), tag="te3")
                        kb.store("gpsimd", lt_.slc((0, 128), (32, 40)), wsb.at(), tag="tw")
                        kb.store("gpsimd", lt_.slc((0, 128), (40, 45)),
                                 rsb_.at(0, 128, 0, 5), tag="tr")
                    tp = (kb.ps.h[0:8, 7 * 512:7 * 512 + 128], kb.ps.preg(7))

                    def tfn(e, oap=tp[0], iap=wsb.ap()):
                        return e.transpose(oap, iap, kb.ident_f.ap())
                    em.op("tensor", tfn, reads=[wsb.reg(), kb.ident_f.reg()],
                          writes=[kb.ps.preg(7)], tag=f"Mtp{b}{c}{tt_}")
                    kb.copy("vector", wrow.at(0, 8, 0, 128), tp, tag=f"Mte{b}{c}{tt_}")
                    wap = bass.AP(wrt.t, b * NE * T + t0 + i0, [[T, 8], [1, 128]])
                    kb.store("gpsimd",
                             (wap, ("D" + wrt.name, 0, 1, b * NE * T + t0 + i0,
                                    b * NE * T + (NE - 1) * T + t0 + i0 + 128)),
                             wrow.at(0, 8, 0, 128), tag=f"Mts{b}{c}{tt_}")

    with contextlib.ExitStack() as s3:
        xng = SB(nc, s3, "m_xng", DK * TG, BF16)
        hact = SB(nc, s3, "m_h", 32 * TG, BF16)
        wstage = SB(nc, s3, "m_ws", DK * 512, F32)
        wgb = SB(nc, s3, "m_wgb", DK * 512, BF16)
        wub = SB(nc, s3, "m_wub", DK * 512, BF16)
        sg = SB(nc, s3, "m_sg", TC, F32)
        wdst = SB(nc, s3, "m_wds", 32 * 128, F32)
        wdb = SB(nc, s3, "m_wdb", 2 * 32 * 128, BF16)
        wbc = SB(nc, s3, "m_wbc", TG, F32)
        oacc = SB(nc, s3, "m_oac", TC, F32)
        for b in range(B):
            for g in range(T // TG):
                t0 = g * TG
                for dk in range(DK):
                    kb.load("sync", xng.at(0, 128, dk * TG, (dk + 1) * TG),
                            xn_sp.slc(b, (dk * 128, (dk + 1) * 128), (t0, t0 + TG)),
                            tag=f"Gx{b}{g}{dk}")
                for e in range(NE):
                    kb.load("scalar", wbc.at(),
                            wrt.bcast(128, b, (e, e + 1), (t0, t0 + TG)),
                            tag=f"Gw{b}{g}{e}")
                    for hc in range(8):
                        kb.load("sync", wstage.at(),
                                W["l1f_WgT"].wslice(e, D, hc * 512, (hc + 1) * 512),
                                tag=f"Gg{b}{g}{e}{hc}")
                        kb.copy("vector", wgb.at(), wstage.at(), tag=f"Ggc{b}{g}{e}{hc}")
                        kb.load("sync", wstage.at(),
                                W["l1f_WuT"].wslice(e, D, hc * 512, (hc + 1) * 512),
                                tag=f"Gu{b}{g}{e}{hc}")
                        kb.copy("vector", wub.at(), wstage.at(), tag=f"Guc{b}{g}{e}{hc}")
                        for hs in range(4):
                            for tt_ in range(2):
                                bk = (hs * 2 + tt_) % 4
                                gp = ps.bank(bk, 0, TC)
                                up = ps.bank(4 + bk, 0, TC)
                                gpairs = [((wgb.ap(0, 128, dk * 512 + hs * 128,
                                                   dk * 512 + (hs + 1) * 128),
                                            wgb.reg(0, 128, dk * 512 + hs * 128,
                                                    dk * 512 + (hs + 1) * 128)),
                                           xng.at(0, 128, dk * TG + tt_ * TC,
                                                  dk * TG + (tt_ + 1) * TC))
                                          for dk in range(DK)]
                                kb.mm_group(gp, gpairs, tag=f"GG{b}{g}{e}{hc}{hs}{tt_}")
                                upairs = [((wub.ap(0, 128, dk * 512 + hs * 128,
                                                   dk * 512 + (hs + 1) * 128),
                                            wub.reg(0, 128, dk * 512 + hs * 128,
                                                    dk * 512 + (hs + 1) * 128)),
                                           xng.at(0, 128, dk * TG + tt_ * TC,
                                                  dk * TG + (tt_ + 1) * TC))
                                          for dk in range(DK)]
                                kb.mm_group(up, upairs, tag=f"GU{b}{g}{e}{hc}{hs}{tt_}")
                                kb.act(AF.Silu, (sg.ap(), sg.reg()), gp,
                                       tag=f"GS{b}{g}{e}{hc}{hs}{tt_}")
                                hrow = hc * 4 + hs
                                kb.tt("vector", ALU.mult,
                                      hact.at(0, 128, hrow * TG + tt_ * TC,
                                              hrow * TG + (tt_ + 1) * TC),
                                      up, sg.at(), tag=f"GH{b}{g}{e}{hc}{hs}{tt_}")
                    for oc in range(DK):
                        op_ = oc % 2
                        kb.load("sync", wdst.at(),
                                W["l1f_WdT"].wslice(e, HID, oc * 128, (oc + 1) * 128),
                                tag=f"Gd{b}{g}{e}{oc}")
                        kb.copy("vector", wdb.at(0, 128, op_ * 32 * 128,
                                                 (op_ + 1) * 32 * 128),
                                wdst.at(), tag=f"Gdc{b}{g}{e}{oc}")
                        for tt_ in range(2):
                            dp = ps.bank(oc % 4, 0, TC)
                            pairs = [((wdb.ap(0, 128, op_ * 32 * 128 + hk * 128,
                                              op_ * 32 * 128 + (hk + 1) * 128),
                                       wdb.reg(0, 128, op_ * 32 * 128 + hk * 128,
                                               op_ * 32 * 128 + (hk + 1) * 128)),
                                      hact.at(0, 128, hk * TG + tt_ * TC,
                                              hk * TG + (tt_ + 1) * TC))
                                     for hk in range(32)]
                            kb.mm_group(dp, pairs, tag=f"GD{b}{g}{e}{oc}{tt_}")
                            kb.tt("vector", ALU.mult, oacc.at(), dp,
                                  wbc.at(0, 128, tt_ * TC, (tt_ + 1) * TC),
                                  tag=f"GE{b}{g}{e}{oc}{tt_}")
                            kb.store("gpsimd",
                                     moe.slc(b, (oc * 128, (oc + 1) * 128),
                                             (t0 + tt_ * TC, t0 + (tt_ + 1) * TC)),
                                     oacc.at(), accum=(e > 0),
                                     tag=f"GT{b}{g}{e}{oc}{tt_}")

    with contextlib.ExitStack() as s9:
        f_sb = SB(nc, s9, "m9_f", DK * TC, F32)
        xrr = SB(nc, s9, "m9_x", DK * TC, F32)
        stage = SB(nc, s9, "m9_st", 2 * DK * TC, BF16)
        tt1 = SB(nc, s9, "m9_t1", TC, F32)
        tt2 = SB(nc, s9, "m9_t2", TC, F32)
        for b in range(B):
            for c in range(NCH):
                t0 = c * TC
                for dk in range(DK):
                    kb.load("sync", f_sb.at(0, 128, dk * TC, (dk + 1) * TC),
                            moe.slc(b, (dk * 128, (dk + 1) * 128), (t0, t0 + TC)),
                            tag=f"Y1{b}{c}{dk}")
                    kb.load("scalar", xrr.at(0, 128, dk * TC, (dk + 1) * TC),
                            x_in.slc(b, (dk * 128, (dk + 1) * 128), (t0, t0 + TC)),
                            tag=f"Y2{b}{c}{dk}")
                ft = [f_sb.at(0, 128, dk * TC, (dk + 1) * TC) for dk in range(DK)]
                mub, rsb = kb.ln_stats(ft, TC, (0, 2), stage, tag=f"Yl{b}{c}")
                for dk in range(DK):
                    kb.tt("vector", ALU.add, tt1.at(), ft[dk], mub, tag=f"Ya{b}{c}{dk}")
                    kb.tt("vector", ALU.mult, tt2.at(), tt1.at(), rsb, tag=f"Ym{b}{c}{dk}")
                    kb.tt("vector", ALU.add, xrr.at(0, 128, dk * TC, (dk + 1) * TC),
                          xrr.at(0, 128, dk * TC, (dk + 1) * TC), tt2.at(),
                          tag=f"Yr{b}{c}{dk}")
                    kb.store("gpsimd",
                             x_out.slc(b, (dk * 128, (dk + 1) * 128), (t0, t0 + TC)),
                             xrr.at(0, 128, dk * TC, (dk + 1) * TC), tag=f"Ys{b}{c}{dk}")



L0A_INPUTS = [("l0a_inwT", [D, 4096]), ("l0a_convw", [128, 64]),
              ("l0a_convb", [128, 16]), ("l0a_xprojT", [2048, 128]),
              ("l0a_dtwT", [64, 2048]), ("l0a_dtb", [128, 16]),
              ("l0a_Alog", [128, 256]), ("l0a_diagD", [128, 16 * 128]),
              ("l0a_outwT", [2048, 1024]), ("l0a_lng", [128, 8]),
              ("l0a_lnb", [128, 8])]

L0F_INPUTS = [("l0f_wqkv", [2, 1024, 1536]), ("l0f_wo", [2, 512, 1024]),
              ("l0f_qs", [128, 2]), ("l0f_ks", [128, 2]),
              ("l0f_lng", [128, 16]), ("l0f_lnb", [128, 16]),
              ("l0f_w1p", [2, 1024, 5632]), ("l0f_b1p", [128, 88]),
              ("l0f_w2p", [2, 2816, 1024]), ("l0f_b2", [128, 16]),
              ("cst_e2", [128, 2]), ("cst_e2b", [2, 128]), ("cst_rot", [128, 128]),
              ("l0f_qcs", [128, T]), ("l0f_qss", [128, T]),
              ("l0f_kcs", [128, T]), ("l0f_kss", [128, T]),
              ("l0f_mask", [2, 512, 1024])]

L1A_INPUTS = [("l1a_wqkvT", [D, 3072]), ("l1a_wconv2", [8, 3, 128, 128]),
              ("l1a_bconv", [128, 8]), ("l1a_nullk", [128, 8]),
              ("l1a_nullv", [128, 8]), ("l1a_woutT", [D, D]),
              ("l1a_bout", [128, 8]), ("l1a_mask", [T, 684], BF16)]

L1F_INPUTS = [("l1f_wgT", [D, 8]), ("l1f_bg", [1, 8]),
              ("l1f_WgT", [8, D, 4096]), ("l1f_WuT", [8, D, 4096]),
              ("l1f_WdT", [8, 4096, D])]

BLOCKS = {
    "l0a": (emit_l0a, L0A_INPUTS, host_prep_l0a),
    "l0f": (emit_l0f, L0F_INPUTS, host_prep_l0f),
    "l1a": (emit_l1a, L1A_INPUTS, host_prep_l1a),
    "l1f": (emit_l1f, L1F_INPUTS, host_prep_l1f),
}


def _build(block_names):
    nc = bass.Bass("TRN2", target_bir_lowering=False)
    stack = contextlib.ExitStack()
    em = Em(nc)
    x_in = DR(nc, "x_fm", [B, D, T], F32, kind="ExternalInput")
    x_out = DR(nc, "xres_out", [B, D, T], F32, kind="ExternalOutput")
    W = {}
    for bn in block_names:
        for ent in BLOCKS[bn][1]:
            n, s = ent[0], ent[1]
            dt = ent[2] if len(ent) > 2 else F32
            if n not in W:
                W[n] = DR(nc, n, s, dt, kind="ExternalInput")
    kb = Kb(nc, stack, em)
    cur_in = x_in
    for i, bn in enumerate(block_names):
        cur_out = x_out if i == len(block_names) - 1 else DR(
            nc, f"xmid_{i}", [B, D, T], F32)
        BLOCKS[bn][0](kb, cur_in, cur_out, W)
        cur_in = cur_out
    em.finalize()
    stack.close()
    nc.finalize()
    return nc


PIPELINE = [["l0a"], ["l0f"], ["l1a"], ["l1f"]]


def kernel(**inputs):
    x = np.asarray(inputs["L0_attn_x"])
    Wh = {}
    seen = set()
    for grp in PIPELINE:
        for bn in grp:
            if bn not in seen:
                Wh.update(BLOCKS[bn][2](inputs))
                seen.add(bn)
    xc = [np.ascontiguousarray(x[c * B:(c + 1) * B].transpose(0, 2, 1),
                               dtype=np.float32) for c in range(NCORES)]
    for grp in PIPELINE:
        nc = _build(grp)
        need = set()
        for bn in grp:
            need.update(ent[0] for ent in BLOCKS[bn][1])
        in_maps = []
        for c in range(NCORES):
            m = {"x_fm": xc[c]}
            for n in need:
                m[n] = Wh[n]
            in_maps.append(m)
        res = run_bass_kernel_spmd(nc, in_maps, core_ids=list(range(NCORES)))
        xc = [np.asarray(res.results[c]["xres_out"]) for c in range(NCORES)]
    outs = [o.transpose(0, 2, 1) for o in xc]
    return np.ascontiguousarray(np.concatenate(outs, axis=0), dtype=np.float32)


# version 05 (final)
# speedup vs baseline: 1.0132x; 1.0132x over previous
import contextlib
import numpy as np
import ml_dtypes
import concourse.bass as bass
import concourse.mybir as mybir
from concourse.bass_utils import run_bass_kernel_spmd

F32 = mybir.dt.float32
BF16 = mybir.dt.bfloat16
AF = mybir.ActivationFunctionType
ALU = mybir.AluOpType
AX = mybir.AxisListType

NCORES = 8
B = 2
T = 2048
D = 1024
DK = D // 128
LN_EPS = 1e-5

ENGINES = ("sync", "vector", "scalar", "tensor", "gpsimd")
_PAGE = 8192



class Op:
    __slots__ = ("idx", "eng", "fn", "reads", "writes", "is_dma",
                 "needs_inc", "inc_prefix", "waits", "tag", "pre_wait", "slow_wb",
                 "slow_rd")

    def __init__(self, idx, eng, fn, reads, writes, is_dma, tag=""):
        self.idx = idx
        self.eng = eng
        self.fn = fn
        self.reads = reads
        self.writes = writes
        self.is_dma = is_dma
        self.needs_inc = is_dma
        self.inc_prefix = 0
        self.waits = []
        self.tag = tag
        self.pre_wait = None
        self.slow_wb = False
        self.slow_rd = False


def _ovl(a, b):
    return (a[1] < b[2] and b[1] < a[2] and a[3] < b[4] and b[3] < a[4])


class Em:
    def __init__(self, nc):
        self.nc = nc
        self.ops = []
        self._w = {}
        self._r = {}
        self._edges = []
        self._qcount = {e: 0 for e in ENGINES}
        self._last_verify = {}

    @staticmethod
    def _pages(reg):
        return range(reg[3] // _PAGE, max(reg[4] - 1, reg[3]) // _PAGE + 1)

    def _append(self, o):
        o.idx = len(self.ops)
        self.ops.append(o)
        return o.idx

    def _mk_verify(self, q):
        cum = self._qcount[q]
        o = Op(0, q, (lambda e: e.nop()), [], [], False, tag=f"verify_{q}_{cum}")
        o.pre_wait = (("dma", q), 16 * cum)
        idx = self._append(o)
        self._edges.append(set())
        self._last_verify[q] = (cum, idx)
        return idx

    def op(self, eng, fn, reads=(), writes=(), is_dma=False, tag="", slow=False,
           slow_rd=False):
        o = Op(0, eng, fn, list(reads), list(writes), is_dma, tag)
        o.slow_wb = slow
        o.slow_rd = slow_rd
        deps = set()
        for r in o.reads:
            sp = self._w.get(r[0])
            if sp:
                for pg in self._pages(r):
                    for key, widx in sp.get(pg, {}).items():
                        if _ovl(r, (r[0],) + key):
                            deps.add(widx)
        for w in o.writes:
            sp = self._w.get(w[0])
            if sp:
                for pg in self._pages(w):
                    for key, widx in sp.get(pg, {}).items():
                        if _ovl(w, (w[0],) + key):
                            deps.add(widx)
            sp = self._r.get(w[0])
            if sp:
                for pg in self._pages(w):
                    for key, ridx in sp.get(pg, {}).items():
                        if _ovl(w, (w[0],) + key[:4]):
                            deps.add(ridx)
        ndeps = set()
        for j in deps:
            p = self.ops[j]
            if p.is_dma:
                q = p.eng
                if is_dma and q == eng:
                    continue
                lv = self._last_verify.get(q)
                if lv is None or lv[0] < p.inc_prefix:
                    ndeps.add(self._mk_verify(q))
                else:
                    ndeps.add(lv[1])
            else:
                ndeps.add(j)
        if is_dma:
            self._qcount[eng] += 1
            o.inc_prefix = self._qcount[eng]
        idx = self._append(o)
        ndeps.discard(idx)
        self._edges.append(ndeps)
        for w in o.writes:
            sp = self._w.setdefault(w[0], {})
            spr = self._r.setdefault(w[0], {})
            for pg in self._pages(w):
                d = sp.get(pg)
                if d:
                    for k in [k for k in d if _ovl(w, (w[0],) + k)]:
                        del d[k]
                dr = spr.get(pg)
                if dr:
                    for k in [k for k in dr if _ovl(w, (w[0],) + k[:4])]:
                        del dr[k]
                sp.setdefault(pg, {})[(w[1], w[2], w[3], w[4])] = idx
        for r in o.reads:
            sp = self._r.setdefault(r[0], {})
            for pg in self._pages(r):
                sp.setdefault(pg, {})[(r[1], r[2], r[3], r[4], eng)] = idx
        return o

    def finalize(self):
        nc = self.nc
        ops = self.ops

        def _skip(p, o):
            return (p.eng == o.eng and not p.is_dma
                    and p.eng in ("tensor", "gpsimd")
                    and not p.slow_wb and not o.slow_rd)

        for o in ops:
            for j in self._edges[o.idx]:
                p = ops[j]
                if _skip(p, o):
                    continue
                p.needs_inc = True
        dma_count = dict(self._qcount)
        inc_count = {e: 0 for e in ENGINES}
        eng_ops = {e: [] for e in ENGINES}
        for o in ops:
            eng_ops[o.eng].append(o)
            if not o.is_dma and o.needs_inc:
                inc_count[o.eng] += 1
                o.inc_prefix = inc_count[o.eng]
        guaranteed = {}
        for o in ops:
            ws = []
            if getattr(o, "pre_wait", None) is not None:
                ws.append(o.pre_wait)
            vals = []
            for j in self._edges[o.idx]:
                p = ops[j]
                if _skip(p, o):
                    continue
                assert not p.is_dma
                vals.append((("eng", p.eng), p.inc_prefix))
            for key, val in sorted(vals, key=lambda kv: -kv[1]):
                gk = (o.eng, key)
                if guaranteed.get(gk, 0) >= val:
                    continue
                guaranteed[gk] = val
                ws.append((key, val))
            o.waits = ws
        self._stack = contextlib.ExitStack()
        sems = {}
        for e in ENGINES:
            if inc_count[e]:
                sems[("eng", e)] = self._stack.enter_context(nc.semaphore(f"sp_{e}"))
            if dma_count[e]:
                sems[("dma", e)] = self._stack.enter_context(nc.semaphore(f"sq_{e}"))

        with nc.Block() as blk:
            def body(eng_obj, elist, ename):
                for o in elist:
                    for key, val in o.waits:
                        eng_obj.wait_ge(sems[key], val)
                    ins = o.fn(eng_obj)
                    if o.is_dma:
                        ins.then_inc(sems[("dma", o.eng)], 16)
                    elif o.needs_inc:
                        assert ins is not None, f"op {o.tag} returned None"
                        ins.then_inc(sems[("eng", o.eng)], 1)
                if ename == "sync":
                    for e in ENGINES:
                        if dma_count[e]:
                            eng_obj.wait_ge(sems[("dma", e)], 16 * dma_count[e])

            blk.sync(lambda e: body(e, eng_ops["sync"], "sync"))
            if eng_ops["vector"]:
                blk.vector(lambda e: body(e, eng_ops["vector"], "vector"))
            if eng_ops["scalar"]:
                blk.scalar(lambda e: body(e, eng_ops["scalar"], "scalar"))
            if eng_ops["tensor"]:
                blk.tensor(lambda e: body(e, eng_ops["tensor"], "tensor"))
            if eng_ops["gpsimd"]:
                blk.gpsimd(lambda e: body(e, eng_ops["gpsimd"], "gpsimd"))



_SB_UID = [0]


class SB:
    def __init__(self, nc, stack, name, cols, dtype, parts=128):
        _SB_UID[0] += 1
        self.h = stack.enter_context(
            nc.sbuf_tensor(f"{name}_{_SB_UID[0]}", [parts, cols], dtype))
        self.cols = cols
        self.parts = parts
        self.esz = 2 if dtype == BF16 else 4
        self.base = nc.lookup_mloc(self.h).addr

    def ap(self, p0=0, p1=None, c0=0, c1=None):
        p1 = self.parts if p1 is None else p1
        c1 = self.cols if c1 is None else c1
        return self.h[p0:p1, c0:c1]

    def reg(self, p0=0, p1=None, c0=0, c1=None):
        p1 = self.parts if p1 is None else p1
        c1 = self.cols if c1 is None else c1
        return ("SB", p0, p1, self.base + c0 * self.esz, self.base + c1 * self.esz)

    def at(self, p0=0, p1=None, c0=0, c1=None):
        return self.ap(p0, p1, c0, c1), self.reg(p0, p1, c0, c1)


class PSx:
    def __init__(self, nc, stack):
        self.h = stack.enter_context(nc.psum_tensor("PSALL", [128, 8 * 512], F32))

    def bank(self, b, c0=0, c1=512, p0=0, p1=128):
        return (self.h[p0:p1, b * 512 + c0: b * 512 + c1],
                ("PS", p0, p1, b, b + 1))

    def preg(self, b):
        return ("PS", 0, 128, b, b + 1)


class DR:
    def __init__(self, nc, name, shape, dtype, kind="Internal"):
        self.t = nc.dram_tensor(name, list(shape), dtype, kind=kind)
        self.name = name
        self.shape = tuple(shape)
        st = []
        s = 1
        for d in reversed(shape):
            st.insert(0, s)
            s *= d
        self.strides = st
        self.size = s

    def slc(self, *ranges):
        offset = 0
        dims = []
        norm = []
        for r in ranges:
            if isinstance(r, int):
                norm.append((r, r + 1, False))
            else:
                norm.append((r[0], r[1], True))
        assert len(norm) == len(self.shape), (self.name, ranges)
        for i, (a, b2, keep) in enumerate(norm):
            offset += a * self.strides[i]
            if keep:
                dims.append([self.strides[i], b2 - a])
        lo = offset
        hi = offset + 1
        for i, (a, b2, _) in enumerate(norm):
            hi += (b2 - a - 1) * self.strides[i]
        if not dims:
            dims = [[1, 1]]
        return bass.AP(self.t, offset, dims), ("D" + self.name, 0, 1, lo, hi)

    def wslice(self, lidx, din, c0, c1, nd=None):
        nd = (din // 128) if nd is None else nd
        dout = self.shape[-1]
        off = c0 + (lidx * self.strides[0] if lidx is not None else 0)
        dims = [[dout, 128], [128 * dout, nd], [1, c1 - c0]]
        lo = off
        hi = off + (nd * 128 - 1) * dout + (c1 - c0)
        return bass.AP(self.t, off, dims), ("D" + self.name, 0, 1, lo, hi)

    def bcast(self, parts, *ranges):
        ap, reg = self.slc(*ranges)
        dims = [list(d) for d in ap.ap if d[1] > 1]
        if not dims:
            dims = [[1, 1]]
        ap2 = bass.AP(self.t, ap.offset, [[0, parts]] + dims)
        return ap2, reg



class Kb:
    def __init__(self, nc, stack, em):
        self.nc = nc
        self.stack = stack
        self.em = em
        self.ps = PSx(nc, stack)
        self.ident = SB(nc, stack, "c_ident", 128, BF16)
        self.ones_col = SB(nc, stack, "c_ones", 1, BF16)
        self.ones_row = SB(nc, stack, "c_onesr", 128, BF16, parts=1)
        self.pcol = SB(nc, stack, "c_pcol", 1, F32)
        self._scr = SB(nc, stack, "c_scr", 128, F32)
        self.smal = SB(nc, stack, "c_smal", 2048, F32, parts=1)
        self.smal_bf = SB(nc, stack, "c_smalb", 2048, BF16, parts=1)
        self.c_sd = SB(nc, stack, "c_sd", 512, F32)
        self.c_rsb = SB(nc, stack, "c_rsb", 512, F32)
        self.ident_f = SB(nc, stack, "c_identf", 128, F32)
        em_ = em
        scr, ident, pcol = self._scr, self.ident, self.pcol
        em_.op("gpsimd", lambda e: e.iota(scr.ap(), [[1, 128]], channel_multiplier=0,
                                          allow_small_or_imprecise_dtypes=True),
               writes=[scr.reg()], tag="iota_j")
        em_.op("gpsimd", lambda e: e.iota(pcol.ap(), [[0, 1]], channel_multiplier=1,
                                          allow_small_or_imprecise_dtypes=True),
               writes=[pcol.reg()], tag="iota_p")
        em_.op("vector", lambda e: e.tensor_scalar(ident.ap(), scr.ap(), pcol.ap(),
                                                   None, ALU.is_equal),
               reads=[scr.reg(), pcol.reg()], writes=[ident.reg()], tag="ident")
        em_.op("vector", lambda e: e.tensor_scalar(self.ident_f.ap(), scr.ap(),
                                                   pcol.ap(), None, ALU.is_equal),
               reads=[scr.reg(), pcol.reg()], writes=[self.ident_f.reg()],
               tag="identf")
        em_.op("vector", lambda e: e.memset(self.ones_col.ap(), 1.0),
               writes=[self.ones_col.reg()], tag="ones_c")
        em_.op("vector", lambda e: e.memset(self.ones_row.ap(), 1.0),
               writes=[self.ones_row.reg()], tag="ones_r")
        self.constcol(1.0)
        self.constcol(LN_EPS)

    def load(self, queue, sbuf_tv, dram_tv, tag="ld"):
        (dap, dreg) = dram_tv
        (sap, sreg) = sbuf_tv

        def fn(eng):
            return eng.dma_start(out=sap, in_=dap)
        self.em.op(queue, fn, reads=[dreg], writes=[sreg], is_dma=True, tag=tag)

    def store(self, queue, dram_tv, sbuf_tv, tag="st", accum=False):
        (dap, dreg) = dram_tv
        (sap, sreg) = sbuf_tv
        kw = {"accum_op": ALU.add} if accum else {}

        def fn(eng):
            return eng.dma_start(out=dap, in_=sap, **kw)
        self.em.op(queue, fn, reads=[sreg] + ([dreg] if accum else []),
                   writes=[dreg], is_dma=True, tag=tag)

    def mm(self, out_tv, lhsT_tv, rhs_tv, start, stop, tag="mm"):
        (oap, oreg) = out_tv
        (lap, lreg) = lhsT_tv
        (rap, rreg) = rhs_tv

        def fn(eng):
            return eng.matmul(oap, lap, rap, start=start, stop=stop)
        self.em.op("tensor", fn, reads=[lreg, rreg], writes=[oreg], tag=tag)

    def mm_group(self, out_tv, pairs, tag="mmg"):
        (oap, oreg) = out_tv
        reads = []
        for (lap, lreg), (rap, rreg) in pairs:
            reads.extend([lreg, rreg])
        n = len(pairs)

        def fn(eng):
            ins = None
            for i, ((lap, _), (rap, _)) in enumerate(pairs):
                ins = eng.matmul(oap, lap, rap, start=(i == 0), stop=(i == n - 1))
            return ins
        self.em.op("tensor", fn, reads=reads, writes=[oreg], tag=tag)

    def constcol(self, val):
        key = float(val)
        if not hasattr(self, "_ccols"):
            self._ccols = {}
            self._cpool = SB(self.nc, self.stack, "c_cpool", 16, F32)
        if key not in self._ccols:
            i = len(self._ccols)
            assert i < 16
            self.memset("vector", self._cpool.at(0, 128, i, i + 1), key, tag=f"cset{i}")
            self._ccols[key] = i
        i = self._ccols[key]
        return self._cpool, i

    def act(self, func, out_tv, in_tv, bias=0.0, scale=1.0, extra=(), tag="act",
            flush=False):
        (oap, oreg) = out_tv
        (iap, ireg) = in_tv
        extra = list(extra)
        if isinstance(bias, float) and bias != 0.0:
            pool, i = self.constcol(bias)
            np_ = out_tv[1][2] - out_tv[1][1]
            bias = pool.ap(0, np_, i, i + 1)
            extra.append(pool.reg(0, 128, i, i + 1))

        def fn(eng):
            ins = eng.activation(oap, iap, func, bias=bias, scale=scale)
            if flush:
                ins = eng.drain()
            return ins
        self.em.op("scalar", fn, reads=[ireg] + extra, writes=[oreg], tag=tag)

    def tt(self, eng, op, out_tv, a_tv, b_tv, tag="tt"):
        (oap, oreg) = out_tv
        (aap, areg) = a_tv
        (bap, breg) = b_tv

        def fn(e):
            return e.tensor_tensor(oap, aap, bap, op=op)
        self.em.op(eng, fn, reads=[areg, breg], writes=[oreg], tag=tag)

    def ts(self, out_tv, in_tv, s1, op0, s2=None, op1=None, extra=(), tag="ts"):
        (oap, oreg) = out_tv
        (iap, ireg) = in_tv

        def fn(e):
            if s2 is None:
                return e.tensor_single_scalar(oap, iap, s1, op=op0)
            return e.tensor_scalar(oap, iap, s1, s2, op0, op1)
        self.em.op("vector", fn, reads=[ireg] + list(extra), writes=[oreg], tag=tag)

    def stt(self, out_tv, in0_tv, scalar, in1_tv, op0, op1, extra=(), tag="stt"):
        (oap, oreg) = out_tv
        (aap, areg) = in0_tv
        (bap, breg) = in1_tv

        def fn(e):
            return e.scalar_tensor_tensor(oap, aap, scalar, bap, op0, op1)
        self.em.op("vector", fn, reads=[areg, breg] + list(extra),
                   writes=[oreg], tag=tag)

    def copy(self, eng, out_tv, in_tv, tag="cp"):
        (oap, oreg) = out_tv
        (iap, ireg) = in_tv

        def fn(e):
            return e.tensor_copy(oap, iap)
        self.em.op(eng, fn, reads=[ireg], writes=[oreg], tag=tag)

    def memset(self, eng, tv, val, tag="ms"):
        (ap, reg) = tv

        def fn(e):
            return e.memset(ap, val)
        self.em.op(eng, fn, writes=[reg], tag=tag)

    def scan(self, out_tv, a_tv, b_tv, initial, extra=(), tag="scan"):
        (oap, oreg) = out_tv
        (aap, areg) = a_tv
        (bap, breg) = b_tv

        def fn(e):
            return e.tensor_tensor_scan(oap, aap, bap, initial, ALU.mult, ALU.add)
        self.em.op("vector", fn, reads=[areg, breg] + list(extra),
                   writes=[oreg], tag=tag, slow=True)

    def reduce(self, op, out_tv, in_tv, negate=False, tag="red"):
        (oap, oreg) = out_tv
        (iap, ireg) = in_tv

        def fn(e):
            return e.tensor_reduce(oap, iap, axis=AX.X, op=op, negate=negate)
        self.em.op("vector", fn, reads=[ireg], writes=[oreg], tag=tag, slow=True,
                   slow_rd=True)

    def recip(self, out_tv, in_tv, tag="recip"):
        (oap, oreg) = out_tv
        (iap, ireg) = in_tv

        def fn(e):
            return e.reciprocal(oap, iap)
        self.em.op("vector", fn, reads=[ireg], writes=[oreg], tag=tag, slow=True)

    def ln_stats(self, xf_tiles, Tc, banks, stage, tag="ln", n_feat=None):
        nd = len(xf_tiles)
        n = n_feat if n_feat is not None else 128 * nd
        sx = self.ps.bank(banks[0], 0, Tc, 0, 1)
        sxx = self.ps.bank(banks[0] + 1, 0, Tc, 0, 1)
        px, pxx = [], []
        ones = (self.ones_col.ap(), self.ones_col.reg())
        for i, xt in enumerate(xf_tiles):
            xbf = stage.at(0, 128, 2 * i * Tc, (2 * i + 1) * Tc)
            xsq = stage.at(0, 128, (2 * i + 1) * Tc, (2 * i + 2) * Tc)
            self.copy("vector", xbf, xt, tag=f"{tag}cv{i}")
            self.tt("gpsimd", ALU.mult, xsq, xt, xt, tag=f"{tag}sq{i}")
            px.append((ones, xbf))
            pxx.append((ones, xsq))
        self.mm_group(sx, px, tag=f"{tag}SX")
        self.mm_group(sxx, pxx, tag=f"{tag}SXX")
        sm = self.smal
        negmu = sm.at(0, 1, 0, Tc)
        mu2 = sm.at(0, 1, Tc, 2 * Tc)
        ex2 = sm.at(0, 1, 2 * Tc, 3 * Tc)
        var = sm.at(0, 1, 3 * Tc, 4 * Tc)
        self.ts(negmu, sx, -1.0 / n, ALU.mult, tag=f"{tag}mu")
        self.tt("vector", ALU.mult, mu2, negmu, negmu, tag=f"{tag}mu2")
        self.ts(ex2, sxx, 1.0 / n, ALU.mult, tag=f"{tag}ex2")
        self.tt("vector", ALU.subtract, var, ex2, mu2, tag=f"{tag}var")
        nb = self.smal_bf
        negmu_b0 = nb.at(0, 1, 0, Tc)
        var_b0 = nb.at(0, 1, Tc, 2 * Tc)
        self.copy("vector", negmu_b0, negmu, tag=f"{tag}mub")
        self.copy("vector", var_b0, var, tag=f"{tag}vrb")
        negmu_b = nb.at(0, 1, 2 * Tc, 3 * Tc)
        var_b = nb.at(0, 1, 3 * Tc, 4 * Tc)
        self.copy("gpsimd", negmu_b, negmu_b0, tag=f"{tag}mubr")
        self.copy("gpsimd", var_b, var_b0, tag=f"{tag}vrbr")
        mub = self.ps.bank(banks[1], 0, Tc)
        vbp = self.ps.bank(banks[1] + 1, 0, Tc)
        lhs1 = (self.ones_row.ap(0, 1, 0, 128), self.ones_row.reg())
        self.mm_group(mub, [(lhs1, negmu_b)], tag=f"{tag}BCmu")
        self.mm_group(vbp, [(lhs1, var_b)], tag=f"{tag}BCvar")
        sdw = self.c_sd.at(0, 128, 0, Tc)
        rsb = self.c_rsb.at(0, 128, 0, Tc)
        self.act(AF.Sqrt, sdw, vbp, bias=LN_EPS, tag=f"{tag}sd", flush=True)
        self.recip(rsb, sdw, tag=f"{tag}rs")
        if DBG.get("ln_tap") == tag:
            lt = DR(self.nc, "tap_ln", [8, Tc], F32, kind="ExternalOutput")
            st_t = DR(self.nc, "tap_stage", [128, 2 * len(xf_tiles) * Tc], BF16,
                      kind="ExternalOutput")
            row = self._lntap_row
            self.copy("vector", row.at(0, 1, 0, Tc), sx, tag="lt1")
            self.store("gpsimd", lt.slc((0, 1), (0, Tc)), row.at(0, 1, 0, Tc), tag="lt1s")
            self.copy("vector", row.at(0, 1, 0, Tc), sxx, tag="lt2")
            self.store("gpsimd", lt.slc((1, 2), (0, Tc)), row.at(0, 1, 0, Tc), tag="lt2s")
            self.store("gpsimd", lt.slc((2, 3), (0, Tc)), negmu, tag="lt3s")
            self.store("gpsimd", lt.slc((3, 4), (0, Tc)),
                       (rsb[0][0:1, 0:Tc], rsb[1]), tag="lt4s")
            self.store("gpsimd", st_t.slc((0, 128), (0, 2 * len(xf_tiles) * Tc)),
                       stage.at(0, 128, 0, 2 * len(xf_tiles) * Tc), tag="lt5s")
        return mub, rsb



def _f32(a):
    return np.ascontiguousarray(a, dtype=np.float32)


def host_prep_l0a(inp):
    out = {}
    out["l0a_inwT"] = _f32(np.asarray(inp["L0_attn_in_w"]).T)
    cw = np.asarray(inp["L0_attn_conv_w"])[:, 0, :]
    out["l0a_convw"] = _f32(cw.reshape(16, 128, 4).transpose(1, 0, 2).reshape(128, 64))
    out["l0a_convb"] = _f32(np.asarray(inp["L0_attn_conv_b"]).reshape(16, 128).T)
    xpT = np.asarray(inp["L0_attn_xproj_w"]).T
    pad = np.zeros((2048, 128), dtype=np.float32)
    pad[:, 0:16] = xpT[:, 64:80]
    pad[:, 32:48] = xpT[:, 80:96]
    pad[:, 64:128] = xpT[:, 0:64]
    out["l0a_xprojT"] = _f32(pad)
    out["l0a_dtwT"] = _f32(np.asarray(inp["L0_attn_dt_w"]).T)
    out["l0a_dtb"] = _f32(np.asarray(inp["L0_attn_dt_b"]).reshape(16, 128).T)
    al = np.asarray(inp["L0_attn_A_log"])
    out["l0a_Alog"] = _f32(al.reshape(16, 128, 16).transpose(1, 0, 2).reshape(128, 256))
    Dv = np.asarray(inp["L0_attn_D"], dtype=np.float32)
    dd = np.zeros((16, 128, 128), dtype=np.float32)
    i = np.arange(128)
    for dc in range(16):
        dd[dc, i, i] = Dv[dc * 128 + i]
    out["l0a_diagD"] = _f32(dd.transpose(1, 0, 2).reshape(128, 16 * 128))
    out["l0a_outwT"] = _f32(np.asarray(inp["L0_attn_out_w"]).T)
    out["l0a_lng"] = _f32(np.asarray(inp["L0_attn_norm_g"]).reshape(8, 128).T)
    out["l0a_lnb"] = _f32(np.asarray(inp["L0_attn_norm_b"]).reshape(8, 128).T)
    return out



def emit_l0a(kb, x_in, x_out, W):
    nc, st, em = kb.nc, kb.stack, kb.em
    ps = kb.ps
    DI = 2048
    NDC = 16
    NST = 16

    xs_sp = DR(nc, "l0a_xs", [B, D, T], F32)
    z_sp = DR(nc, "l0a_z", [B, DI, T], BF16)
    u32_sp = DR(nc, "l0a_u32", [B, DI, T], F32)
    ubf_sp = DR(nc, "l0a_ubf", [B, DI, T], BF16)
    dl_sp = DR(nc, "l0a_dl", [B, DI, T], F32)
    B_sp = DR(nc, "l0a_B", [B, NST, T], F32)
    C_sp = DR(nc, "l0a_C", [B, NST, T], F32)

    convw = SB(nc, st, "s0a_cw", 64, F32)
    convb = SB(nc, st, "s0a_cb", 16, F32)
    dtb = SB(nc, st, "sb0_dtb", 16, F32)
    lng = SB(nc, st, "s0a_g", 8, F32)
    lnb = SB(nc, st, "s0a_b", 8, F32)
    Alog = SB(nc, st, "s0a_Al", 256, F32)
    Acoef = SB(nc, st, "s0a_A", 256, F32)
    diagD_bf = SB(nc, st, "s0a_dD", NDC * 128, BF16)
    kb.load("sync", convw.at(), W["l0a_convw"].slc((0, 128), (0, 64)), tag="cw")
    kb.load("sync", convb.at(), W["l0a_convb"].slc((0, 128), (0, 16)), tag="cb")
    kb.load("sync", dtb.at(), W["l0a_dtb"].slc((0, 128), (0, 16)), tag="dtb")
    kb.load("sync", lng.at(), W["l0a_lng"].slc((0, 128), (0, 8)), tag="lng")
    kb.load("sync", lnb.at(), W["l0a_lnb"].slc((0, 128), (0, 8)), tag="lnb")
    kb.load("sync", Alog.at(), W["l0a_Alog"].slc((0, 128), (0, 256)), tag="alog")
    kb.act(AF.Exp, Acoef.at(), Alog.at(), tag="aexp")
    kb.ts(Acoef.at(), Acoef.at(), -1.0, ALU.mult, tag="aneg")

    TA = 512
    NCH = T // TA
    with contextlib.ExitStack() as sW:
        inw_bf = SB(nc, sW, "s0a_inw", DK * 4096, BF16)
        xproj_bf = SB(nc, sW, "s0a_xp", NDC * 128, BF16)
        dtw_bf = SB(nc, sW, "s0a_dtw", 2048, BF16)
        with contextlib.ExitStack() as s2:
            wst = SB(nc, s2, "l0a_wst", 4096, F32)
            for dk in range(DK):
                kb.load("sync", wst.at(),
                        W["l0a_inwT"].slc((dk * 128, (dk + 1) * 128), (0, 4096)),
                        tag=f"inw{dk}")
                kb.copy("vector", inw_bf.at(0, 128, dk * 4096, (dk + 1) * 4096),
                        wst.at(), tag=f"inwc{dk}")
            for dc in range(NDC):
                kb.load("sync", wst.at(0, 128, 0, 128),
                        W["l0a_xprojT"].slc((dc * 128, (dc + 1) * 128), (0, 128)),
                        tag=f"xp{dc}")
                kb.copy("vector", xproj_bf.at(0, 128, dc * 128, (dc + 1) * 128),
                        wst.at(0, 128, 0, 128), tag=f"xpc{dc}")
            kb.load("sync", wst.at(64, 128, 0, 2048),
                    W["l0a_dtwT"].slc((0, 64), (0, 2048)), tag="dtw")
            kb.copy("vector", dtw_bf.at(64, 128, 0, 2048), wst.at(64, 128, 0, 2048),
                    tag="dtwc")
            kb.load("sync", wst.at(0, 128, 0, NDC * 128),
                    W["l0a_diagD"].slc((0, 128), (0, NDC * 128)), tag="dD")
            kb.copy("vector", diagD_bf.at(), wst.at(0, 128, 0, NDC * 128), tag="dDc")

        with contextlib.ExitStack() as sA:
            x_sb = SB(nc, sA, "pa_x", DK * TA, F32)
            stage = SB(nc, sA, "pa_stage", 2 * DK * TA, BF16)
            xn_bf = SB(nc, sA, "pa_xn", DK * TA, BF16)
            xcv = SB(nc, sA, "pa_xcv", NDC * 520, F32)
            tail = SB(nc, sA, "pa_tail", NDC * 4, F32)
            ubf_sb = SB(nc, sA, "pa_ubf", NDC * TA, BF16)
            u32st = SB(nc, sA, "pa_u32", TA, F32)
            zst = SB(nc, sA, "pa_z", TA, BF16)
            ts1 = SB(nc, sA, "pa_t1", TA, F32)
            ts2 = SB(nc, sA, "pa_t2", TA, F32)
            draw_bf = SB(nc, sA, "pa_draw", TA, BF16)
            bc_st = SB(nc, sA, "pa_bc", 2 * TA, F32, parts=48)
            dsoft = SB(nc, sA, "pa_ds", 2 * TA, F32)
            dst = SB(nc, sA, "pa_dst", TA, F32)

            for b in range(B):
                for dc in range(NDC):
                    kb.memset("gpsimd", xcv.at(0, 128, dc * 520, dc * 520 + 3), 0.0,
                              tag=f"h0_{b}_{dc}")
                for c in range(NCH):
                    t0 = c * TA
                    for dk in range(DK):
                        kb.load("sync", x_sb.at(0, 128, dk * TA, (dk + 1) * TA),
                                x_in.slc(b, (dk * 128, (dk + 1) * 128), (t0, t0 + TA)),
                                tag=f"Ax{b}_{c}_{dk}")
                    xt = [x_sb.at(0, 128, dk * TA, (dk + 1) * TA) for dk in range(DK)]
                    mub, rsb = kb.ln_stats(xt, TA, (0, 2), stage, tag=f"Aln{b}{c}")
                    for dk in range(DK):
                        xsd = x_sb.at(0, 128, dk * TA, (dk + 1) * TA)
                        kb.tt("vector", ALU.add, ts1.at(), xt[dk], mub, tag=f"Aa{b}{c}{dk}")
                        kb.tt("vector", ALU.mult, xsd, ts1.at(), rsb, tag=f"Am{b}{c}{dk}")
                        kb.store("gpsimd",
                                 xs_sp.slc(b, (dk * 128, (dk + 1) * 128), (t0, t0 + TA)),
                                 xsd, tag=f"Axs{b}{c}{dk}")
                    xst = [x_sb.at(0, 128, dk * TA, (dk + 1) * TA) for dk in range(DK)]
                    mub2, rsb2 = kb.ln_stats(xst, TA, (0, 2), stage, tag=f"Al2{b}{c}")
                    for dk in range(DK):
                        kb.tt("vector", ALU.add, ts1.at(), xst[dk], mub2, tag=f"A2a{b}{c}{dk}")
                        kb.tt("vector", ALU.mult, ts2.at(), ts1.at(), rsb2, tag=f"A2m{b}{c}{dk}")
                        kb.ts(xn_bf.at(0, 128, dk * TA, (dk + 1) * TA), ts2.at(),
                              lng.ap(0, 128, dk, dk + 1), ALU.mult,
                              lnb.ap(0, 128, dk, dk + 1), ALU.add,
                              extra=[lng.reg(), lnb.reg()], tag=f"A2g{b}{c}{dk}")
                    for oc in range(32):
                        pt = ps.bank(4 + (oc % 4), 0, TA)
                        pairs = [((inw_bf.ap(0, 128, dk * 4096 + oc * 128,
                                             dk * 4096 + (oc + 1) * 128),
                                   inw_bf.reg(0, 128, dk * 4096 + oc * 128,
                                              dk * 4096 + (oc + 1) * 128)),
                                  xn_bf.at(0, 128, dk * TA, (dk + 1) * TA))
                                 for dk in range(DK)]
                        kb.mm_group(pt, pairs, tag=f"Aip{b}{c}{oc}")
                        if oc < 16:
                            dc = oc
                            kb.copy("vector", xcv.at(0, 128, dc * 520 + 3, dc * 520 + 515),
                                    pt, tag=f"Axc{b}{c}{oc}")
                        else:
                            zi = oc - 16
                            kb.copy("vector", zst.at(0, 128, 0, TA),
                                    pt, tag=f"Azc{b}{c}{oc}")
                            kb.store("gpsimd",
                                     z_sp.slc(b, (zi * 128, (zi + 1) * 128), (t0, t0 + TA)),
                                     zst.at(0, 128, 0, TA),
                                     tag=f"Azs{b}{c}{oc}")
                    for dc in range(NDC):
                        base = dc * 520
                        xk = [xcv.at(0, 128, base + k, base + k + 512) for k in range(4)]
                        wcol = [convw.ap(0, 128, dc * 4 + k, dc * 4 + k + 1) for k in range(4)]
                        wr = [convw.reg()]
                        kb.ts(ts1.at(), xk[0], wcol[0], ALU.mult, extra=wr, tag=f"Ac0{b}{c}{dc}")
                        kb.stt(ts2.at(), xk[1], wcol[1], ts1.at(), ALU.mult, ALU.add,
                               extra=wr, tag=f"Ac1{b}{c}{dc}")
                        kb.stt(ts1.at(), xk[2], wcol[2], ts2.at(), ALU.mult, ALU.add,
                               extra=wr, tag=f"Ac2{b}{c}{dc}")
                        kb.stt(ts2.at(), xk[3], wcol[3], ts1.at(), ALU.mult, ALU.add,
                               extra=wr, tag=f"Ac3{b}{c}{dc}")
                        kb.copy("gpsimd", tail.at(0, 128, dc * 4, dc * 4 + 3),
                                xcv.at(0, 128, base + 512, base + 515), tag=f"Atl{b}{c}{dc}")
                        kb.copy("gpsimd", xcv.at(0, 128, base, base + 3),
                                tail.at(0, 128, dc * 4, dc * 4 + 3), tag=f"Ath{b}{c}{dc}")
                        u32 = u32st.at(0, 128, 0, TA)
                        kb.act(AF.Silu, u32, ts2.at(), bias=convb.ap(0, 128, dc, dc + 1),
                               extra=[convb.reg()], tag=f"Asl{b}{c}{dc}")
                        kb.copy("vector", ubf_sb.at(0, 128, dc * TA, (dc + 1) * TA), u32,
                                tag=f"Aub{b}{c}{dc}")
                        kb.store("gpsimd",
                                 u32_sp.slc(b, (dc * 128, (dc + 1) * 128), (t0, t0 + TA)),
                                 u32, tag=f"Au3{b}{c}{dc}")
                        kb.store("gpsimd",
                                 ubf_sp.slc(b, (dc * 128, (dc + 1) * 128), (t0, t0 + TA)),
                                 ubf_sb.at(0, 128, dc * TA, (dc + 1) * TA),
                                 tag=f"Aus{b}{c}{dc}")
                    pt = ps.bank(0, 0, TA)
                    pairs = [((xproj_bf.ap(0, 128, dc * 128, (dc + 1) * 128),
                               xproj_bf.reg(0, 128, dc * 128, (dc + 1) * 128)),
                              ubf_sb.at(0, 128, dc * TA, (dc + 1) * TA))
                             for dc in range(NDC)]
                    kb.mm_group(pt, pairs, tag=f"Axp{b}{c}")
                    kb.copy("vector", bc_st.at(0, 16, 0, TA),
                            (ps.h[0:16, 0:TA], ps.preg(0)), tag=f"AB{b}{c}")
                    kb.copy("vector", bc_st.at(32, 48, TA, 2 * TA),
                            (ps.h[32:48, 0:TA], ps.preg(0)), tag=f"AC{b}{c}")
                    kb.copy("vector", draw_bf.at(64, 128, 0, TA),
                            (ps.h[64:128, 0:TA], ps.preg(0)), tag=f"AR{b}{c}")
                    kb.store("gpsimd", B_sp.slc(b, (0, 16), (t0, t0 + TA)),
                             bc_st.at(0, 16, 0, TA), tag=f"ABs{b}{c}")
                    kb.store("gpsimd", C_sp.slc(b, (0, 16), (t0, t0 + TA)),
                             bc_st.at(32, 48, TA, 2 * TA), tag=f"ACs{b}{c}")
                    for di in range(NDC):
                        pt = ps.bank(1 + (di % 2), 0, TA)
                        kb.mm_group(pt, [((dtw_bf.ap(64, 128, di * 128, (di + 1) * 128),
                                           dtw_bf.reg(64, 128, di * 128, (di + 1) * 128)),
                                          draw_bf.at(64, 128, 0, TA))],
                                    tag=f"Adt{b}{c}{di}")
                        dbcol = dtb.ap(0, 128, di, di + 1)
                        dbr = [dtb.reg()]
                        a1 = dsoft.at(0, 128, 0, TA)
                        a2 = dsoft.at(0, 128, TA, 2 * TA)
                        kb.act(AF.Abs, a1, pt, bias=dbcol, extra=dbr, tag=f"Aab{b}{c}{di}")
                        kb.act(AF.Exp, a2, a1, scale=-1.0, tag=f"Aex{b}{c}{di}")
                        kb.act(AF.Ln, a1, a2, bias=1.0, tag=f"Al1{b}{c}{di}")
                        kb.act(AF.Relu, a2, pt, bias=dbcol, extra=dbr, tag=f"Arl{b}{c}{di}")
                        dd = dst.at(0, 128, 0, TA)
                        kb.tt("vector", ALU.add, dd, a1, a2, tag=f"Add{b}{c}{di}")
                        kb.store("gpsimd",
                                 dl_sp.slc(b, (di * 128, (di + 1) * 128), (t0, t0 + TA)),
                                 dd, tag=f"Ads{b}{c}{di}")

    outw_bf = SB(nc, st, "s0a_ow", NDC * 1024, BF16)
    with contextlib.ExitStack() as s2:
        wst = SB(nc, s2, "l0a_wst4", 1024, F32)
        for dc in range(NDC):
            kb.load("sync", wst.at(),
                    W["l0a_outwT"].slc((dc * 128, (dc + 1) * 128), (0, 1024)),
                    tag=f"ow{dc}")
            kb.copy("vector", outw_bf.at(0, 128, dc * 1024, (dc + 1) * 1024), wst.at(),
                    tag=f"owc{dc}")

    TB = 512
    NTC = T // TB
    with contextlib.ExitStack() as sB:
        bbc = SB(nc, sB, "pb_B", NST * TB, BF16)
        cbc = SB(nc, sB, "pb_C", NST * TB, BF16)
        bcst = SB(nc, sB, "pb_bst", 2 * TB, F32)
        dl_sb = SB(nc, sB, "pb_dl", 2 * TB, F32)
        u32_sb = SB(nc, sB, "pb_u32", 2 * TB, F32)
        ubf_sb2 = SB(nc, sB, "pb_ubf", 2 * TB, BF16)
        zbf_sb = SB(nc, sB, "pb_z", 2 * TB, BF16)
        du_sb = SB(nc, sB, "pb_du", 2 * TB, F32)
        sz_sb = SB(nc, sB, "pb_sz", 2 * TB, F32)
        dA_sb = SB(nc, sB, "pb_dA", 2 * TB, F32)
        dBu_sb = SB(nc, sB, "pb_dBu", 2 * TB, F32)
        h_sb = SB(nc, sB, "pb_h", 2 * TB, F32)
        yt_sb = SB(nc, sB, "pb_yt", 2 * TB, BF16)
        hcar = SB(nc, sB, "pb_hc", NDC * NST, F32)
        ygat = SB(nc, sB, "pb_yg", NDC * TB, BF16)
        o_sb = SB(nc, sB, "pc_o", DK * TB, F32)
        xr_sb = SB(nc, sB, "pc_x", DK * TB, F32)
        xsr_sb = SB(nc, sB, "pc_xs", TB, F32)
        stage2 = SB(nc, sB, "pc_stage", 2 * DK * TB, BF16)
        tc1 = SB(nc, sB, "pc_t1", TB, F32)
        tc2 = SB(nc, sB, "pc_t2", TB, F32)

        for b in range(B):
            for tc in range(NTC):
                t0 = tc * TB
                for n in range(NST):
                    np_ = n % 2
                    kb.load("scalar", bcst.at(0, 128, np_ * TB, (np_ + 1) * TB),
                            B_sp.bcast(128, b, (n, n + 1), (t0, t0 + TB)),
                            tag=f"BB{b}{tc}{n}")
                    kb.copy("vector", bbc.at(0, 128, n * TB, (n + 1) * TB),
                            bcst.at(0, 128, np_ * TB, (np_ + 1) * TB), tag=f"BBc{b}{tc}{n}")
                    kb.load("scalar", bcst.at(0, 128, np_ * TB, (np_ + 1) * TB),
                            C_sp.bcast(128, b, (n, n + 1), (t0, t0 + TB)),
                            tag=f"BC{b}{tc}{n}")
                    kb.copy("vector", cbc.at(0, 128, n * TB, (n + 1) * TB),
                            bcst.at(0, 128, np_ * TB, (np_ + 1) * TB), tag=f"BCc{b}{tc}{n}")
                for dc in range(NDC):
                    dp = dc % 2
                    dcs = (dc * 128, (dc + 1) * 128)
                    dls = dl_sb.at(0, 128, dp * TB, (dp + 1) * TB)
                    u32s = u32_sb.at(0, 128, dp * TB, (dp + 1) * TB)
                    ubfs = ubf_sb2.at(0, 128, dp * TB, (dp + 1) * TB)
                    zbfs = zbf_sb.at(0, 128, dp * TB, (dp + 1) * TB)
                    dus = du_sb.at(0, 128, dp * TB, (dp + 1) * TB)
                    szs = sz_sb.at(0, 128, dp * TB, (dp + 1) * TB)
                    kb.load("sync", dls, dl_sp.slc(b, dcs, (t0, t0 + TB)), tag=f"Bd{b}{tc}{dc}")
                    kb.load("sync", u32s, u32_sp.slc(b, dcs, (t0, t0 + TB)), tag=f"Bu{b}{tc}{dc}")
                    kb.load("sync", ubfs, ubf_sp.slc(b, dcs, (t0, t0 + TB)), tag=f"Bv{b}{tc}{dc}")
                    kb.load("sync", zbfs, z_sp.slc(b, dcs, (t0, t0 + TB)), tag=f"Bz{b}{tc}{dc}")
                    kb.tt("gpsimd", ALU.mult, dus, dls, u32s, tag=f"Bdu{b}{tc}{dc}")
                    kb.act(AF.Silu, szs, zbfs, tag=f"Bsz{b}{tc}{dc}")
                    pt = ps.bank(dc % 2, 0, TB)
                    for n in range(NST):
                        pp = n % 2
                        dA = dA_sb.at(0, 128, pp * TB, (pp + 1) * TB)
                        dBu = dBu_sb.at(0, 128, pp * TB, (pp + 1) * TB)
                        hh = h_sb.at(0, 128, pp * TB, (pp + 1) * TB)
                        yt = yt_sb.at(0, 128, pp * TB, (pp + 1) * TB)
                        kb.act(AF.Exp, dA, dls,
                               scale=Acoef.ap(0, 128, dc * 16 + n, dc * 16 + n + 1),
                               extra=[Acoef.reg()], tag=f"BdA{b}{tc}{dc}_{n}")
                        e1 = "gpsimd" if (n % 4 != 3) else "vector"
                        kb.tt(e1, ALU.mult, dBu, dus,
                              bbc.at(0, 128, n * TB, (n + 1) * TB), tag=f"BdB{b}{tc}{dc}_{n}")
                        hc = (hcar.ap(0, 128, dc * NST + n, dc * NST + n + 1),
                              hcar.reg(0, 128, dc * NST + n, dc * NST + n + 1))
                        if tc == 0:
                            kb.scan(hh, dA, dBu, 0.0, tag=f"Bsc{b}{tc}{dc}_{n}")
                        else:
                            kb.scan(hh, dA, dBu, hc[0], extra=[hc[1]],
                                    tag=f"Bsc{b}{tc}{dc}_{n}")
                        if tc < NTC - 1:
                            kb.copy("gpsimd", hc,
                                    (h_sb.ap(0, 128, (pp + 1) * TB - 1, (pp + 1) * TB),
                                     h_sb.reg(0, 128, (pp + 1) * TB - 1, (pp + 1) * TB)),
                                    tag=f"Bhc{b}{tc}{dc}_{n}")
                        e2 = "vector" if (n % 4 != 1) else "gpsimd"
                        kb.tt(e2, ALU.mult, yt, hh,
                              cbc.at(0, 128, n * TB, (n + 1) * TB), tag=f"Byt{b}{tc}{dc}_{n}")
                        kb.mm(pt, (kb.ident.ap(), kb.ident.reg()), yt,
                              start=(n == 0), stop=False, tag=f"Bya{b}{tc}{dc}_{n}")
                    kb.mm(pt, (diagD_bf.ap(0, 128, dc * 128, (dc + 1) * 128),
                               diagD_bf.reg(0, 128, dc * 128, (dc + 1) * 128)),
                          ubfs, start=False, stop=True, tag=f"BDu{b}{tc}{dc}")
                    kb.tt("vector", ALU.mult, ygat.at(0, 128, dc * TB, (dc + 1) * TB),
                          pt, szs, tag=f"Byg{b}{tc}{dc}")

                for dk in range(DK):
                    kb.load("sync", xr_sb.at(0, 128, dk * TB, (dk + 1) * TB),
                            x_in.slc(b, (dk * 128, (dk + 1) * 128), (t0, t0 + TB)),
                            tag=f"Cx{b}{tc}{dk}")
                for oc in range(DK):
                    pt = ps.bank(2 + oc % 2, 0, TB)
                    pairs = [((outw_bf.ap(0, 128, dc * 1024 + oc * 128,
                                          dc * 1024 + (oc + 1) * 128),
                               outw_bf.reg(0, 128, dc * 1024 + oc * 128,
                                           dc * 1024 + (oc + 1) * 128)),
                              ygat.at(0, 128, dc * TB, (dc + 1) * TB))
                             for dc in range(NDC)]
                    kb.mm_group(pt, pairs, tag=f"Cop{b}{tc}{oc}")
                    kb.load("scalar", xsr_sb.at(),
                            xs_sp.slc(b, (oc * 128, (oc + 1) * 128), (t0, t0 + TB)),
                            tag=f"Cxs{b}{tc}{oc}")
                    kb.tt("vector", ALU.add, o_sb.at(0, 128, oc * TB, (oc + 1) * TB),
                          pt, xsr_sb.at(), tag=f"Co{b}{tc}{oc}")
                ot = [o_sb.at(0, 128, dk * TB, (dk + 1) * TB) for dk in range(DK)]
                mub, rsb = kb.ln_stats(ot, TB, (4, 6), stage2, tag=f"Cln{b}{tc}")
                for dk in range(DK):
                    kb.tt("vector", ALU.add, tc1.at(), ot[dk], mub, tag=f"Ca{b}{tc}{dk}")
                    kb.tt("vector", ALU.mult, tc2.at(), tc1.at(), rsb, tag=f"Cm{b}{tc}{dk}")
                    kb.tt("vector", ALU.add, xr_sb.at(0, 128, dk * TB, (dk + 1) * TB),
                          xr_sb.at(0, 128, dk * TB, (dk + 1) * TB), tc2.at(),
                          tag=f"Cr{b}{tc}{dk}")
                    kb.store("gpsimd",
                             x_out.slc(b, (dk * 128, (dk + 1) * 128), (t0, t0 + TB)),
                             xr_sb.at(0, 128, dk * TB, (dk + 1) * TB), tag=f"Cst{b}{tc}{dk}")




def host_prep_l0f(inp):
    out = {}
    out["l0f_wqkv"] = _f32(inp["L0_ffn_wqkv"])
    out["l0f_wo"] = _f32(inp["L0_ffn_wo"])
    qs = np.asarray(inp["L0_ffn_q_scale"])
    ks = np.asarray(inp["L0_ffn_k_scale"])
    p = np.arange(128)
    out["l0f_qs"] = _f32(qs[:, p % 64].T)
    out["l0f_ks"] = _f32(ks[:, p % 64].T)
    lg = np.asarray(inp["L0_ffn_ln_g"])
    lb = np.asarray(inp["L0_ffn_ln_b"])
    out["l0f_lng"] = _f32(np.stack([lg[l].reshape(8, 128).T for l in range(2)], -1)
                          .reshape(128, 16))
    out["l0f_lnb"] = _f32(np.stack([lb[l].reshape(8, 128).T for l in range(2)], -1)
                          .reshape(128, 16))
    HID = 2730
    HP = 2816
    w1 = np.asarray(inp["L0_ffn_w1"])
    b1 = np.asarray(inp["L0_ffn_b1"])
    w1p = np.zeros((2, 1024, 2 * HP), dtype=np.float32)
    w1p[:, :, 0:HID] = w1[:, :, 0:HID]
    w1p[:, :, HP:HP + HID] = w1[:, :, HID:2 * HID]
    out["l0f_w1p"] = _f32(w1p)
    b1p = np.zeros((2, 2 * HP), dtype=np.float32)
    b1p[:, 0:HID] = b1[:, 0:HID]
    b1p[:, HP:HP + HID] = b1[:, HID:2 * HID]
    nb = np.zeros((128, 88), dtype=np.float32)
    for l in range(2):
        for oc in range(44):
            nb[:, l * 44 + oc] = b1p[l, oc * 128:(oc + 1) * 128]
    out["l0f_b1p"] = _f32(nb)
    w2 = np.asarray(inp["L0_ffn_w2"])
    w2p = np.zeros((2, HP, 1024), dtype=np.float32)
    w2p[:, 0:HID, :] = w2
    out["l0f_w2p"] = _f32(w2p)
    b2 = np.asarray(inp["L0_ffn_b2"])
    out["l0f_b2"] = _f32(np.stack([b2[l].reshape(8, 128).T for l in range(2)], -1)
                         .reshape(128, 16))
    out["cst_e2"] = _f32((np.arange(128)[:, None] // 64 == np.arange(2)[None, :]))
    out["cst_e2b"] = _f32((np.arange(2)[:, None] == np.arange(128)[None, :] // 64))
    R = np.zeros((128, 128), dtype=np.float32)
    for blk in range(2):
        o = blk * 64
        for dd in range(32):
            R[o + dd, o + dd + 32] = -1.0
            R[o + dd + 32, o + dd] = 1.0
    out["cst_rot"] = _f32(R.T)
    DIM_HEAD = 64
    WINDOW = 512
    half = np.arange(0, DIM_HEAD, 2, dtype=np.float64)
    inv_freq = 1.0 / 10000.0 ** (half / DIM_HEAD)
    t = np.arange(T, dtype=np.float64)
    freqs = t[:, None] * inv_freq[None, :]
    freqs = np.concatenate([freqs, freqs], -1)
    base = (half + 0.4 * DIM_HEAD) / (1.4 * DIM_HEAD)
    power = (t - T // 2) / float(WINDOW // 2)
    scale = base[None, :] ** power[:, None]
    scale = np.concatenate([scale, scale], -1)
    cos = np.cos(freqs)
    sin = np.sin(freqs)
    p64 = np.arange(128) % 64
    out["l0f_qcs"] = _f32((cos * scale * 8.0).T[p64])
    out["l0f_qss"] = _f32((sin * scale * 8.0).T[p64])
    out["l0f_kcs"] = _f32((cos / scale).T[p64])
    out["l0f_kss"] = _f32((sin / scale).T[p64])
    W_ = 512
    il = np.arange(W_)
    jl = np.arange(2 * W_)
    masks = np.zeros((2, W_, 2 * W_), dtype=np.float32)
    for wv in range(2):
        w = 1 if wv else 0
        ig = w * W_ + il
        jg = (w - 1) * W_ + jl
        valid = np.ones((W_, 2 * W_), dtype=bool)
        jgb = np.broadcast_to(jg[None, :], (W_, 2 * W_))
        igb = np.broadcast_to(ig[:, None], (W_, 2 * W_))
        bad = (igb < jgb) | (igb > jgb + W_) | (jgb < 0)
        masks[wv][bad] = -1e30
    out["l0f_mask"] = _f32(masks)
    return out


DBG = {}


def emit_l0f(kb, x_in, x_out, W):
    nc, st, em = kb.nc, kb.stack, kb.em
    ps = kb.ps
    NH = 8
    NW = 4
    WS = 512
    HPK = 22
    TC = 512
    NCH = T // TC
    n_layers = DBG.get("l0f_layers", 2)
    skip_attn = DBG.get("l0f_skip_attn", False)
    skip_ffn = DBG.get("l0f_skip_ffn", False)

    ffs = DR(nc, "l0f_ffs", [B, D, T], F32,
             kind="ExternalOutput" if DBG.get("l0f_tap") else "Internal")

    e2 = SB(nc, st, "f_e2", 2, BF16)
    e2b = SB(nc, st, "f_e2b", 128, BF16, parts=2)
    rotm = SB(nc, st, "f_rot", 128, BF16)
    qscol = SB(nc, st, "f_qs", 2, F32)
    kscol = SB(nc, st, "f_ks", 2, F32)
    lngc = SB(nc, st, "f_lng", 16, F32)
    lnbc = SB(nc, st, "f_lnb", 16, F32)
    b1c = SB(nc, st, "f_b1", 88, F32)
    b2c = SB(nc, st, "f_b2", 16, F32)
    with contextlib.ExitStack() as s2:
        wst = SB(nc, s2, "f_wst0", 128, F32)
        kb.load("sync", wst.at(0, 128, 0, 2), W["cst_e2"].slc((0, 128), (0, 2)), tag="e2")
        kb.copy("vector", e2.at(), wst.at(0, 128, 0, 2), tag="e2c")
        kb.load("sync", wst.at(0, 2, 0, 128), W["cst_e2b"].slc((0, 2), (0, 128)), tag="e2b")
        kb.copy("vector", e2b.at(0, 2, 0, 128), wst.at(0, 2, 0, 128), tag="e2bc")
        kb.load("sync", wst.at(), W["cst_rot"].slc((0, 128), (0, 128)), tag="rot")
        kb.copy("vector", rotm.at(), wst.at(), tag="rotc")
    kb.load("sync", qscol.at(), W["l0f_qs"].slc((0, 128), (0, 2)), tag="qs")
    kb.load("sync", kscol.at(), W["l0f_ks"].slc((0, 128), (0, 2)), tag="ks")
    kb.load("sync", lngc.at(), W["l0f_lng"].slc((0, 128), (0, 16)), tag="flng")
    kb.load("sync", lnbc.at(), W["l0f_lnb"].slc((0, 128), (0, 16)), tag="flnb")
    kb.load("sync", b1c.at(), W["l0f_b1p"].slc((0, 128), (0, 88)), tag="fb1")
    kb.load("sync", b2c.at(), W["l0f_b2"].slc((0, 128), (0, 16)), tag="fb2")

    with contextlib.ExitStack() as s0:
        x_sb = SB(nc, s0, "f0_x", DK * TC, F32)
        stage = SB(nc, s0, "f0_st", 2 * DK * TC, BF16)
        tt1 = SB(nc, s0, "f0_t1", TC, F32)
        for b in range(B):
            for c in range(NCH):
                t0 = c * TC
                for dk in range(DK):
                    kb.load("sync", x_sb.at(0, 128, dk * TC, (dk + 1) * TC),
                            x_in.slc(b, (dk * 128, (dk + 1) * 128), (t0, t0 + TC)),
                            tag=f"F0x{b}{c}{dk}")
                xt = [x_sb.at(0, 128, dk * TC, (dk + 1) * TC) for dk in range(DK)]
                mub, rsb = kb.ln_stats(xt, TC, (0, 2), stage, tag=f"F0l{b}{c}")
                for dk in range(DK):
                    kb.tt("vector", ALU.add, tt1.at(), xt[dk], mub, tag=f"F0a{b}{c}{dk}")
                    kb.tt("vector", ALU.mult, x_sb.at(0, 128, dk * TC, (dk + 1) * TC),
                          tt1.at(), rsb, tag=f"F0m{b}{c}{dk}")
                    kb.store("gpsimd", ffs.slc(b, (dk * 128, (dk + 1) * 128), (t0, t0 + TC)),
                             x_sb.at(0, 128, dk * TC, (dk + 1) * TC), tag=f"F0s{b}{c}{dk}")

    def pe_transpose(out_tv, in_tv, tag):
        (oap, oreg) = out_tv
        (iap, ireg) = in_tv

        def fn(e):
            return e.transpose(oap, iap, kb.ident.ap())
        em.op("tensor", fn, reads=[ireg, kb.ident.reg()], writes=[oreg], tag=tag)

    for l in range(n_layers):
        for b in range(B):
            if not skip_attn:
                with contextlib.ExitStack() as sq:
                    q_f = SB(nc, sq, "fq_q", 4 * T, BF16)
                    k_f = SB(nc, sq, "fq_k", 4 * (T + WS), BF16)
                    vt = SB(nc, sq, "fa_vt", 20 * 512, BF16)
                    for rc in range(4):
                        kb.memset("gpsimd", k_f.at(0, 128, rc * (T + WS), rc * (T + WS) + WS),
                                  0.0, tag=f"kp{l}{b}{rc}")
                    kb.memset("gpsimd", vt.at(0, 128, 0, 4 * 512), 0.0, tag=f"vtp{l}{b}")
                    with contextlib.ExitStack() as sv2:
                        v_f = SB(nc, sv2, "fq_v", 4 * (T + WS), BF16)
                        xsh = SB(nc, sv2, "fq_xsh", DK * 513, F32)
                        xshb = SB(nc, sv2, "fq_xshb", DK * 513, BF16)
                        wqst = SB(nc, sv2, "fq_wst", DK * 256, F32)
                        wq_bf = SB(nc, sv2, "fq_wb", DK * 1536, BF16)
                        qcs_t = SB(nc, sv2, "fq_qcs", T, F32)
                        qss_t = SB(nc, sv2, "fq_qss", T, F32)
                        kcs_t = SB(nc, sv2, "fq_kcs", T, F32)
                        kss_t = SB(nc, sv2, "fq_kss", T, F32)
                        sqs = SB(nc, sv2, "fq_sq", TC, BF16)
                        qraw = SB(nc, sv2, "fq_qraw", TC, F32)
                        nrm = SB(nc, sv2, "fq_nrm", 3 * TC, F32, parts=2)
                        nrmb = SB(nc, sv2, "fq_nrmb", TC, BF16, parts=2)
                        rbc = SB(nc, sv2, "fq_rbc", TC, F32)
                        qn32 = SB(nc, sv2, "fq_qn", TC, F32)
                        qnb = SB(nc, sv2, "fq_qnb", TC, BF16)
                        s1 = SB(nc, sv2, "fq_s1", TC, F32)
                        s2b = SB(nc, sv2, "fq_s2", TC, F32)
                        kb.memset("gpsimd", v_f.at(0, 128, 0, 4 * (T + WS)), 0.0,
                                  tag=f"vp{l}{b}")
                        kb.load("sync", qcs_t.at(), W["l0f_qcs"].slc((0, 128), (0, T)),
                                tag=f"tq{l}{b}")
                        kb.load("sync", qss_t.at(), W["l0f_qss"].slc((0, 128), (0, T)),
                                tag=f"tq2{l}{b}")
                        kb.load("sync", kcs_t.at(), W["l0f_kcs"].slc((0, 128), (0, T)),
                                tag=f"tk{l}{b}")
                        kb.load("sync", kss_t.at(), W["l0f_kss"].slc((0, 128), (0, T)),
                                tag=f"tk2{l}{b}")
                        for sl in range(6):
                            kb.load("sync", wqst.at(),
                                    W["l0f_wqkv"].wslice(l, D, sl * 256, (sl + 1) * 256),
                                    tag=f"Qw{l}{b}{sl}")
                            for dk in range(DK):
                                kb.copy("vector",
                                        wq_bf.at(0, 128, dk * 1536 + sl * 256,
                                                 dk * 1536 + (sl + 1) * 256),
                                        wqst.at(0, 128, dk * 256, (dk + 1) * 256),
                                        tag=f"Qwc{l}{b}{sl}{dk}")
                        for c in range(NCH):
                            t0 = c * TC
                            for dk in range(DK):
                                if t0 == 0:
                                    kb.memset("gpsimd", xsh.at(0, 128, dk * 513, dk * 513 + 1),
                                              0.0, tag=f"Qh{l}{b}{c}{dk}")
                                    kb.load("sync", xsh.at(0, 128, dk * 513 + 1, dk * 513 + 513),
                                            ffs.slc(b, (dk * 128, (dk + 1) * 128), (0, TC)),
                                            tag=f"Qx{l}{b}{c}{dk}")
                                else:
                                    kb.load("sync", xsh.at(0, 128, dk * 513, dk * 513 + 513),
                                            ffs.slc(b, (dk * 128, (dk + 1) * 128),
                                                    (t0 - 1, t0 + TC)),
                                            tag=f"Qx{l}{b}{c}{dk}")
                                kb.copy("vector", xshb.at(0, 128, dk * 513, (dk + 1) * 513),
                                        xsh.at(0, 128, dk * 513, (dk + 1) * 513),
                                        tag=f"Qc{l}{b}{c}{dk}")
                            for oc in range(12):
                                pt = ps.bank(4 + oc % 3, 0, TC)
                                pairs = []
                                for dk in range(DK):
                                    off = 1 if dk < 4 else 0
                                    pairs.append(
                                        ((wq_bf.ap(0, 128, dk * 1536 + oc * 128,
                                                   dk * 1536 + (oc + 1) * 128),
                                          wq_bf.reg(0, 128, dk * 1536 + oc * 128,
                                                    dk * 1536 + (oc + 1) * 128)),
                                         xshb.at(0, 128, dk * 513 + off, dk * 513 + off + TC)))
                                kb.mm_group(pt, pairs, tag=f"Qmm{l}{b}{c}{oc}")
                                if oc >= 8:
                                    rc = oc - 8
                                    kb.copy("vector",
                                            v_f.at(0, 128, rc * (T + WS) + WS + t0,
                                                   rc * (T + WS) + WS + t0 + TC),
                                            pt, tag=f"Qv{l}{b}{c}{oc}")
                                    continue
                                isq = oc < 4
                                rc = oc % 4
                                kb.copy("vector", qraw.at(), pt, tag=f"Qqr{l}{b}{c}{oc}")
                                kb.tt("gpsimd", ALU.mult, sqs.at(), qraw.at(), qraw.at(),
                                      tag=f"Qsq{l}{b}{c}{oc}")
                                ssp = ps.bank(7, 0, TC, 0, 2)
                                kb.mm_group(ssp, [((e2.ap(), e2.reg()), sqs.at())],
                                            tag=f"Qss{l}{b}{c}{oc}")
                                kb.act(AF.Sqrt, nrm.at(0, 2, 0, TC), ssp, tag=f"Qn1{l}{b}{c}{oc}")
                                kb.ts(nrm.at(0, 2, TC, 2 * TC), nrm.at(0, 2, 0, TC),
                                      1e-12, ALU.max, tag=f"Qn2{l}{b}{c}{oc}")
                                kb.recip(nrm.at(0, 2, 2 * TC, 3 * TC),
                                         nrm.at(0, 2, TC, 2 * TC), tag=f"Qn3{l}{b}{c}{oc}")
                                kb.copy("vector", nrmb.at(0, 2, 0, TC),
                                        nrm.at(0, 2, 2 * TC, 3 * TC), tag=f"Qn4{l}{b}{c}{oc}")
                                rbp = ps.bank(7, 0, TC)
                                kb.mm_group(rbp, [((e2b.ap(0, 2, 0, 128), e2b.reg(0, 2, 0, 128)),
                                                   nrmb.at(0, 2, 0, TC))],
                                            tag=f"Qrb{l}{b}{c}{oc}")
                                kb.copy("vector", rbc.at(), rbp, tag=f"Qrc{l}{b}{c}{oc}")
                                scol = qscol if isq else kscol
                                kb.stt(qn32.at(), pt, scol.ap(0, 128, l, l + 1), rbc.at(),
                                       ALU.mult, ALU.mult, extra=[scol.reg()],
                                       tag=f"Qsc{l}{b}{c}{oc}")
                                kb.copy("vector", qnb.at(), qn32.at(), tag=f"Qnb{l}{b}{c}{oc}")
                                rp = ps.bank(7, 0, TC)
                                kb.mm_group(rp, [((rotm.ap(), rotm.reg()), qnb.at())],
                                            tag=f"Qrt{l}{b}{c}{oc}")
                                cst = qcs_t if isq else kcs_t
                                snt = qss_t if isq else kss_t
                                kb.tt("vector", ALU.mult, s1.at(), qn32.at(),
                                      cst.at(0, 128, t0, t0 + TC), tag=f"Q1{l}{b}{c}{oc}")
                                kb.tt("vector", ALU.mult, s2b.at(), rp,
                                      snt.at(0, 128, t0, t0 + TC), tag=f"Q2{l}{b}{c}{oc}")
                                if isq:
                                    dstv = q_f.at(0, 128, rc * T + t0, rc * T + t0 + TC)
                                else:
                                    dstv = k_f.at(0, 128, rc * (T + WS) + WS + t0,
                                                  rc * (T + WS) + WS + t0 + TC)
                                kb.tt("vector", ALU.add, dstv, s1.at(), s2b.at(),
                                      tag=f"Q3{l}{b}{c}{oc}")
                        for rc in range(4):
                            for jc in range(4, 20):
                                tpb = (kb.ps.h[0:128, 7 * 512:7 * 512 + 64].bitcast(BF16),
                                       kb.ps.preg(7))
                                pe_transpose(tpb,
                                             (v_f.ap(0, 128, rc * (T + WS) + jc * 128,
                                                     rc * (T + WS) + (jc + 1) * 128),
                                              v_f.reg(0, 128, rc * (T + WS) + jc * 128,
                                                      rc * (T + WS) + (jc + 1) * 128)),
                                             tag=f"vt{l}{b}{rc}{jc}")
                                kb.copy("vector",
                                        vt.at(0, 128, jc * 512 + rc * 128,
                                              jc * 512 + (rc + 1) * 128),
                                        tpb, tag=f"vte{l}{b}{rc}{jc}")
                    if DBG.get("l0f_tapqkv") and l == 0 and b == 0:
                        qt_t = DR(nc, "tap_q", [128, 4 * T], F32, kind="ExternalOutput")
                        kt_t = DR(nc, "tap_k", [128, 4 * (T + WS)], F32, kind="ExternalOutput")
                        vt_t = DR(nc, "tap_vt", [128, 20 * 512], F32, kind="ExternalOutput")
                        with contextlib.ExitStack() as sT:
                            cvt = SB(nc, sT, "tapc", 4 * (T + WS), F32)
                            kb.copy("vector", cvt.at(0, 128, 0, 4 * T), q_f.at(), tag="tapq")
                            kb.store("gpsimd", qt_t.slc((0, 128), (0, 4 * T)),
                                     cvt.at(0, 128, 0, 4 * T), tag="tapqs")
                            kb.copy("vector", cvt.at(), k_f.at(), tag="tapk")
                            kb.store("gpsimd", kt_t.slc((0, 128), (0, 4 * (T + WS))),
                                     cvt.at(), tag="tapks")
                            kb.copy("vector", cvt.at(0, 128, 0, 20 * 512), vt.at(), tag="tapv")
                            kb.store("gpsimd", vt_t.slc((0, 128), (0, 20 * 512)),
                                     cvt.at(0, 128, 0, 20 * 512), tag="tapvs")
                    with contextlib.ExitStack() as sa:
                        masks = SB(nc, sa, "fa_msk", 2 * 4 * 1024, F32)
                        att = SB(nc, sa, "fa_att", 8 * 512, BF16)
                        smf = SB(nc, sa, "fa_sm", 1024, F32)
                        pb = SB(nc, sa, "fa_p", 1024, BF16)
                        pnb = SB(nc, sa, "fa_pn", 1024, BF16)
                        stat = SB(nc, sa, "fa_stt", 4, F32)
                        o_f = SB(nc, sa, "fa_o", 8 * T, BF16, parts=64)
                        for wv in range(2):
                            for ic in range(4):
                                kb.load("sync",
                                        masks.at(0, 128, (wv * 4 + ic) * 1024,
                                                 (wv * 4 + ic + 1) * 1024),
                                        W["l0f_mask"].slc(wv, (ic * 128, (ic + 1) * 128),
                                                          (0, 1024)),
                                        tag=f"Fm{l}{b}{wv}{ic}")
                        for h in range(NH):
                            rc = h // 2
                            p0 = 64 * (h % 2)
                            for w in range(NW):
                                wv = 1 if w else 0
                                for ic in range(4):
                                    for jh in range(2):
                                        sb_ = ps.bank(jh, 0, 512)
                                        kb.mm_group(
                                            sb_,
                                            [((q_f.ap(p0, p0 + 64,
                                                      rc * T + w * WS + ic * 128,
                                                      rc * T + w * WS + (ic + 1) * 128),
                                               q_f.reg(p0, p0 + 64,
                                                       rc * T + w * WS + ic * 128,
                                                       rc * T + w * WS + (ic + 1) * 128)),
                                              (k_f.ap(p0, p0 + 64,
                                                      rc * (T + WS) + w * WS + jh * 512,
                                                      rc * (T + WS) + w * WS + (jh + 1) * 512),
                                               k_f.reg(p0, p0 + 64,
                                                       rc * (T + WS) + w * WS + jh * 512,
                                                       rc * (T + WS) + w * WS + (jh + 1) * 512)))],
                                            tag=f"SC{l}{b}{h}{w}{ic}{jh}")
                                        kb.tt("vector", ALU.add,
                                              smf.at(0, 128, jh * 512, (jh + 1) * 512), sb_,
                                              masks.at(0, 128,
                                                       (wv * 4 + ic) * 1024 + jh * 512,
                                                       (wv * 4 + ic) * 1024 + (jh + 1) * 512),
                                              tag=f"SM{l}{b}{h}{w}{ic}{jh}")
                                    kb.reduce(ALU.max, stat.at(0, 128, 0, 1), smf.at(),
                                              negate=True, tag=f"SR{l}{b}{h}{w}{ic}")
                                    kb.act(AF.Exp, pb.at(), smf.at(),
                                           bias=stat.ap(0, 128, 0, 1),
                                           extra=[stat.reg(0, 128, 0, 1)],
                                           tag=f"SE{l}{b}{h}{w}{ic}")
                                    kb.reduce(ALU.add, stat.at(0, 128, 1, 2), pb.at(),
                                              tag=f"SS{l}{b}{h}{w}{ic}")
                                    kb.recip(stat.at(0, 128, 2, 3), stat.at(0, 128, 1, 2),
                                             tag=f"SV{l}{b}{h}{w}{ic}")
                                    kb.ts(pnb.at(), pb.at(), stat.ap(0, 128, 2, 3), ALU.mult,
                                          extra=[stat.reg(0, 128, 2, 3)],
                                          tag=f"SN{l}{b}{h}{w}{ic}")
                                    if (DBG.get("l0f_tapsc") and l == 0 and b == 0
                                            and h == 0 and w == 1 and ic == 0):
                                        sm_t = DR(nc, "tap_sm", [128, 1024], F32,
                                                  kind="ExternalOutput")
                                        pn_t = DR(nc, "tap_pn", [128, 1024], F32,
                                                  kind="ExternalOutput")
                                        kb.store("gpsimd", sm_t.slc((0, 128), (0, 1024)),
                                                 smf.at(), tag="tapsm")
                                        kb.copy("vector", smf.at(), pnb.at(), tag="tappncv")
                                        kb.store("gpsimd", pn_t.slc((0, 128), (0, 1024)),
                                                 smf.at(), tag="tappn")
                                    for jc in range(8):
                                        dstp = (kb.ps.h[0:128, 7 * 512 + jc * 64:
                                                        7 * 512 + (jc + 1) * 64].bitcast(BF16),
                                                kb.ps.preg(7))
                                        pe_transpose(dstp,
                                                     pnb.at(0, 128, jc * 128, (jc + 1) * 128),
                                                     tag=f"AT{l}{b}{h}{w}{ic}{jc}")
                                    dstap = bass.AP(att.h, ic * 128,
                                                    [[att.cols, 128], [512, 8], [1, 128]])
                                    srcap = kb.ps.h[0:128, 7 * 512:8 * 512].bitcast(BF16)

                                    def evfn(e, oap=dstap, iap=srcap):
                                        return e.tensor_copy(oap, iap)
                                    em.op("vector", evfn, reads=[kb.ps.preg(7)],
                                          writes=[att.reg(0, 128, 0, 8 * 512)],
                                          tag=f"AE{l}{b}{h}{w}{ic}")
                                avp = ps.bank(2 + (w % 2), 0, 512, 0, 64)
                                pairs = []
                                for jc in range(8):
                                    jcg = w * 4 + jc
                                    pairs.append(
                                        ((vt.ap(0, 128, jcg * 512 + rc * 128 + p0,
                                                jcg * 512 + rc * 128 + p0 + 64),
                                          vt.reg(0, 128, jcg * 512 + rc * 128 + p0,
                                                 jcg * 512 + rc * 128 + p0 + 64)),
                                         att.at(0, 128, jc * 512, (jc + 1) * 512)))
                                kb.mm_group(avp, pairs, tag=f"AV{l}{b}{h}{w}")
                                kb.copy("vector",
                                        o_f.at(0, 64, h * T + w * WS, h * T + (w + 1) * WS),
                                        avp, tag=f"AO{l}{b}{h}{w}")
                        if DBG.get("l0f_tapqkv") and l == 0 and b == 0:
                            of_t = DR(nc, "tap_o", [64, 8 * T], F32, kind="ExternalOutput")
                            with contextlib.ExitStack() as sT2:
                                cvt2 = SB(nc, sT2, "tapc2", 4 * T, F32, parts=64)
                                for hf in range(2):
                                    kb.copy("vector", cvt2.at(0, 64, 0, 4 * T),
                                            o_f.at(0, 64, hf * 4 * T, (hf + 1) * 4 * T), tag=f"tapo{hf}")
                                    kb.store("gpsimd", of_t.slc((0, 64), (hf * 4 * T, (hf + 1) * 4 * T)),
                                             cvt2.at(0, 64, 0, 4 * T), tag=f"tapos{hf}")
                        with contextlib.ExitStack() as so:
                            wo_bf = SB(nc, so, "fo_wo", 8 * 1024, BF16, parts=64)
                            wst = SB(nc, so, "fo_wst", 1024, F32, parts=64)
                            xsb = SB(nc, so, "fo_x", TC, F32)
                            for h in range(NH):
                                kb.load("sync", wst.at(0, 64, 0, 1024),
                                        W["l0f_wo"].slc(l, (h * 64, (h + 1) * 64), (0, 1024)),
                                        tag=f"WO{l}{b}{h}")
                                kb.copy("vector", wo_bf.at(0, 64, h * 1024, (h + 1) * 1024),
                                        wst.at(0, 64, 0, 1024), tag=f"WOc{l}{b}{h}")
                            for c in range(NCH):
                                t0 = c * TC
                                for oc in range(DK):
                                    pt = ps.bank(4 + oc % 2, 0, TC)
                                    pairs = [((wo_bf.ap(0, 64, h * 1024 + oc * 128,
                                                        h * 1024 + (oc + 1) * 128),
                                               wo_bf.reg(0, 64, h * 1024 + oc * 128,
                                                         h * 1024 + (oc + 1) * 128)),
                                              o_f.at(0, 64, h * T + t0, h * T + t0 + TC))
                                             for h in range(NH)]
                                    kb.mm_group(pt, pairs, tag=f"WOm{l}{b}{c}{oc}")
                                    kb.load("scalar", xsb.at(),
                                            ffs.slc(b, (oc * 128, (oc + 1) * 128),
                                                    (t0, t0 + TC)),
                                            tag=f"WOx{l}{b}{c}{oc}")
                                    kb.tt("vector", ALU.add, xsb.at(), pt, xsb.at(),
                                          tag=f"WOa{l}{b}{c}{oc}")
                                    kb.store("gpsimd",
                                             ffs.slc(b, (oc * 128, (oc + 1) * 128),
                                                     (t0, t0 + TC)),
                                             xsb.at(), tag=f"WOs{l}{b}{c}{oc}")

            if skip_ffn:
                continue
            with contextlib.ExitStack() as sf:
                xsh = SB(nc, sf, "ff_xsh", DK * 513, F32)
                htail = SB(nc, sf, "ff_htl", DK, F32)
                stage = SB(nc, sf, "ff_st", 2 * DK * TC, BF16)
                h2n = SB(nc, sf, "ff_h2n", DK * TC, BF16)
                gel = SB(nc, sf, "ff_gel", HPK * TC, BF16)
                w1st = SB(nc, sf, "ff_w1s", DK * 512, F32)
                w1b = SB(nc, sf, "ff_w1b", 2 * DK * 512, BF16)
                w2b = SB(nc, sf, "ff_w2b", HPK * 1024, BF16)
                w2st = SB(nc, sf, "ff_w2s", 1024, F32)
                tt1 = SB(nc, sf, "ff_t1", TC, F32)
                tt2 = SB(nc, sf, "ff_t2", TC, F32)
                for hk in range(HPK):
                    kb.load("sync", w2st.at(),
                            W["l0f_w2p"].slc(l, (hk * 128, (hk + 1) * 128), (0, 1024)),
                            tag=f"W2{l}{b}{hk}")
                    kb.copy("vector", w2b.at(0, 128, hk * 1024, (hk + 1) * 1024),
                            w2st.at(), tag=f"W2c{l}{b}{hk}")
                for c in range(NCH):
                    t0 = c * TC
                    for dk in range(DK):
                        if t0 == 0:
                            kb.memset("gpsimd", xsh.at(0, 128, dk * 513, dk * 513 + 1),
                                      0.0, tag=f"Gh{l}{b}{c}{dk}")
                        else:
                            kb.copy("gpsimd", xsh.at(0, 128, dk * 513, dk * 513 + 1),
                                    htail.at(0, 128, dk, dk + 1), tag=f"Gh{l}{b}{c}{dk}")
                        kb.load("sync", xsh.at(0, 128, dk * 513 + 1, dk * 513 + 513),
                                ffs.slc(b, (dk * 128, (dk + 1) * 128), (t0, t0 + TC)),
                                tag=f"Gx{l}{b}{c}{dk}")
                        if c < NCH - 1:
                            kb.copy("gpsimd", htail.at(0, 128, dk, dk + 1),
                                    xsh.at(0, 128, dk * 513 + 512, dk * 513 + 513),
                                    tag=f"Gt{l}{b}{c}{dk}")
                    sh = []
                    for dk in range(DK):
                        off = 1 if dk < 4 else 0
                        sh.append(xsh.at(0, 128, dk * 513 + off, dk * 513 + off + TC))
                    mub, rsb = kb.ln_stats(sh, TC, (0, 2), stage, tag=f"Gl{l}{b}{c}")
                    for dk in range(DK):
                        kb.tt("vector", ALU.add, tt1.at(), sh[dk], mub, tag=f"Ga{l}{b}{c}{dk}")
                        kb.tt("vector", ALU.mult, tt2.at(), tt1.at(), rsb,
                              tag=f"Gm{l}{b}{c}{dk}")
                        kb.ts(h2n.at(0, 128, dk * TC, (dk + 1) * TC), tt2.at(),
                              lngc.ap(0, 128, dk * 2 + l, dk * 2 + l + 1), ALU.mult,
                              lnbc.ap(0, 128, dk * 2 + l, dk * 2 + l + 1), ALU.add,
                              extra=[lngc.reg(), lnbc.reg()], tag=f"Gg{l}{b}{c}{dk}")
                    for gfirst in (True, False):
                        for oi in range(HPK):
                            oc = oi + HPK if gfirst else oi
                            wp = oi % 2
                            kb.load("sync", w1st.at(0, 128, 0, DK * 128),
                                    W["l0f_w1p"].wslice(l, D, oc * 128, (oc + 1) * 128),
                                    tag=f"G1{l}{b}{c}{oc}")
                            kb.copy("vector",
                                    w1b.at(0, 128, wp * DK * 128, wp * DK * 128 + DK * 128),
                                    w1st.at(0, 128, 0, DK * 128), tag=f"G1c{l}{b}{c}{oc}")
                            pt = ps.bank(4 + oi % 3, 0, TC)
                            pairs = [((w1b.ap(0, 128, wp * DK * 128 + dk * 128,
                                              wp * DK * 128 + (dk + 1) * 128),
                                       w1b.reg(0, 128, wp * DK * 128 + dk * 128,
                                               wp * DK * 128 + (dk + 1) * 128)),
                                      h2n.at(0, 128, dk * TC, (dk + 1) * TC))
                                     for dk in range(DK)]
                            kb.mm_group(pt, pairs, tag=f"G1m{l}{b}{c}{oc}")
                            bcol = b1c.ap(0, 128, l * 44 + oc, l * 44 + oc + 1)
                            breg = [b1c.reg()]
                            if gfirst:
                                kb.act(AF.Gelu, gel.at(0, 128, oi * TC, (oi + 1) * TC),
                                       pt, bias=bcol, extra=breg, tag=f"G1g{l}{b}{c}{oc}")
                            else:
                                kb.stt(gel.at(0, 128, oi * TC, (oi + 1) * TC), pt, bcol,
                                       gel.at(0, 128, oi * TC, (oi + 1) * TC),
                                       ALU.add, ALU.mult, extra=breg,
                                       tag=f"G1h{l}{b}{c}{oc}")
                    for oc in range(DK):
                        pt = ps.bank(4 + oc % 2, 0, TC)
                        pairs = [((w2b.ap(0, 128, hk * 1024 + oc * 128,
                                          hk * 1024 + (oc + 1) * 128),
                                   w2b.reg(0, 128, hk * 1024 + oc * 128,
                                           hk * 1024 + (oc + 1) * 128)),
                                  gel.at(0, 128, hk * TC, (hk + 1) * TC))
                                 for hk in range(HPK)]
                        kb.mm_group(pt, pairs, tag=f"G2m{l}{b}{c}{oc}")
                        kb.stt(tt1.at(), pt,
                               b2c.ap(0, 128, oc * 2 + l, oc * 2 + l + 1),
                               xsh.at(0, 128, oc * 513 + 1, oc * 513 + 1 + TC),
                               ALU.add, ALU.add, extra=[b2c.reg()],
                               tag=f"G2a{l}{b}{c}{oc}")
                        kb.store("gpsimd",
                                 ffs.slc(b, (oc * 128, (oc + 1) * 128), (t0, t0 + TC)),
                                 tt1.at(), tag=f"G2s{l}{b}{c}{oc}")

    with contextlib.ExitStack() as s9:
        f_sb = SB(nc, s9, "f9_f", DK * TC, F32)
        xrr = SB(nc, s9, "f9_x", DK * TC, F32)
        stage = SB(nc, s9, "f9_st", 2 * DK * TC, BF16)
        tt1 = SB(nc, s9, "f9_t1", TC, F32)
        tt2 = SB(nc, s9, "f9_t2", TC, F32)
        kb._lntap_row = tt2
        for b in range(B):
            for c in range(NCH):
                t0 = c * TC
                for dk in range(DK):
                    kb.load("sync", f_sb.at(0, 128, dk * TC, (dk + 1) * TC),
                            ffs.slc(b, (dk * 128, (dk + 1) * 128), (t0, t0 + TC)),
                            tag=f"E1{b}{c}{dk}")
                    kb.load("scalar", xrr.at(0, 128, dk * TC, (dk + 1) * TC),
                            x_in.slc(b, (dk * 128, (dk + 1) * 128), (t0, t0 + TC)),
                            tag=f"E2{b}{c}{dk}")
                ft = [f_sb.at(0, 128, dk * TC, (dk + 1) * TC) for dk in range(DK)]
                mub, rsb = kb.ln_stats(ft, TC, (0, 2), stage, tag=f"El{b}{c}")
                for dk in range(DK):
                    kb.tt("vector", ALU.add, tt1.at(), ft[dk], mub, tag=f"Ea{b}{c}{dk}")
                    kb.tt("vector", ALU.mult, tt2.at(), tt1.at(), rsb, tag=f"Em{b}{c}{dk}")
                    kb.tt("vector", ALU.add, xrr.at(0, 128, dk * TC, (dk + 1) * TC),
                          xrr.at(0, 128, dk * TC, (dk + 1) * TC), tt2.at(),
                          tag=f"Er{b}{c}{dk}")
                    kb.store("gpsimd",
                             x_out.slc(b, (dk * 128, (dk + 1) * 128), (t0, t0 + TC)),
                             xrr.at(0, 128, dk * TC, (dk + 1) * TC), tag=f"Es{b}{c}{dk}")




def host_prep_l1a(inp):
    out = {}
    out["l1a_wqkvT"] = _f32(np.asarray(inp["L1_attn_w_qkv"]).T)
    wc = np.asarray(inp["L1_attn_w_conv"])
    w2 = np.zeros((8, 3, 128, 128), dtype=np.float32)
    for ch in range(8):
        for tau in range(3):
            for a in range(2):
                g = 2 * ch + a
                w2[ch, tau, 64 * a:64 * a + 64, 64 * a:64 * a + 64] = \
                    wc[g * 64:(g + 1) * 64, :, tau].T
    out["l1a_wconv2"] = _f32(w2)
    bc = np.asarray(inp["L1_attn_b_conv"])
    out["l1a_bconv"] = _f32(bc.reshape(8, 128).T)
    out["l1a_nullk"] = _f32(np.asarray(inp["L1_attn_null_k"]).reshape(1024)
                            .reshape(8, 128).T)
    out["l1a_nullv"] = _f32(np.asarray(inp["L1_attn_null_v"]).reshape(1024)
                            .reshape(8, 128).T)
    out["l1a_woutT"] = _f32(np.asarray(inp["L1_attn_w_out"]).T)
    out["l1a_bout"] = _f32(np.asarray(inp["L1_attn_b_out"]).reshape(8, 128).T)
    i = np.arange(2048)[:, None]
    jc = np.arange(683)[None, :]
    msk = np.zeros((2048, 684), dtype=np.float32)
    msk[:, 1:] = np.where(i < np.minimum(3 * jc + 1, 2047), np.float32(-1e30), 0.0)
    out["l1a_mask"] = np.ascontiguousarray(msk, dtype=ml_dtypes.bfloat16)
    return out


def emit_l1a(kb, x_in, x_out, W):
    nc, st, em = kb.nc, kb.stack, kb.em
    ps = kb.ps
    NH = 16
    J = 684
    JP = 768
    NJC = 6
    SCALE = 1.0 / 32.0

    aout = DR(nc, "l1a_out", [B, D, T], F32,
              kind="ExternalOutput" if DBG.get("l1a_tap") else "Internal")

    wconv_bf = SB(nc, st, "a1_wc", 8 * 3 * 128, BF16)
    bconv = SB(nc, st, "a1_bc", 8, F32)
    nullk = SB(nc, st, "a1_nk", 8, F32)
    nullv = SB(nc, st, "a1_nv", 8, F32)
    bout = SB(nc, st, "a1_bo", 8, F32)
    with contextlib.ExitStack() as s2:
        wst = SB(nc, s2, "a1_wst", 8 * 3 * 128, F32)
        wc2 = W["l1a_wconv2"]
        wc_ap = bass.AP(wc2.t, 0, [[128, 128], [3 * 128 * 128, 8],
                                   [128 * 128, 3], [1, 128]])
        kb.load("sync", wst.at(), (wc_ap, ("D" + wc2.name, 0, 1, 0, wc2.size)),
                tag="a1wc")
        kb.copy("vector", wconv_bf.at(), wst.at(), tag="a1wcc")
    kb.load("sync", bconv.at(), W["l1a_bconv"].slc((0, 128), (0, 8)), tag="a1bc")
    kb.load("sync", nullk.at(), W["l1a_nullk"].slc((0, 128), (0, 8)), tag="a1nk")
    kb.load("sync", nullv.at(), W["l1a_nullv"].slc((0, 128), (0, 8)), tag="a1nv")
    kb.load("sync", bout.at(), W["l1a_bout"].slc((0, 128), (0, 8)), tag="a1bo")

    def pe_transpose(out_tv, in_tv, tag):
        (oap, oreg) = out_tv
        (iap, ireg) = in_tv

        def fn(e):
            return e.transpose(oap, iap, kb.ident.ap())
        em.op("tensor", fn, reads=[ireg, kb.ident.reg()], writes=[oreg], tag=tag)

    TC = 512
    NCH = T // TC
    for b in range(B):
        with contextlib.ExitStack() as sb_:
            xn_bf = SB(nc, sb_, "a1_xn", DK * T, BF16)
            q_f = SB(nc, sb_, "a1_q", 8 * T, BF16)
            kc_f = SB(nc, sb_, "a1_kc", 8 * JP, BF16)
            vt_f = SB(nc, sb_, "a1_vt", 8 * NJC * 128, BF16)
            with contextlib.ExitStack() as sq:
                kv_f = SB(nc, sq, "a1_kv", 8 * (T + 4), BF16)
                vc_f = SB(nc, sq, "a1_vc", 8 * JP, BF16)
                x_sb = SB(nc, sq, "a1_x", DK * TC, F32)
                stage = SB(nc, sq, "a1_st", 2 * DK * TC, BF16)
                tt1 = SB(nc, sq, "a1_t1", TC, F32)
                wqst = SB(nc, sq, "a1_wq", DK * 256, F32)
                qtmp = SB(nc, sq, "a1_qt", TC, BF16)
                wq_bf = SB(nc, sq, "a1_wqb", DK * 256, BF16)
                for ch in range(8):
                    kb.memset("gpsimd", kv_f.at(0, 128, ch * (T + 4), ch * (T + 4) + 1),
                              0.0, tag=f"akp{b}{ch}")
                for c in range(NCH):
                    t0 = c * TC
                    for dk in range(DK):
                        kb.load("sync", x_sb.at(0, 128, dk * TC, (dk + 1) * TC),
                                x_in.slc(b, (dk * 128, (dk + 1) * 128), (t0, t0 + TC)),
                                tag=f"Ax{b}{c}{dk}")
                    xt = [x_sb.at(0, 128, dk * TC, (dk + 1) * TC) for dk in range(DK)]
                    mub, rsb = kb.ln_stats(xt, TC, (0, 2), stage, tag=f"Aln{b}{c}")
                    for dk in range(DK):
                        kb.tt("vector", ALU.add, tt1.at(), xt[dk], mub, tag=f"Aa{b}{c}{dk}")
                        kb.tt("vector", ALU.mult,
                              xn_bf.at(0, 128, dk * T + t0, dk * T + t0 + TC),
                              tt1.at(), rsb, tag=f"Am{b}{c}{dk}")

                def qkv_part(sls, kind, tagp):
                    for sl in sls:
                        kb.load("sync", wqst.at(),
                                W["l1a_wqkvT"].wslice(None, D, sl * 256, (sl + 1) * 256),
                                tag=f"Aw{b}{tagp}{sl}")
                        for dk in range(DK):
                            kb.copy("vector",
                                    wq_bf.at(0, 128, dk * 256, (dk + 1) * 256),
                                    wqst.at(0, 128, dk * 256, (dk + 1) * 256),
                                    tag=f"Awc{b}{tagp}{sl}{dk}")
                        for oc4 in range(2):
                            oc = sl * 2 + oc4
                            ch = oc % 8
                            for tt_ in range(4):
                                pt = ps.bank(4 + (oc4 * 4 + tt_) % 4, 0, TC)
                                pairs = [((wq_bf.ap(0, 128, dk * 256 + oc4 * 128,
                                                    dk * 256 + (oc4 + 1) * 128),
                                           wq_bf.reg(0, 128, dk * 256 + oc4 * 128,
                                                     dk * 256 + (oc4 + 1) * 128)),
                                          xn_bf.at(0, 128, dk * T + tt_ * TC,
                                                   dk * T + (tt_ + 1) * TC))
                                         for dk in range(DK)]
                                kb.mm_group(pt, pairs, tag=f"Aq{b}{tagp}{oc}{tt_}")
                                if kind == 0:
                                    kb.ts(qtmp.at(), pt, SCALE, ALU.mult,
                                          tag=f"Aqe{b}{oc}{tt_}")
                                    kb.copy("gpsimd",
                                            q_f.at(0, 128, ch * T + tt_ * TC,
                                                   ch * T + (tt_ + 1) * TC),
                                            qtmp.at(), tag=f"Aqr{b}{oc}{tt_}")
                                else:
                                    kb.copy("vector",
                                            kv_f.at(0, 128, ch * (T + 4) + 1 + tt_ * TC,
                                                    ch * (T + 4) + 1 + (tt_ + 1) * TC),
                                            pt, tag=f"Ake{b}{tagp}{oc}{tt_}")

                def compress(dst_f, nullc, which):
                    for ch in range(8):
                        for jh in range(2):
                            j0 = jh * 512
                            nj = 512 if jh == 0 else 683 - 512
                            pt = ps.bank(jh, 0, nj)
                            pairs = []
                            for tau in range(3):
                                col0 = ch * (T + 4) + 3 * j0 + tau
                                rsl = kv_f.h[0:128, col0:col0 + 3 * nj]
                                rap = bass.AP(rsl.tensor, rsl.offset,
                                              [list(rsl.ap[0]), [3, nj]])
                                pairs.append(
                                    ((wconv_bf.ap(0, 128, (ch * 3 + tau) * 128,
                                                  (ch * 3 + tau + 1) * 128),
                                      wconv_bf.reg(0, 128, (ch * 3 + tau) * 128,
                                                   (ch * 3 + tau + 1) * 128)),
                                     (rap, kv_f.reg(0, 128, ch * (T + 4),
                                                    (ch + 1) * (T + 4)))))
                            kb.mm_group(pt, pairs, tag=f"Ac{b}{which}{ch}{jh}")
                            kb.ts((dst_f.ap(0, 128, ch * JP + 1 + j0,
                                            ch * JP + 1 + j0 + nj),
                                   dst_f.reg(0, 128, ch * JP + 1 + j0,
                                             ch * JP + 1 + j0 + nj)),
                                  pt, bconv.ap(0, 128, ch, ch + 1),
                                  ALU.add, extra=[bconv.reg()],
                                  tag=f"Ace{b}{which}{ch}{jh}")
                    for ch in range(8):
                        kb.copy("vector",
                                (dst_f.ap(0, 128, ch * JP, ch * JP + 1),
                                 dst_f.reg(0, 128, ch * JP, ch * JP + 1)),
                                (nullc.ap(0, 128, ch, ch + 1), nullc.reg()),
                                tag=f"Acn{b}{which}{ch}")
                        kb.memset("gpsimd",
                                  (dst_f.ap(0, 128, ch * JP + J, (ch + 1) * JP),
                                   dst_f.reg(0, 128, ch * JP + J, (ch + 1) * JP)),
                                  0.0, tag=f"Acz{b}{which}{ch}")

                qkv_part([4, 5, 6, 7], 1, "k")
                compress(kc_f, nullk, 0)
                qkv_part([8, 9, 10, 11], 1, "v")
                compress(vc_f, nullv, 1)
                qkv_part([0, 1, 2, 3], 0, "q")
                for ch in range(8):
                    for jc in range(NJC):
                        tpb = (kb.ps.h[0:128, 7 * 512:7 * 512 + 64].bitcast(BF16),
                               kb.ps.preg(7))
                        pe_transpose(tpb,
                                     (vc_f.ap(0, 128, ch * JP + jc * 128,
                                              ch * JP + (jc + 1) * 128),
                                      vc_f.reg(0, 128, ch * JP + jc * 128,
                                               ch * JP + (jc + 1) * 128)),
                                     tag=f"avt{b}{ch}{jc}")
                        kb.copy("vector",
                                vt_f.at(0, 128, (ch * NJC + jc) * 128,
                                        (ch * NJC + jc + 1) * 128),
                                tpb, tag=f"avte{b}{ch}{jc}")
            with contextlib.ExitStack() as sa:
                masks = SB(nc, sa, "a1_msk", 16 * J, BF16)
                att = SB(nc, sa, "a1_att", NJC * 512, BF16)
                smf = SB(nc, sa, "a1_sm", J, F32)
                pnb = SB(nc, sa, "a1_pn", JP, BF16)
                pbb = SB(nc, sa, "a1_pb", JP, BF16)
                stat = SB(nc, sa, "a1_stt", 4, F32)
                o_f = SB(nc, sa, "a1_o", NH * 512, BF16, parts=64)
                wout_bf = SB(nc, sa, "a1_wo", NH * 1024, BF16, parts=64)
                wst = SB(nc, sa, "a1_ws", 1024, F32, parts=64)
                ot1 = SB(nc, sa, "a1_ot", TC, F32)
                kb.memset("gpsimd", pnb.at(), 0.0, tag=f"apz{b}")
                kb.memset("gpsimd", pbb.at(), 0.0, tag=f"apz2{b}")
                for ic in range(16):
                    kb.load("sync", masks.at(0, 128, ic * J, (ic + 1) * J),
                            W["l1a_mask"].slc((ic * 128, (ic + 1) * 128), (0, J)),
                            tag=f"aml{b}{ic}")
                for h in range(NH):
                    kb.load("sync", wst.at(0, 64, 0, 1024),
                            W["l1a_woutT"].slc((h * 64, (h + 1) * 64), (0, 1024)),
                            tag=f"awo{b}{h}")
                    kb.copy("vector", wout_bf.at(0, 64, h * 1024, (h + 1) * 1024),
                            wst.at(0, 64, 0, 1024), tag=f"awoc{b}{h}")
                for ig in range(4):
                    for h in range(NH):
                        ch = h // 2
                        p0 = 64 * (h % 2)
                        for ic4 in range(4):
                            ic = ig * 4 + ic4
                            i0 = ic * 128
                            for jh in range(2):
                                j0 = jh * 512
                                nj = 512 if jh == 0 else J - 512
                                sb2 = ps.bank(jh, 0, nj)
                                kb.mm_group(
                                    sb2,
                                    [((q_f.ap(p0, p0 + 64, ch * T + i0, ch * T + i0 + 128),
                                       q_f.reg(p0, p0 + 64, ch * T + i0, ch * T + i0 + 128)),
                                      (kc_f.ap(p0, p0 + 64, ch * JP + j0, ch * JP + j0 + nj),
                                       kc_f.reg(p0, p0 + 64, ch * JP + j0, ch * JP + j0 + nj)))],
                                    tag=f"S{b}{h}{ic}{jh}")
                                kb.tt("vector", ALU.add,
                                      smf.at(0, 128, j0, j0 + nj), sb2,
                                      masks.at(0, 128, ic * J + j0, ic * J + j0 + nj),
                                      tag=f"Sm{b}{h}{ic}{jh}")
                            kb.reduce(ALU.max, stat.at(0, 128, 0, 1), smf.at(),
                                      negate=True, tag=f"Sr{b}{h}{ic}")
                            kb.act(AF.Exp, pbb.at(0, 128, 0, J), smf.at(),
                                   bias=stat.ap(0, 128, 0, 1),
                                   extra=[stat.reg(0, 128, 0, 1)], tag=f"Se{b}{h}{ic}")
                            kb.reduce(ALU.add, stat.at(0, 128, 1, 2),
                                      pbb.at(0, 128, 0, J), tag=f"Ss{b}{h}{ic}")
                            kb.recip(stat.at(0, 128, 2, 3), stat.at(0, 128, 1, 2),
                                     tag=f"Sv{b}{h}{ic}")
                            kb.ts(pnb.at(0, 128, 0, J), pbb.at(0, 128, 0, J),
                                  stat.ap(0, 128, 2, 3), ALU.mult,
                                  extra=[stat.reg(0, 128, 2, 3)], tag=f"Sn{b}{h}{ic}")
                            if DBG.get("l1a_tap") and b == 0 and h == 0 and ic == 0:
                                smt = DR(nc, "tap_sm1", [128, J], F32,
                                         kind="ExternalOutput")
                                pnt = DR(nc, "tap_pn1", [128, JP], BF16,
                                         kind="ExternalOutput")
                                qt_ = DR(nc, "tap_q1", [64, 128], BF16,
                                         kind="ExternalOutput")
                                kct = DR(nc, "tap_kc1", [64, JP], BF16,
                                         kind="ExternalOutput")
                                kb.store("gpsimd", smt.slc((0, 128), (0, J)),
                                         smf.at(), tag="t1sm")
                                kb.store("gpsimd", pnt.slc((0, 128), (0, JP)),
                                         pnb.at(), tag="t1pn")
                                kb.store("gpsimd", qt_.slc((0, 64), (0, 128)),
                                         (q_f.ap(0, 64, 0, 128),
                                          q_f.reg(0, 64, 0, 128)), tag="t1q")
                                kb.store("gpsimd", kct.slc((0, 64), (0, JP)),
                                         (kc_f.ap(0, 64, 0, JP),
                                          kc_f.reg(0, 64, 0, JP)), tag="t1kc")
                            for jc in range(NJC):
                                dstp = (kb.ps.h[0:128, 7 * 512 + jc * 64:
                                                7 * 512 + (jc + 1) * 64].bitcast(BF16),
                                        kb.ps.preg(7))
                                pe_transpose(dstp,
                                             pnb.at(0, 128, jc * 128, (jc + 1) * 128),
                                             tag=f"St{b}{h}{ic}{jc}")
                            dstap = bass.AP(att.h, ic4 * 128,
                                            [[att.cols, 128], [512, NJC], [1, 128]])
                            srcap = kb.ps.h[0:128, 7 * 512:7 * 512 + NJC * 64].bitcast(BF16)

                            def evfn(e, oap=dstap, iap=srcap):
                                return e.tensor_copy(oap, iap)
                            em.op("vector", evfn, reads=[kb.ps.preg(7)],
                                  writes=[att.reg(0, 128, 0, NJC * 512)],
                                  tag=f"Ste{b}{h}{ic}")
                        avp = ps.bank(2 + (h % 2), 0, 512, 0, 64)
                        pairs = []
                        for jc in range(NJC):
                            pairs.append(
                                ((vt_f.ap(0, 128, (ch * NJC + jc) * 128 + p0,
                                          (ch * NJC + jc) * 128 + p0 + 64),
                                  vt_f.reg(0, 128, (ch * NJC + jc) * 128 + p0,
                                           (ch * NJC + jc) * 128 + p0 + 64)),
                                 att.at(0, 128, jc * 512, (jc + 1) * 512)))
                        kb.mm_group(avp, pairs, tag=f"AV{b}{h}{ig}")
                        kb.copy("vector", o_f.at(0, 64, h * 512, (h + 1) * 512),
                                avp, tag=f"AO{b}{h}{ig}")
                    if DBG.get("l1a_tap") and b == 0 and ig == 0:
                        oft = DR(nc, "tap_of", [64, NH * 512], BF16,
                                 kind="ExternalOutput")
                        attt = DR(nc, "tap_att", [128, NJC * 512], BF16,
                                  kind="ExternalOutput")
                        kb.store("gpsimd", oft.slc((0, 64), (0, NH * 512)),
                                 o_f.at(0, 64, 0, NH * 512), tag="t2of")
                        kb.store("gpsimd", attt.slc((0, 128), (0, NJC * 512)),
                                 att.at(0, 128, 0, NJC * 512), tag="t2att")
                    t0 = ig * 512
                    for oc in range(DK):
                        pt = ps.bank(4 + oc % 2, 0, TC)
                        pairs = [((wout_bf.ap(0, 64, h2 * 1024 + oc * 128,
                                              h2 * 1024 + (oc + 1) * 128),
                                   wout_bf.reg(0, 64, h2 * 1024 + oc * 128,
                                               h2 * 1024 + (oc + 1) * 128)),
                                  o_f.at(0, 64, h2 * 512, (h2 + 1) * 512))
                                 for h2 in range(NH)]
                        kb.mm_group(pt, pairs, tag=f"AW{b}{ig}{oc}")
                        kb.ts(ot1.at(), pt, bout.ap(0, 128, oc, oc + 1), ALU.add,
                              extra=[bout.reg()], tag=f"AWb{b}{ig}{oc}")
                        kb.store("gpsimd",
                                 aout.slc(b, (oc * 128, (oc + 1) * 128), (t0, t0 + TC)),
                                 ot1.at(), tag=f"AWs{b}{ig}{oc}")

    with contextlib.ExitStack() as s9:
        f_sb = SB(nc, s9, "a9_f", DK * TC, F32)
        xrr = SB(nc, s9, "a9_x", DK * TC, F32)
        stage = SB(nc, s9, "a9_st", 2 * DK * TC, BF16)
        tt1 = SB(nc, s9, "a9_t1", TC, F32)
        tt2 = SB(nc, s9, "a9_t2", TC, F32)
        kb._lntap_row = tt2
        for b in range(B):
            for c in range(NCH):
                t0 = c * TC
                for dk in range(DK):
                    kb.load("sync", f_sb.at(0, 128, dk * TC, (dk + 1) * TC),
                            aout.slc(b, (dk * 128, (dk + 1) * 128), (t0, t0 + TC)),
                            tag=f"Z1{b}{c}{dk}")
                    kb.load("scalar", xrr.at(0, 128, dk * TC, (dk + 1) * TC),
                            x_in.slc(b, (dk * 128, (dk + 1) * 128), (t0, t0 + TC)),
                            tag=f"Z2{b}{c}{dk}")
                ft = [f_sb.at(0, 128, dk * TC, (dk + 1) * TC) for dk in range(DK)]
                mub, rsb = kb.ln_stats(ft, TC, (0, 2), stage, tag=f"Zl{b}{c}")
                for dk in range(DK):
                    kb.tt("vector", ALU.add, tt1.at(), ft[dk], mub, tag=f"Za{b}{c}{dk}")
                    kb.tt("vector", ALU.mult, tt2.at(), tt1.at(), rsb, tag=f"Zm{b}{c}{dk}")
                    kb.tt("vector", ALU.add, xrr.at(0, 128, dk * TC, (dk + 1) * TC),
                          xrr.at(0, 128, dk * TC, (dk + 1) * TC), tt2.at(),
                          tag=f"Zr{b}{c}{dk}")
                    kb.store("gpsimd",
                             x_out.slc(b, (dk * 128, (dk + 1) * 128), (t0, t0 + TC)),
                             xrr.at(0, 128, dk * TC, (dk + 1) * TC), tag=f"Zs{b}{c}{dk}")




def host_prep_l1f(inp):
    out = {}
    out["l1f_wgT"] = _f32(np.asarray(inp["L1_ffn_wg"]).T)
    out["l1f_bg"] = _f32(np.asarray(inp["L1_ffn_bg"]).reshape(1, 8))
    out["l1f_WgT"] = _f32(np.asarray(inp["L1_ffn_Wgate"]).transpose(0, 2, 1))
    out["l1f_WuT"] = _f32(np.asarray(inp["L1_ffn_Wup"]).transpose(0, 2, 1))
    out["l1f_WdT"] = _f32(np.asarray(inp["L1_ffn_Wdown"]).transpose(0, 2, 1))
    return out


def emit_l1f(kb, x_in, x_out, W):
    nc, st, em = kb.nc, kb.stack, kb.em
    ps = kb.ps
    NE = 8
    HID = 4096
    TG = 1024
    TC = 512
    NCH = T // TC

    moe = DR(nc, "l1f_moe", [B, D, T], F32)
    wrt = DR(nc, "l1f_wrt", [B, NE, T], F32,
             kind="ExternalOutput" if DBG.get("l1f_tap") else "Internal")
    xn_sp = DR(nc, "l1f_xn", [B, D, T], BF16)

    wg_bf = SB(nc, st, "m_wg", DK * 8, BF16)
    bg_bf = SB(nc, st, "m_bg", 8, BF16, parts=1)
    with contextlib.ExitStack() as s2:
        wst = SB(nc, s2, "m_wst0", DK * 8, F32)
        kb.load("sync", wst.at(), W["l1f_wgT"].wslice(None, D, 0, 8), tag="mwg")
        kb.copy("vector", wg_bf.at(), wst.at(), tag="mwgc")
        kb.load("sync", wst.at(0, 1, 0, 8), W["l1f_bg"].slc((0, 1), (0, 8)), tag="mbg")
        kb.copy("vector", bg_bf.at(0, 1, 0, 8), wst.at(0, 1, 0, 8), tag="mbgc")

    with contextlib.ExitStack() as s1:
        x_sb = SB(nc, s1, "m_x", DK * TC, F32)
        stage = SB(nc, s1, "m_st", 2 * DK * TC, BF16)
        tt1 = SB(nc, s1, "m_t1", TC, F32)
        xnc = SB(nc, s1, "m_xn", DK * TC, BF16)
        lsb = SB(nc, s1, "m_l", 8, F32)
        rsb_ = SB(nc, s1, "m_r", 16, F32)
        wsb = SB(nc, s1, "m_w", 8, F32)
        e1 = SB(nc, s1, "m_e1", 8, F32)
        e2 = SB(nc, s1, "m_e2", 8, F32)
        e3 = SB(nc, s1, "m_e3", 8, F32)
        wrow = SB(nc, s1, "m_wr", 128, F32, parts=8)
        for b in range(B):
            for c in range(NCH):
                t0 = c * TC
                for dk in range(DK):
                    kb.load("sync", x_sb.at(0, 128, dk * TC, (dk + 1) * TC),
                            x_in.slc(b, (dk * 128, (dk + 1) * 128), (t0, t0 + TC)),
                            tag=f"Mx{b}{c}{dk}")
                xt = [x_sb.at(0, 128, dk * TC, (dk + 1) * TC) for dk in range(DK)]
                mub, rsb = kb.ln_stats(xt, TC, (0, 2), stage, tag=f"Ml{b}{c}")
                for dk in range(DK):
                    kb.tt("vector", ALU.add, tt1.at(), xt[dk], mub, tag=f"Ma{b}{c}{dk}")
                    kb.tt("vector", ALU.mult, xnc.at(0, 128, dk * TC, (dk + 1) * TC),
                          tt1.at(), rsb, tag=f"Mm{b}{c}{dk}")
                    kb.store("gpsimd",
                             xn_sp.slc(b, (dk * 128, (dk + 1) * 128), (t0, t0 + TC)),
                             xnc.at(0, 128, dk * TC, (dk + 1) * TC), tag=f"Ms{b}{c}{dk}")
                for tt_ in range(4):
                    i0 = tt_ * 128
                    lp = ps.bank(4, 0, 8)
                    pairs = [((xnc.ap(0, 128, dk * TC + i0, dk * TC + i0 + 128),
                               xnc.reg(0, 128, dk * TC + i0, dk * TC + i0 + 128)),
                              wg_bf.at(0, 128, dk * 8, (dk + 1) * 8))
                             for dk in range(DK)]
                    pairs.append(((kb.ones_row.ap(0, 1, 0, 128), kb.ones_row.reg()),
                                  bg_bf.at(0, 1, 0, 8)))
                    kb.mm_group(lp, pairs, tag=f"Mr{b}{c}{tt_}")
                    kb.copy("vector", lsb.at(), lp, tag=f"Mre{b}{c}{tt_}")
                    if DBG.get("l1f_tap") and b == 0 and c == 0 and tt_ == 0:
                        lt_ = DR(nc, "tap_lg", [128, 48], F32, kind="ExternalOutput")
                        kb.store("gpsimd", lt_.slc((0, 128), (0, 8)), lsb.at(),
                                 tag="tlg")
                    kb.reduce(ALU.max, rsb_.at(0, 128, 0, 1), lsb.at(),
                              tag=f"Mm1{b}{c}{tt_}")
                    kb.reduce(ALU.max, rsb_.at(0, 128, 1, 2), lsb.at(), negate=True,
                              tag=f"Mn1{b}{c}{tt_}")
                    kb.ts(e1.at(), lsb.at(), rsb_.ap(0, 128, 0, 1), ALU.is_equal,
                          extra=[rsb_.reg(0, 128, 0, 1)], tag=f"Meq{b}{c}{tt_}")
                    kb.stt(e2.at(), e1.at(), -1e30, lsb.at(), ALU.mult, ALU.add,
                           tag=f"Ml2{b}{c}{tt_}")
                    kb.reduce(ALU.max, rsb_.at(0, 128, 2, 3), e2.at(),
                              tag=f"Mm2{b}{c}{tt_}")
                    kb.ts(e1.at(), lsb.at(), rsb_.ap(0, 128, 2, 3), ALU.is_ge,
                          extra=[rsb_.reg(0, 128, 2, 3)], tag=f"Mkp{b}{c}{tt_}")
                    kb.act(AF.Exp, e2.at(), lsb.at(), bias=rsb_.ap(0, 128, 1, 2),
                           extra=[rsb_.reg(0, 128, 1, 2)], tag=f"Mex{b}{c}{tt_}",
                           flush=True)
                    kb.tt("vector", ALU.mult, e3.at(), e1.at(), e2.at(),
                          tag=f"Mpk{b}{c}{tt_}")
                    kb.reduce(ALU.add, rsb_.at(0, 128, 3, 4), e3.at(),
                              tag=f"Msu{b}{c}{tt_}")
                    kb.recip(rsb_.at(0, 128, 4, 5), rsb_.at(0, 128, 3, 4),
                             tag=f"Mrc{b}{c}{tt_}")
                    kb.ts(wsb.at(), e3.at(), rsb_.ap(0, 128, 4, 5), ALU.mult,
                          extra=[rsb_.reg(0, 128, 4, 5)], tag=f"Mwn{b}{c}{tt_}")
                    if DBG.get("l1f_tap") and b == 0 and c == 0 and tt_ == 0:
                        kb.store("gpsimd", lt_.slc((0, 128), (8, 16)), e1.at(), tag="te1")
                        kb.store("gpsimd", lt_.slc((0, 128), (16, 24)), e2.at(), tag="te2")
                        kb.store("gpsimd", lt_.slc((0, 128), (24, 32)), e3.at(), tag="te3")
                        kb.store("gpsimd", lt_.slc((0, 128), (32, 40)), wsb.at(), tag="tw")
                        kb.store("gpsimd", lt_.slc((0, 128), (40, 45)),
                                 rsb_.at(0, 128, 0, 5), tag="tr")
                    tp = (kb.ps.h[0:8, 7 * 512:7 * 512 + 128], kb.ps.preg(7))

                    def tfn(e, oap=tp[0], iap=wsb.ap()):
                        return e.transpose(oap, iap, kb.ident_f.ap())
                    em.op("tensor", tfn, reads=[wsb.reg(), kb.ident_f.reg()],
                          writes=[kb.ps.preg(7)], tag=f"Mtp{b}{c}{tt_}")
                    kb.copy("vector", wrow.at(0, 8, 0, 128), tp, tag=f"Mte{b}{c}{tt_}")
                    wap = bass.AP(wrt.t, b * NE * T + t0 + i0, [[T, 8], [1, 128]])
                    kb.store("gpsimd",
                             (wap, ("D" + wrt.name, 0, 1, b * NE * T + t0 + i0,
                                    b * NE * T + (NE - 1) * T + t0 + i0 + 128)),
                             wrow.at(0, 8, 0, 128), tag=f"Mts{b}{c}{tt_}")

    with contextlib.ExitStack() as s3:
        xng = SB(nc, s3, "m_xng", DK * TG, BF16)
        hact = SB(nc, s3, "m_h", 32 * TG, BF16)
        wstage = SB(nc, s3, "m_ws", DK * 512, F32)
        wgb = SB(nc, s3, "m_wgb", DK * 512, BF16)
        wub = SB(nc, s3, "m_wub", DK * 512, BF16)
        sg = SB(nc, s3, "m_sg", TC, F32)
        wdst = SB(nc, s3, "m_wds", 32 * 128, F32)
        wdb = SB(nc, s3, "m_wdb", 2 * 32 * 128, BF16)
        wbc = SB(nc, s3, "m_wbc", TG, F32)
        oacc = SB(nc, s3, "m_oac", TC, F32)
        for b in range(B):
            for g in range(T // TG):
                t0 = g * TG
                for dk in range(DK):
                    kb.load("sync", xng.at(0, 128, dk * TG, (dk + 1) * TG),
                            xn_sp.slc(b, (dk * 128, (dk + 1) * 128), (t0, t0 + TG)),
                            tag=f"Gx{b}{g}{dk}")
                for e in range(NE):
                    kb.load("scalar", wbc.at(),
                            wrt.bcast(128, b, (e, e + 1), (t0, t0 + TG)),
                            tag=f"Gw{b}{g}{e}")
                    for hc in range(8):
                        kb.load("sync", wstage.at(),
                                W["l1f_WgT"].wslice(e, D, hc * 512, (hc + 1) * 512),
                                tag=f"Gg{b}{g}{e}{hc}")
                        kb.copy("vector", wgb.at(), wstage.at(), tag=f"Ggc{b}{g}{e}{hc}")
                        kb.load("sync", wstage.at(),
                                W["l1f_WuT"].wslice(e, D, hc * 512, (hc + 1) * 512),
                                tag=f"Gu{b}{g}{e}{hc}")
                        kb.copy("vector", wub.at(), wstage.at(), tag=f"Guc{b}{g}{e}{hc}")
                        for hs in range(4):
                            for tt_ in range(2):
                                bk = (hs * 2 + tt_) % 4
                                gp = ps.bank(bk, 0, TC)
                                up = ps.bank(4 + bk, 0, TC)
                                gpairs = [((wgb.ap(0, 128, dk * 512 + hs * 128,
                                                   dk * 512 + (hs + 1) * 128),
                                            wgb.reg(0, 128, dk * 512 + hs * 128,
                                                    dk * 512 + (hs + 1) * 128)),
                                           xng.at(0, 128, dk * TG + tt_ * TC,
                                                  dk * TG + (tt_ + 1) * TC))
                                          for dk in range(DK)]
                                kb.mm_group(gp, gpairs, tag=f"GG{b}{g}{e}{hc}{hs}{tt_}")
                                upairs = [((wub.ap(0, 128, dk * 512 + hs * 128,
                                                   dk * 512 + (hs + 1) * 128),
                                            wub.reg(0, 128, dk * 512 + hs * 128,
                                                    dk * 512 + (hs + 1) * 128)),
                                           xng.at(0, 128, dk * TG + tt_ * TC,
                                                  dk * TG + (tt_ + 1) * TC))
                                          for dk in range(DK)]
                                kb.mm_group(up, upairs, tag=f"GU{b}{g}{e}{hc}{hs}{tt_}")
                                kb.act(AF.Silu, (sg.ap(), sg.reg()), gp,
                                       tag=f"GS{b}{g}{e}{hc}{hs}{tt_}")
                                hrow = hc * 4 + hs
                                kb.tt("vector", ALU.mult,
                                      hact.at(0, 128, hrow * TG + tt_ * TC,
                                              hrow * TG + (tt_ + 1) * TC),
                                      up, sg.at(), tag=f"GH{b}{g}{e}{hc}{hs}{tt_}")
                    for oc in range(DK):
                        op_ = oc % 2
                        kb.load("sync", wdst.at(),
                                W["l1f_WdT"].wslice(e, HID, oc * 128, (oc + 1) * 128),
                                tag=f"Gd{b}{g}{e}{oc}")
                        kb.copy("vector", wdb.at(0, 128, op_ * 32 * 128,
                                                 (op_ + 1) * 32 * 128),
                                wdst.at(), tag=f"Gdc{b}{g}{e}{oc}")
                        for tt_ in range(2):
                            dp = ps.bank(oc % 4, 0, TC)
                            pairs = [((wdb.ap(0, 128, op_ * 32 * 128 + hk * 128,
                                              op_ * 32 * 128 + (hk + 1) * 128),
                                       wdb.reg(0, 128, op_ * 32 * 128 + hk * 128,
                                               op_ * 32 * 128 + (hk + 1) * 128)),
                                      hact.at(0, 128, hk * TG + tt_ * TC,
                                              hk * TG + (tt_ + 1) * TC))
                                     for hk in range(32)]
                            kb.mm_group(dp, pairs, tag=f"GD{b}{g}{e}{oc}{tt_}")
                            kb.tt("vector", ALU.mult, oacc.at(), dp,
                                  wbc.at(0, 128, tt_ * TC, (tt_ + 1) * TC),
                                  tag=f"GE{b}{g}{e}{oc}{tt_}")
                            kb.store("gpsimd",
                                     moe.slc(b, (oc * 128, (oc + 1) * 128),
                                             (t0 + tt_ * TC, t0 + (tt_ + 1) * TC)),
                                     oacc.at(), accum=(e > 0),
                                     tag=f"GT{b}{g}{e}{oc}{tt_}")

    with contextlib.ExitStack() as s9:
        f_sb = SB(nc, s9, "m9_f", DK * TC, F32)
        xrr = SB(nc, s9, "m9_x", DK * TC, F32)
        stage = SB(nc, s9, "m9_st", 2 * DK * TC, BF16)
        tt1 = SB(nc, s9, "m9_t1", TC, F32)
        tt2 = SB(nc, s9, "m9_t2", TC, F32)
        for b in range(B):
            for c in range(NCH):
                t0 = c * TC
                for dk in range(DK):
                    kb.load("sync", f_sb.at(0, 128, dk * TC, (dk + 1) * TC),
                            moe.slc(b, (dk * 128, (dk + 1) * 128), (t0, t0 + TC)),
                            tag=f"Y1{b}{c}{dk}")
                    kb.load("scalar", xrr.at(0, 128, dk * TC, (dk + 1) * TC),
                            x_in.slc(b, (dk * 128, (dk + 1) * 128), (t0, t0 + TC)),
                            tag=f"Y2{b}{c}{dk}")
                ft = [f_sb.at(0, 128, dk * TC, (dk + 1) * TC) for dk in range(DK)]
                mub, rsb = kb.ln_stats(ft, TC, (0, 2), stage, tag=f"Yl{b}{c}")
                for dk in range(DK):
                    kb.tt("vector", ALU.add, tt1.at(), ft[dk], mub, tag=f"Ya{b}{c}{dk}")
                    kb.tt("vector", ALU.mult, tt2.at(), tt1.at(), rsb, tag=f"Ym{b}{c}{dk}")
                    kb.tt("vector", ALU.add, xrr.at(0, 128, dk * TC, (dk + 1) * TC),
                          xrr.at(0, 128, dk * TC, (dk + 1) * TC), tt2.at(),
                          tag=f"Yr{b}{c}{dk}")
                    kb.store("gpsimd",
                             x_out.slc(b, (dk * 128, (dk + 1) * 128), (t0, t0 + TC)),
                             xrr.at(0, 128, dk * TC, (dk + 1) * TC), tag=f"Ys{b}{c}{dk}")



L0A_INPUTS = [("l0a_inwT", [D, 4096]), ("l0a_convw", [128, 64]),
              ("l0a_convb", [128, 16]), ("l0a_xprojT", [2048, 128]),
              ("l0a_dtwT", [64, 2048]), ("l0a_dtb", [128, 16]),
              ("l0a_Alog", [128, 256]), ("l0a_diagD", [128, 16 * 128]),
              ("l0a_outwT", [2048, 1024]), ("l0a_lng", [128, 8]),
              ("l0a_lnb", [128, 8])]

L0F_INPUTS = [("l0f_wqkv", [2, 1024, 1536]), ("l0f_wo", [2, 512, 1024]),
              ("l0f_qs", [128, 2]), ("l0f_ks", [128, 2]),
              ("l0f_lng", [128, 16]), ("l0f_lnb", [128, 16]),
              ("l0f_w1p", [2, 1024, 5632]), ("l0f_b1p", [128, 88]),
              ("l0f_w2p", [2, 2816, 1024]), ("l0f_b2", [128, 16]),
              ("cst_e2", [128, 2]), ("cst_e2b", [2, 128]), ("cst_rot", [128, 128]),
              ("l0f_qcs", [128, T]), ("l0f_qss", [128, T]),
              ("l0f_kcs", [128, T]), ("l0f_kss", [128, T]),
              ("l0f_mask", [2, 512, 1024])]

L1A_INPUTS = [("l1a_wqkvT", [D, 3072]), ("l1a_wconv2", [8, 3, 128, 128]),
              ("l1a_bconv", [128, 8]), ("l1a_nullk", [128, 8]),
              ("l1a_nullv", [128, 8]), ("l1a_woutT", [D, D]),
              ("l1a_bout", [128, 8]), ("l1a_mask", [T, 684], BF16)]

L1F_INPUTS = [("l1f_wgT", [D, 8]), ("l1f_bg", [1, 8]),
              ("l1f_WgT", [8, D, 4096]), ("l1f_WuT", [8, D, 4096]),
              ("l1f_WdT", [8, 4096, D])]

BLOCKS = {
    "l0a": (emit_l0a, L0A_INPUTS, host_prep_l0a),
    "l0f": (emit_l0f, L0F_INPUTS, host_prep_l0f),
    "l1a": (emit_l1a, L1A_INPUTS, host_prep_l1a),
    "l1f": (emit_l1f, L1F_INPUTS, host_prep_l1f),
}


def _build(block_names):
    nc = bass.Bass("TRN2", target_bir_lowering=False)
    stack = contextlib.ExitStack()
    em = Em(nc)
    x_in = DR(nc, "x_fm", [B, D, T], F32, kind="ExternalInput")
    x_out = DR(nc, "xres_out", [B, D, T], F32, kind="ExternalOutput")
    W = {}
    for bn in block_names:
        for ent in BLOCKS[bn][1]:
            n, s = ent[0], ent[1]
            dt = ent[2] if len(ent) > 2 else F32
            if n not in W:
                W[n] = DR(nc, n, s, dt, kind="ExternalInput")
    kb = Kb(nc, stack, em)
    cur_in = x_in
    for i, bn in enumerate(block_names):
        cur_out = x_out if i == len(block_names) - 1 else DR(
            nc, f"xmid_{i}", [B, D, T], F32)
        bst = contextlib.ExitStack()
        kb.stack = bst
        BLOCKS[bn][0](kb, cur_in, cur_out, W)
        bst.close()
        kb.stack = stack
        cur_in = cur_out
    em.finalize()
    stack.close()
    nc.finalize()
    return nc


PIPELINE = [["l0a", "l0f"], ["l1a", "l1f"]]


def kernel(**inputs):
    x = np.asarray(inputs["L0_attn_x"])
    Wh = {}
    seen = set()
    for grp in PIPELINE:
        for bn in grp:
            if bn not in seen:
                Wh.update(BLOCKS[bn][2](inputs))
                seen.add(bn)
    xc = [np.ascontiguousarray(x[c * B:(c + 1) * B].transpose(0, 2, 1),
                               dtype=np.float32) for c in range(NCORES)]
    for grp in PIPELINE:
        nc = _build(grp)
        need = set()
        for bn in grp:
            need.update(ent[0] for ent in BLOCKS[bn][1])
        in_maps = []
        for c in range(NCORES):
            m = {"x_fm": xc[c]}
            for n in need:
                m[n] = Wh[n]
            in_maps.append(m)
        res = run_bass_kernel_spmd(nc, in_maps, core_ids=list(range(NCORES)))
        xc = [np.asarray(res.results[c]["xres_out"]) for c in range(NCORES)]
    outs = [o.transpose(0, 2, 1) for o in xc]
    return np.ascontiguousarray(np.concatenate(outs, axis=0), dtype=np.float32)


# version 06
# speedup vs baseline: 1.0163x; 1.0031x over previous
import contextlib
import numpy as np
import ml_dtypes
import concourse.bass as bass
import concourse.mybir as mybir
from concourse.bass_utils import run_bass_kernel_spmd

F32 = mybir.dt.float32
BF16 = mybir.dt.bfloat16
AF = mybir.ActivationFunctionType
ALU = mybir.AluOpType
AX = mybir.AxisListType

NCORES = 8
B = 2
T = 2048
D = 1024
DK = D // 128
LN_EPS = 1e-5

ENGINES = ("sync", "vector", "scalar", "tensor", "gpsimd")
_PAGE = 8192



class Op:
    __slots__ = ("idx", "eng", "fn", "reads", "writes", "is_dma",
                 "needs_inc", "inc_prefix", "waits", "tag", "pre_wait", "slow_wb",
                 "slow_rd")

    def __init__(self, idx, eng, fn, reads, writes, is_dma, tag=""):
        self.idx = idx
        self.eng = eng
        self.fn = fn
        self.reads = reads
        self.writes = writes
        self.is_dma = is_dma
        self.needs_inc = is_dma
        self.inc_prefix = 0
        self.waits = []
        self.tag = tag
        self.pre_wait = None
        self.slow_wb = False
        self.slow_rd = False


def _ovl(a, b):
    return (a[1] < b[2] and b[1] < a[2] and a[3] < b[4] and b[3] < a[4])


class Em:
    def __init__(self, nc):
        self.nc = nc
        self.ops = []
        self._w = {}
        self._r = {}
        self._edges = []
        self._qcount = {e: 0 for e in ENGINES}
        self._last_verify = {}

    @staticmethod
    def _pages(reg):
        return range(reg[3] // _PAGE, max(reg[4] - 1, reg[3]) // _PAGE + 1)

    def _append(self, o):
        o.idx = len(self.ops)
        self.ops.append(o)
        return o.idx

    def _mk_verify(self, q):
        cum = self._qcount[q]
        o = Op(0, q, (lambda e: e.nop()), [], [], False, tag=f"verify_{q}_{cum}")
        o.pre_wait = (("dma", q), 16 * cum)
        idx = self._append(o)
        self._edges.append(set())
        self._last_verify[q] = (cum, idx)
        return idx

    def op(self, eng, fn, reads=(), writes=(), is_dma=False, tag="", slow=False,
           slow_rd=False):
        o = Op(0, eng, fn, list(reads), list(writes), is_dma, tag)
        o.slow_wb = slow
        o.slow_rd = slow_rd
        deps = set()
        for r in o.reads:
            sp = self._w.get(r[0])
            if sp:
                for pg in self._pages(r):
                    for key, widx in sp.get(pg, {}).items():
                        if _ovl(r, (r[0],) + key):
                            deps.add(widx)
        for w in o.writes:
            sp = self._w.get(w[0])
            if sp:
                for pg in self._pages(w):
                    for key, widx in sp.get(pg, {}).items():
                        if _ovl(w, (w[0],) + key):
                            deps.add(widx)
            sp = self._r.get(w[0])
            if sp:
                for pg in self._pages(w):
                    for key, ridx in sp.get(pg, {}).items():
                        if _ovl(w, (w[0],) + key[:4]):
                            deps.add(ridx)
        ndeps = set()
        for j in deps:
            p = self.ops[j]
            if p.is_dma:
                q = p.eng
                if is_dma and q == eng:
                    continue
                lv = self._last_verify.get(q)
                if lv is None or lv[0] < p.inc_prefix:
                    ndeps.add(self._mk_verify(q))
                else:
                    ndeps.add(lv[1])
            else:
                ndeps.add(j)
        if is_dma:
            self._qcount[eng] += 1
            o.inc_prefix = self._qcount[eng]
        idx = self._append(o)
        ndeps.discard(idx)
        self._edges.append(ndeps)
        for w in o.writes:
            sp = self._w.setdefault(w[0], {})
            spr = self._r.setdefault(w[0], {})
            for pg in self._pages(w):
                d = sp.get(pg)
                if d:
                    for k in [k for k in d if _ovl(w, (w[0],) + k)]:
                        del d[k]
                dr = spr.get(pg)
                if dr:
                    for k in [k for k in dr if _ovl(w, (w[0],) + k[:4])]:
                        del dr[k]
                sp.setdefault(pg, {})[(w[1], w[2], w[3], w[4])] = idx
        for r in o.reads:
            sp = self._r.setdefault(r[0], {})
            for pg in self._pages(r):
                sp.setdefault(pg, {})[(r[1], r[2], r[3], r[4], eng)] = idx
        return o

    def finalize(self):
        nc = self.nc
        ops = self.ops

        def _skip(p, o):
            return (p.eng == o.eng and not p.is_dma
                    and p.eng in ("tensor", "gpsimd")
                    and not p.slow_wb and not o.slow_rd)

        for o in ops:
            for j in self._edges[o.idx]:
                p = ops[j]
                if _skip(p, o):
                    continue
                p.needs_inc = True
        dma_count = dict(self._qcount)
        inc_count = {e: 0 for e in ENGINES}
        eng_ops = {e: [] for e in ENGINES}
        for o in ops:
            eng_ops[o.eng].append(o)
            if not o.is_dma and o.needs_inc:
                inc_count[o.eng] += 1
                o.inc_prefix = inc_count[o.eng]
        guaranteed = {}
        for o in ops:
            ws = []
            if getattr(o, "pre_wait", None) is not None:
                ws.append(o.pre_wait)
            vals = []
            for j in self._edges[o.idx]:
                p = ops[j]
                if _skip(p, o):
                    continue
                assert not p.is_dma
                vals.append((("eng", p.eng), p.inc_prefix))
            for key, val in sorted(vals, key=lambda kv: -kv[1]):
                gk = (o.eng, key)
                if guaranteed.get(gk, 0) >= val:
                    continue
                guaranteed[gk] = val
                ws.append((key, val))
            o.waits = ws
        self._stack = contextlib.ExitStack()
        sems = {}
        for e in ENGINES:
            if inc_count[e]:
                sems[("eng", e)] = self._stack.enter_context(nc.semaphore(f"sp_{e}"))
            if dma_count[e]:
                sems[("dma", e)] = self._stack.enter_context(nc.semaphore(f"sq_{e}"))

        with nc.Block() as blk:
            def body(eng_obj, elist, ename):
                for o in elist:
                    for key, val in o.waits:
                        eng_obj.wait_ge(sems[key], val)
                    ins = o.fn(eng_obj)
                    if o.is_dma:
                        ins.then_inc(sems[("dma", o.eng)], 16)
                    elif o.needs_inc:
                        assert ins is not None, f"op {o.tag} returned None"
                        ins.then_inc(sems[("eng", o.eng)], 1)
                if ename == "sync":
                    for e in ENGINES:
                        if dma_count[e]:
                            eng_obj.wait_ge(sems[("dma", e)], 16 * dma_count[e])

            blk.sync(lambda e: body(e, eng_ops["sync"], "sync"))
            if eng_ops["vector"]:
                blk.vector(lambda e: body(e, eng_ops["vector"], "vector"))
            if eng_ops["scalar"]:
                blk.scalar(lambda e: body(e, eng_ops["scalar"], "scalar"))
            if eng_ops["tensor"]:
                blk.tensor(lambda e: body(e, eng_ops["tensor"], "tensor"))
            if eng_ops["gpsimd"]:
                blk.gpsimd(lambda e: body(e, eng_ops["gpsimd"], "gpsimd"))



_SB_UID = [0]


class SB:
    def __init__(self, nc, stack, name, cols, dtype, parts=128):
        _SB_UID[0] += 1
        self.h = stack.enter_context(
            nc.sbuf_tensor(f"{name}_{_SB_UID[0]}", [parts, cols], dtype))
        self.cols = cols
        self.parts = parts
        self.esz = 2 if dtype == BF16 else 4
        self.base = nc.lookup_mloc(self.h).addr

    def ap(self, p0=0, p1=None, c0=0, c1=None):
        p1 = self.parts if p1 is None else p1
        c1 = self.cols if c1 is None else c1
        return self.h[p0:p1, c0:c1]

    def reg(self, p0=0, p1=None, c0=0, c1=None):
        p1 = self.parts if p1 is None else p1
        c1 = self.cols if c1 is None else c1
        return ("SB", p0, p1, self.base + c0 * self.esz, self.base + c1 * self.esz)

    def at(self, p0=0, p1=None, c0=0, c1=None):
        return self.ap(p0, p1, c0, c1), self.reg(p0, p1, c0, c1)


class PSx:
    def __init__(self, nc, stack):
        self.h = stack.enter_context(nc.psum_tensor("PSALL", [128, 8 * 512], F32))

    def bank(self, b, c0=0, c1=512, p0=0, p1=128):
        return (self.h[p0:p1, b * 512 + c0: b * 512 + c1],
                ("PS", p0, p1, b, b + 1))

    def preg(self, b):
        return ("PS", 0, 128, b, b + 1)


class DR:
    def __init__(self, nc, name, shape, dtype, kind="Internal"):
        self.t = nc.dram_tensor(name, list(shape), dtype, kind=kind)
        self.name = name
        self.shape = tuple(shape)
        st = []
        s = 1
        for d in reversed(shape):
            st.insert(0, s)
            s *= d
        self.strides = st
        self.size = s

    def slc(self, *ranges):
        offset = 0
        dims = []
        norm = []
        for r in ranges:
            if isinstance(r, int):
                norm.append((r, r + 1, False))
            else:
                norm.append((r[0], r[1], True))
        assert len(norm) == len(self.shape), (self.name, ranges)
        for i, (a, b2, keep) in enumerate(norm):
            offset += a * self.strides[i]
            if keep:
                dims.append([self.strides[i], b2 - a])
        lo = offset
        hi = offset + 1
        for i, (a, b2, _) in enumerate(norm):
            hi += (b2 - a - 1) * self.strides[i]
        if not dims:
            dims = [[1, 1]]
        return bass.AP(self.t, offset, dims), ("D" + self.name, 0, 1, lo, hi)

    def wslice(self, lidx, din, c0, c1, nd=None):
        nd = (din // 128) if nd is None else nd
        dout = self.shape[-1]
        off = c0 + (lidx * self.strides[0] if lidx is not None else 0)
        dims = [[dout, 128], [128 * dout, nd], [1, c1 - c0]]
        lo = off
        hi = off + (nd * 128 - 1) * dout + (c1 - c0)
        return bass.AP(self.t, off, dims), ("D" + self.name, 0, 1, lo, hi)

    def bcast(self, parts, *ranges):
        ap, reg = self.slc(*ranges)
        dims = [list(d) for d in ap.ap if d[1] > 1]
        if not dims:
            dims = [[1, 1]]
        ap2 = bass.AP(self.t, ap.offset, [[0, parts]] + dims)
        return ap2, reg



class Kb:
    def __init__(self, nc, stack, em):
        self.nc = nc
        self.stack = stack
        self.em = em
        self.ps = PSx(nc, stack)
        self.ident = SB(nc, stack, "c_ident", 128, BF16)
        self.ones_col = SB(nc, stack, "c_ones", 1, BF16)
        self.ones_row = SB(nc, stack, "c_onesr", 128, BF16, parts=1)
        self.pcol = SB(nc, stack, "c_pcol", 1, F32)
        self._scr = SB(nc, stack, "c_scr", 128, F32)
        self.smal = SB(nc, stack, "c_smal", 2048, F32, parts=1)
        self.smal_bf = SB(nc, stack, "c_smalb", 2048, BF16, parts=1)
        self.c_sd = SB(nc, stack, "c_sd", 512, F32)
        self.c_rsb = SB(nc, stack, "c_rsb", 512, F32)
        self.ident_f = SB(nc, stack, "c_identf", 128, F32)
        em_ = em
        scr, ident, pcol = self._scr, self.ident, self.pcol
        em_.op("gpsimd", lambda e: e.iota(scr.ap(), [[1, 128]], channel_multiplier=0,
                                          allow_small_or_imprecise_dtypes=True),
               writes=[scr.reg()], tag="iota_j")
        em_.op("gpsimd", lambda e: e.iota(pcol.ap(), [[0, 1]], channel_multiplier=1,
                                          allow_small_or_imprecise_dtypes=True),
               writes=[pcol.reg()], tag="iota_p")
        em_.op("vector", lambda e: e.tensor_scalar(ident.ap(), scr.ap(), pcol.ap(),
                                                   None, ALU.is_equal),
               reads=[scr.reg(), pcol.reg()], writes=[ident.reg()], tag="ident")
        em_.op("vector", lambda e: e.tensor_scalar(self.ident_f.ap(), scr.ap(),
                                                   pcol.ap(), None, ALU.is_equal),
               reads=[scr.reg(), pcol.reg()], writes=[self.ident_f.reg()],
               tag="identf")
        em_.op("vector", lambda e: e.memset(self.ones_col.ap(), 1.0),
               writes=[self.ones_col.reg()], tag="ones_c")
        em_.op("vector", lambda e: e.memset(self.ones_row.ap(), 1.0),
               writes=[self.ones_row.reg()], tag="ones_r")
        self.constcol(1.0)
        self.constcol(LN_EPS)

    def load(self, queue, sbuf_tv, dram_tv, tag="ld"):
        (dap, dreg) = dram_tv
        (sap, sreg) = sbuf_tv

        def fn(eng):
            return eng.dma_start(out=sap, in_=dap)
        self.em.op(queue, fn, reads=[dreg], writes=[sreg], is_dma=True, tag=tag)

    def store(self, queue, dram_tv, sbuf_tv, tag="st", accum=False):
        (dap, dreg) = dram_tv
        (sap, sreg) = sbuf_tv
        kw = {"accum_op": ALU.add} if accum else {}

        def fn(eng):
            return eng.dma_start(out=dap, in_=sap, **kw)
        self.em.op(queue, fn, reads=[sreg] + ([dreg] if accum else []),
                   writes=[dreg], is_dma=True, tag=tag)

    def mm(self, out_tv, lhsT_tv, rhs_tv, start, stop, tag="mm"):
        (oap, oreg) = out_tv
        (lap, lreg) = lhsT_tv
        (rap, rreg) = rhs_tv

        def fn(eng):
            return eng.matmul(oap, lap, rap, start=start, stop=stop)
        self.em.op("tensor", fn, reads=[lreg, rreg], writes=[oreg], tag=tag)

    def mm_group(self, out_tv, pairs, tag="mmg"):
        (oap, oreg) = out_tv
        reads = []
        for (lap, lreg), (rap, rreg) in pairs:
            reads.extend([lreg, rreg])
        n = len(pairs)

        def fn(eng):
            ins = None
            for i, ((lap, _), (rap, _)) in enumerate(pairs):
                ins = eng.matmul(oap, lap, rap, start=(i == 0), stop=(i == n - 1))
            return ins
        self.em.op("tensor", fn, reads=reads, writes=[oreg], tag=tag)

    def constcol(self, val):
        key = float(val)
        if not hasattr(self, "_ccols"):
            self._ccols = {}
            self._cpool = SB(self.nc, self.stack, "c_cpool", 16, F32)
        if key not in self._ccols:
            i = len(self._ccols)
            assert i < 16
            self.memset("vector", self._cpool.at(0, 128, i, i + 1), key, tag=f"cset{i}")
            self._ccols[key] = i
        i = self._ccols[key]
        return self._cpool, i

    def act(self, func, out_tv, in_tv, bias=0.0, scale=1.0, extra=(), tag="act",
            flush=False):
        (oap, oreg) = out_tv
        (iap, ireg) = in_tv
        extra = list(extra)
        if isinstance(bias, float) and bias != 0.0:
            pool, i = self.constcol(bias)
            np_ = out_tv[1][2] - out_tv[1][1]
            bias = pool.ap(0, np_, i, i + 1)
            extra.append(pool.reg(0, 128, i, i + 1))

        def fn(eng):
            ins = eng.activation(oap, iap, func, bias=bias, scale=scale)
            if flush:
                ins = eng.drain()
            return ins
        self.em.op("scalar", fn, reads=[ireg] + extra, writes=[oreg], tag=tag)

    def tt(self, eng, op, out_tv, a_tv, b_tv, tag="tt"):
        (oap, oreg) = out_tv
        (aap, areg) = a_tv
        (bap, breg) = b_tv

        def fn(e):
            return e.tensor_tensor(oap, aap, bap, op=op)
        self.em.op(eng, fn, reads=[areg, breg], writes=[oreg], tag=tag)

    def ts(self, out_tv, in_tv, s1, op0, s2=None, op1=None, extra=(), tag="ts"):
        (oap, oreg) = out_tv
        (iap, ireg) = in_tv

        def fn(e):
            if s2 is None:
                return e.tensor_single_scalar(oap, iap, s1, op=op0)
            return e.tensor_scalar(oap, iap, s1, s2, op0, op1)
        self.em.op("vector", fn, reads=[ireg] + list(extra), writes=[oreg], tag=tag)

    def stt(self, out_tv, in0_tv, scalar, in1_tv, op0, op1, extra=(), tag="stt"):
        (oap, oreg) = out_tv
        (aap, areg) = in0_tv
        (bap, breg) = in1_tv

        def fn(e):
            return e.scalar_tensor_tensor(oap, aap, scalar, bap, op0, op1)
        self.em.op("vector", fn, reads=[areg, breg] + list(extra),
                   writes=[oreg], tag=tag)

    def copy(self, eng, out_tv, in_tv, tag="cp"):
        (oap, oreg) = out_tv
        (iap, ireg) = in_tv

        def fn(e):
            return e.tensor_copy(oap, iap)
        self.em.op(eng, fn, reads=[ireg], writes=[oreg], tag=tag)

    def memset(self, eng, tv, val, tag="ms"):
        (ap, reg) = tv

        def fn(e):
            return e.memset(ap, val)
        self.em.op(eng, fn, writes=[reg], tag=tag)

    def scan(self, out_tv, a_tv, b_tv, initial, extra=(), tag="scan"):
        (oap, oreg) = out_tv
        (aap, areg) = a_tv
        (bap, breg) = b_tv

        def fn(e):
            return e.tensor_tensor_scan(oap, aap, bap, initial, ALU.mult, ALU.add)
        self.em.op("vector", fn, reads=[areg, breg] + list(extra),
                   writes=[oreg], tag=tag, slow=True)

    def reduce(self, op, out_tv, in_tv, negate=False, tag="red"):
        (oap, oreg) = out_tv
        (iap, ireg) = in_tv

        def fn(e):
            return e.tensor_reduce(oap, iap, axis=AX.X, op=op, negate=negate)
        self.em.op("vector", fn, reads=[ireg], writes=[oreg], tag=tag, slow=True,
                   slow_rd=True)

    def recip(self, out_tv, in_tv, tag="recip"):
        (oap, oreg) = out_tv
        (iap, ireg) = in_tv

        def fn(e):
            return e.reciprocal(oap, iap)
        self.em.op("vector", fn, reads=[ireg], writes=[oreg], tag=tag, slow=True)

    def ln_stats(self, xf_tiles, Tc, banks, stage, tag="ln", n_feat=None):
        nd = len(xf_tiles)
        n = n_feat if n_feat is not None else 128 * nd
        sx = self.ps.bank(banks[0], 0, Tc, 0, 1)
        sxx = self.ps.bank(banks[0] + 1, 0, Tc, 0, 1)
        px, pxx = [], []
        ones = (self.ones_col.ap(), self.ones_col.reg())
        for i, xt in enumerate(xf_tiles):
            xbf = stage.at(0, 128, 2 * i * Tc, (2 * i + 1) * Tc)
            xsq = stage.at(0, 128, (2 * i + 1) * Tc, (2 * i + 2) * Tc)
            self.copy("vector", xbf, xt, tag=f"{tag}cv{i}")
            self.tt("gpsimd", ALU.mult, xsq, xt, xt, tag=f"{tag}sq{i}")
            px.append((ones, xbf))
            pxx.append((ones, xsq))
        self.mm_group(sx, px, tag=f"{tag}SX")
        self.mm_group(sxx, pxx, tag=f"{tag}SXX")
        sm = self.smal
        negmu = sm.at(0, 1, 0, Tc)
        mu2 = sm.at(0, 1, Tc, 2 * Tc)
        ex2 = sm.at(0, 1, 2 * Tc, 3 * Tc)
        var = sm.at(0, 1, 3 * Tc, 4 * Tc)
        self.ts(negmu, sx, -1.0 / n, ALU.mult, tag=f"{tag}mu")
        self.tt("vector", ALU.mult, mu2, negmu, negmu, tag=f"{tag}mu2")
        self.ts(ex2, sxx, 1.0 / n, ALU.mult, tag=f"{tag}ex2")
        self.tt("vector", ALU.subtract, var, ex2, mu2, tag=f"{tag}var")
        nb = self.smal_bf
        negmu_b0 = nb.at(0, 1, 0, Tc)
        var_b0 = nb.at(0, 1, Tc, 2 * Tc)
        self.copy("vector", negmu_b0, negmu, tag=f"{tag}mub")
        self.copy("vector", var_b0, var, tag=f"{tag}vrb")
        negmu_b = nb.at(0, 1, 2 * Tc, 3 * Tc)
        var_b = nb.at(0, 1, 3 * Tc, 4 * Tc)
        self.copy("gpsimd", negmu_b, negmu_b0, tag=f"{tag}mubr")
        self.copy("gpsimd", var_b, var_b0, tag=f"{tag}vrbr")
        mub = self.ps.bank(banks[1], 0, Tc)
        vbp = self.ps.bank(banks[1] + 1, 0, Tc)
        lhs1 = (self.ones_row.ap(0, 1, 0, 128), self.ones_row.reg())
        self.mm_group(mub, [(lhs1, negmu_b)], tag=f"{tag}BCmu")
        self.mm_group(vbp, [(lhs1, var_b)], tag=f"{tag}BCvar")
        sdw = self.c_sd.at(0, 128, 0, Tc)
        rsb = self.c_rsb.at(0, 128, 0, Tc)
        self.act(AF.Sqrt, sdw, vbp, bias=LN_EPS, tag=f"{tag}sd", flush=True)
        self.recip(rsb, sdw, tag=f"{tag}rs")
        if DBG.get("ln_tap") == tag:
            lt = DR(self.nc, "tap_ln", [8, Tc], F32, kind="ExternalOutput")
            st_t = DR(self.nc, "tap_stage", [128, 2 * len(xf_tiles) * Tc], BF16,
                      kind="ExternalOutput")
            row = self._lntap_row
            self.copy("vector", row.at(0, 1, 0, Tc), sx, tag="lt1")
            self.store("gpsimd", lt.slc((0, 1), (0, Tc)), row.at(0, 1, 0, Tc), tag="lt1s")
            self.copy("vector", row.at(0, 1, 0, Tc), sxx, tag="lt2")
            self.store("gpsimd", lt.slc((1, 2), (0, Tc)), row.at(0, 1, 0, Tc), tag="lt2s")
            self.store("gpsimd", lt.slc((2, 3), (0, Tc)), negmu, tag="lt3s")
            self.store("gpsimd", lt.slc((3, 4), (0, Tc)),
                       (rsb[0][0:1, 0:Tc], rsb[1]), tag="lt4s")
            self.store("gpsimd", st_t.slc((0, 128), (0, 2 * len(xf_tiles) * Tc)),
                       stage.at(0, 128, 0, 2 * len(xf_tiles) * Tc), tag="lt5s")
        return mub, rsb



def _f32(a):
    return np.ascontiguousarray(a, dtype=np.float32)


def host_prep_l0a(inp):
    out = {}
    out["l0a_inwT"] = _f32(np.asarray(inp["L0_attn_in_w"]).T)
    cw = np.asarray(inp["L0_attn_conv_w"])[:, 0, :]
    out["l0a_convw"] = _f32(cw.reshape(16, 128, 4).transpose(1, 0, 2).reshape(128, 64))
    out["l0a_convb"] = _f32(np.asarray(inp["L0_attn_conv_b"]).reshape(16, 128).T)
    xpT = np.asarray(inp["L0_attn_xproj_w"]).T
    pad = np.zeros((2048, 128), dtype=np.float32)
    pad[:, 0:16] = xpT[:, 64:80]
    pad[:, 32:48] = xpT[:, 80:96]
    pad[:, 64:128] = xpT[:, 0:64]
    out["l0a_xprojT"] = _f32(pad)
    out["l0a_dtwT"] = _f32(np.asarray(inp["L0_attn_dt_w"]).T)
    out["l0a_dtb"] = _f32(np.asarray(inp["L0_attn_dt_b"]).reshape(16, 128).T)
    al = np.asarray(inp["L0_attn_A_log"])
    out["l0a_Alog"] = _f32(al.reshape(16, 128, 16).transpose(1, 0, 2).reshape(128, 256))
    Dv = np.asarray(inp["L0_attn_D"], dtype=np.float32)
    dd = np.zeros((16, 128, 128), dtype=np.float32)
    i = np.arange(128)
    for dc in range(16):
        dd[dc, i, i] = Dv[dc * 128 + i]
    out["l0a_diagD"] = _f32(dd.transpose(1, 0, 2).reshape(128, 16 * 128))
    out["l0a_outwT"] = _f32(np.asarray(inp["L0_attn_out_w"]).T)
    out["l0a_lng"] = _f32(np.asarray(inp["L0_attn_norm_g"]).reshape(8, 128).T)
    out["l0a_lnb"] = _f32(np.asarray(inp["L0_attn_norm_b"]).reshape(8, 128).T)
    return out



def emit_l0a(kb, x_in, x_out, W):
    nc, st, em = kb.nc, kb.stack, kb.em
    ps = kb.ps
    DI = 2048
    NDC = 16
    NST = 16

    xs_sp = DR(nc, "l0a_xs", [B, D, T], F32)
    z_sp = DR(nc, "l0a_z", [B, DI, T], BF16)
    u32_sp = DR(nc, "l0a_u32", [B, DI, T], F32)
    ubf_sp = DR(nc, "l0a_ubf", [B, DI, T], BF16)
    dl_sp = DR(nc, "l0a_dl", [B, DI, T], F32)
    B_sp = DR(nc, "l0a_B", [B, NST, T], F32)
    C_sp = DR(nc, "l0a_C", [B, NST, T], F32)

    convw = SB(nc, st, "s0a_cw", 64, F32)
    convb = SB(nc, st, "s0a_cb", 16, F32)
    dtb = SB(nc, st, "sb0_dtb", 16, F32)
    lng = SB(nc, st, "s0a_g", 8, F32)
    lnb = SB(nc, st, "s0a_b", 8, F32)
    Alog = SB(nc, st, "s0a_Al", 256, F32)
    Acoef = SB(nc, st, "s0a_A", 256, F32)
    diagD_bf = SB(nc, st, "s0a_dD", NDC * 128, BF16)
    kb.load("sync", convw.at(), W["l0a_convw"].slc((0, 128), (0, 64)), tag="cw")
    kb.load("sync", convb.at(), W["l0a_convb"].slc((0, 128), (0, 16)), tag="cb")
    kb.load("sync", dtb.at(), W["l0a_dtb"].slc((0, 128), (0, 16)), tag="dtb")
    kb.load("sync", lng.at(), W["l0a_lng"].slc((0, 128), (0, 8)), tag="lng")
    kb.load("sync", lnb.at(), W["l0a_lnb"].slc((0, 128), (0, 8)), tag="lnb")
    kb.load("sync", Alog.at(), W["l0a_Alog"].slc((0, 128), (0, 256)), tag="alog")
    kb.act(AF.Exp, Acoef.at(), Alog.at(), tag="aexp")
    kb.ts(Acoef.at(), Acoef.at(), -1.0, ALU.mult, tag="aneg")

    TA = 512
    NCH = T // TA
    with contextlib.ExitStack() as sW:
        inw_bf = SB(nc, sW, "s0a_inw", DK * 4096, BF16)
        xproj_bf = SB(nc, sW, "s0a_xp", NDC * 128, BF16)
        dtw_bf = SB(nc, sW, "s0a_dtw", 2048, BF16)
        with contextlib.ExitStack() as s2:
            wst = SB(nc, s2, "l0a_wst", 4096, F32)
            for dk in range(DK):
                kb.load("sync", wst.at(),
                        W["l0a_inwT"].slc((dk * 128, (dk + 1) * 128), (0, 4096)),
                        tag=f"inw{dk}")
                kb.copy("vector", inw_bf.at(0, 128, dk * 4096, (dk + 1) * 4096),
                        wst.at(), tag=f"inwc{dk}")
            for dc in range(NDC):
                kb.load("sync", wst.at(0, 128, 0, 128),
                        W["l0a_xprojT"].slc((dc * 128, (dc + 1) * 128), (0, 128)),
                        tag=f"xp{dc}")
                kb.copy("vector", xproj_bf.at(0, 128, dc * 128, (dc + 1) * 128),
                        wst.at(0, 128, 0, 128), tag=f"xpc{dc}")
            kb.load("sync", wst.at(64, 128, 0, 2048),
                    W["l0a_dtwT"].slc((0, 64), (0, 2048)), tag="dtw")
            kb.copy("vector", dtw_bf.at(64, 128, 0, 2048), wst.at(64, 128, 0, 2048),
                    tag="dtwc")
            kb.load("sync", wst.at(0, 128, 0, NDC * 128),
                    W["l0a_diagD"].slc((0, 128), (0, NDC * 128)), tag="dD")
            kb.copy("vector", diagD_bf.at(), wst.at(0, 128, 0, NDC * 128), tag="dDc")

        with contextlib.ExitStack() as sA:
            x_sb = SB(nc, sA, "pa_x", DK * TA, F32)
            stage = SB(nc, sA, "pa_stage", 2 * DK * TA, BF16)
            xn_bf = SB(nc, sA, "pa_xn", DK * TA, BF16)
            xcv = SB(nc, sA, "pa_xcv", NDC * 520, F32)
            tail = SB(nc, sA, "pa_tail", NDC * 4, F32)
            ubf_sb = SB(nc, sA, "pa_ubf", NDC * TA, BF16)
            u32st = SB(nc, sA, "pa_u32", TA, F32)
            zst = SB(nc, sA, "pa_z", TA, BF16)
            ts1 = SB(nc, sA, "pa_t1", TA, F32)
            ts2 = SB(nc, sA, "pa_t2", TA, F32)
            draw_bf = SB(nc, sA, "pa_draw", TA, BF16)
            bc_st = SB(nc, sA, "pa_bc", 2 * TA, F32, parts=48)
            dsoft = SB(nc, sA, "pa_ds", 2 * TA, F32)
            dst = SB(nc, sA, "pa_dst", TA, F32)

            for b in range(B):
                for dc in range(NDC):
                    kb.memset("gpsimd", xcv.at(0, 128, dc * 520, dc * 520 + 3), 0.0,
                              tag=f"h0_{b}_{dc}")
                for c in range(NCH):
                    t0 = c * TA
                    for dk in range(DK):
                        kb.load("sync", x_sb.at(0, 128, dk * TA, (dk + 1) * TA),
                                x_in.slc(b, (dk * 128, (dk + 1) * 128), (t0, t0 + TA)),
                                tag=f"Ax{b}_{c}_{dk}")
                    xt = [x_sb.at(0, 128, dk * TA, (dk + 1) * TA) for dk in range(DK)]
                    mub, rsb = kb.ln_stats(xt, TA, (0, 2), stage, tag=f"Aln{b}{c}")
                    for dk in range(DK):
                        xsd = x_sb.at(0, 128, dk * TA, (dk + 1) * TA)
                        kb.tt("vector", ALU.add, ts1.at(), xt[dk], mub, tag=f"Aa{b}{c}{dk}")
                        kb.tt("vector", ALU.mult, xsd, ts1.at(), rsb, tag=f"Am{b}{c}{dk}")
                        kb.store("gpsimd",
                                 xs_sp.slc(b, (dk * 128, (dk + 1) * 128), (t0, t0 + TA)),
                                 xsd, tag=f"Axs{b}{c}{dk}")
                    xst = [x_sb.at(0, 128, dk * TA, (dk + 1) * TA) for dk in range(DK)]
                    mub2, rsb2 = kb.ln_stats(xst, TA, (0, 2), stage, tag=f"Al2{b}{c}")
                    for dk in range(DK):
                        kb.tt("vector", ALU.add, ts1.at(), xst[dk], mub2, tag=f"A2a{b}{c}{dk}")
                        kb.tt("vector", ALU.mult, ts2.at(), ts1.at(), rsb2, tag=f"A2m{b}{c}{dk}")
                        kb.ts(xn_bf.at(0, 128, dk * TA, (dk + 1) * TA), ts2.at(),
                              lng.ap(0, 128, dk, dk + 1), ALU.mult,
                              lnb.ap(0, 128, dk, dk + 1), ALU.add,
                              extra=[lng.reg(), lnb.reg()], tag=f"A2g{b}{c}{dk}")
                    for oc in range(32):
                        pt = ps.bank(4 + (oc % 4), 0, TA)
                        pairs = [((inw_bf.ap(0, 128, dk * 4096 + oc * 128,
                                             dk * 4096 + (oc + 1) * 128),
                                   inw_bf.reg(0, 128, dk * 4096 + oc * 128,
                                              dk * 4096 + (oc + 1) * 128)),
                                  xn_bf.at(0, 128, dk * TA, (dk + 1) * TA))
                                 for dk in range(DK)]
                        kb.mm_group(pt, pairs, tag=f"Aip{b}{c}{oc}")
                        if oc < 16:
                            dc = oc
                            kb.copy("vector", xcv.at(0, 128, dc * 520 + 3, dc * 520 + 515),
                                    pt, tag=f"Axc{b}{c}{oc}")
                        else:
                            zi = oc - 16
                            kb.copy("vector", zst.at(0, 128, 0, TA),
                                    pt, tag=f"Azc{b}{c}{oc}")
                            kb.store("gpsimd",
                                     z_sp.slc(b, (zi * 128, (zi + 1) * 128), (t0, t0 + TA)),
                                     zst.at(0, 128, 0, TA),
                                     tag=f"Azs{b}{c}{oc}")
                    for dc in range(NDC):
                        base = dc * 520
                        xk = [xcv.at(0, 128, base + k, base + k + 512) for k in range(4)]
                        wcol = [convw.ap(0, 128, dc * 4 + k, dc * 4 + k + 1) for k in range(4)]
                        wr = [convw.reg()]
                        kb.ts(ts1.at(), xk[0], wcol[0], ALU.mult, extra=wr, tag=f"Ac0{b}{c}{dc}")
                        kb.stt(ts2.at(), xk[1], wcol[1], ts1.at(), ALU.mult, ALU.add,
                               extra=wr, tag=f"Ac1{b}{c}{dc}")
                        kb.stt(ts1.at(), xk[2], wcol[2], ts2.at(), ALU.mult, ALU.add,
                               extra=wr, tag=f"Ac2{b}{c}{dc}")
                        kb.stt(ts2.at(), xk[3], wcol[3], ts1.at(), ALU.mult, ALU.add,
                               extra=wr, tag=f"Ac3{b}{c}{dc}")
                        kb.copy("gpsimd", tail.at(0, 128, dc * 4, dc * 4 + 3),
                                xcv.at(0, 128, base + 512, base + 515), tag=f"Atl{b}{c}{dc}")
                        kb.copy("gpsimd", xcv.at(0, 128, base, base + 3),
                                tail.at(0, 128, dc * 4, dc * 4 + 3), tag=f"Ath{b}{c}{dc}")
                        u32 = u32st.at(0, 128, 0, TA)
                        kb.act(AF.Silu, u32, ts2.at(), bias=convb.ap(0, 128, dc, dc + 1),
                               extra=[convb.reg()], tag=f"Asl{b}{c}{dc}")
                        kb.copy("vector", ubf_sb.at(0, 128, dc * TA, (dc + 1) * TA), u32,
                                tag=f"Aub{b}{c}{dc}")
                        kb.store("gpsimd",
                                 u32_sp.slc(b, (dc * 128, (dc + 1) * 128), (t0, t0 + TA)),
                                 u32, tag=f"Au3{b}{c}{dc}")
                        kb.store("gpsimd",
                                 ubf_sp.slc(b, (dc * 128, (dc + 1) * 128), (t0, t0 + TA)),
                                 ubf_sb.at(0, 128, dc * TA, (dc + 1) * TA),
                                 tag=f"Aus{b}{c}{dc}")
                    pt = ps.bank(0, 0, TA)
                    pairs = [((xproj_bf.ap(0, 128, dc * 128, (dc + 1) * 128),
                               xproj_bf.reg(0, 128, dc * 128, (dc + 1) * 128)),
                              ubf_sb.at(0, 128, dc * TA, (dc + 1) * TA))
                             for dc in range(NDC)]
                    kb.mm_group(pt, pairs, tag=f"Axp{b}{c}")
                    kb.copy("vector", bc_st.at(0, 16, 0, TA),
                            (ps.h[0:16, 0:TA], ps.preg(0)), tag=f"AB{b}{c}")
                    kb.copy("vector", bc_st.at(32, 48, TA, 2 * TA),
                            (ps.h[32:48, 0:TA], ps.preg(0)), tag=f"AC{b}{c}")
                    kb.copy("vector", draw_bf.at(64, 128, 0, TA),
                            (ps.h[64:128, 0:TA], ps.preg(0)), tag=f"AR{b}{c}")
                    kb.store("gpsimd", B_sp.slc(b, (0, 16), (t0, t0 + TA)),
                             bc_st.at(0, 16, 0, TA), tag=f"ABs{b}{c}")
                    kb.store("gpsimd", C_sp.slc(b, (0, 16), (t0, t0 + TA)),
                             bc_st.at(32, 48, TA, 2 * TA), tag=f"ACs{b}{c}")
                    for di in range(NDC):
                        pt = ps.bank(1 + (di % 2), 0, TA)
                        kb.mm_group(pt, [((dtw_bf.ap(64, 128, di * 128, (di + 1) * 128),
                                           dtw_bf.reg(64, 128, di * 128, (di + 1) * 128)),
                                          draw_bf.at(64, 128, 0, TA))],
                                    tag=f"Adt{b}{c}{di}")
                        dbcol = dtb.ap(0, 128, di, di + 1)
                        dbr = [dtb.reg()]
                        a1 = dsoft.at(0, 128, 0, TA)
                        a2 = dsoft.at(0, 128, TA, 2 * TA)
                        kb.act(AF.Abs, a1, pt, bias=dbcol, extra=dbr, tag=f"Aab{b}{c}{di}")
                        kb.act(AF.Exp, a2, a1, scale=-1.0, tag=f"Aex{b}{c}{di}")
                        kb.act(AF.Ln, a1, a2, bias=1.0, tag=f"Al1{b}{c}{di}")
                        kb.act(AF.Relu, a2, pt, bias=dbcol, extra=dbr, tag=f"Arl{b}{c}{di}")
                        dd = dst.at(0, 128, 0, TA)
                        kb.tt("vector", ALU.add, dd, a1, a2, tag=f"Add{b}{c}{di}")
                        kb.store("gpsimd",
                                 dl_sp.slc(b, (di * 128, (di + 1) * 128), (t0, t0 + TA)),
                                 dd, tag=f"Ads{b}{c}{di}")

    outw_bf = SB(nc, st, "s0a_ow", NDC * 1024, BF16)
    with contextlib.ExitStack() as s2:
        wst = SB(nc, s2, "l0a_wst4", 1024, F32)
        for dc in range(NDC):
            kb.load("sync", wst.at(),
                    W["l0a_outwT"].slc((dc * 128, (dc + 1) * 128), (0, 1024)),
                    tag=f"ow{dc}")
            kb.copy("vector", outw_bf.at(0, 128, dc * 1024, (dc + 1) * 1024), wst.at(),
                    tag=f"owc{dc}")

    TB = 512
    NTC = T // TB
    with contextlib.ExitStack() as sB:
        bbc = SB(nc, sB, "pb_B", NST * TB, BF16)
        cbc = SB(nc, sB, "pb_C", NST * TB, BF16)
        bcst = SB(nc, sB, "pb_bst", 2 * TB, F32)
        dl_sb = SB(nc, sB, "pb_dl", 2 * TB, F32)
        u32_sb = SB(nc, sB, "pb_u32", 2 * TB, F32)
        ubf_sb2 = SB(nc, sB, "pb_ubf", 2 * TB, BF16)
        zbf_sb = SB(nc, sB, "pb_z", 2 * TB, BF16)
        du_sb = SB(nc, sB, "pb_du", 2 * TB, F32)
        sz_sb = SB(nc, sB, "pb_sz", 2 * TB, F32)
        dA_sb = SB(nc, sB, "pb_dA", 2 * TB, F32)
        dBu_sb = SB(nc, sB, "pb_dBu", 2 * TB, F32)
        h_sb = SB(nc, sB, "pb_h", 2 * TB, F32)
        yt_sb = SB(nc, sB, "pb_yt", 2 * TB, BF16)
        hcar = SB(nc, sB, "pb_hc", NDC * NST, F32)
        ygat = SB(nc, sB, "pb_yg", NDC * TB, BF16)
        o_sb = SB(nc, sB, "pc_o", DK * TB, F32)
        xr_sb = SB(nc, sB, "pc_x", DK * TB, F32)
        xsr_sb = SB(nc, sB, "pc_xs", TB, F32)
        stage2 = SB(nc, sB, "pc_stage", 2 * DK * TB, BF16)
        tc1 = SB(nc, sB, "pc_t1", TB, F32)
        tc2 = SB(nc, sB, "pc_t2", TB, F32)

        for b in range(B):
            for tc in range(NTC):
                t0 = tc * TB
                for n in range(NST):
                    np_ = n % 2
                    kb.load("scalar", bcst.at(0, 128, np_ * TB, (np_ + 1) * TB),
                            B_sp.bcast(128, b, (n, n + 1), (t0, t0 + TB)),
                            tag=f"BB{b}{tc}{n}")
                    kb.copy("vector", bbc.at(0, 128, n * TB, (n + 1) * TB),
                            bcst.at(0, 128, np_ * TB, (np_ + 1) * TB), tag=f"BBc{b}{tc}{n}")
                    kb.load("scalar", bcst.at(0, 128, np_ * TB, (np_ + 1) * TB),
                            C_sp.bcast(128, b, (n, n + 1), (t0, t0 + TB)),
                            tag=f"BC{b}{tc}{n}")
                    kb.copy("vector", cbc.at(0, 128, n * TB, (n + 1) * TB),
                            bcst.at(0, 128, np_ * TB, (np_ + 1) * TB), tag=f"BCc{b}{tc}{n}")
                for dc in range(NDC):
                    dp = dc % 2
                    dcs = (dc * 128, (dc + 1) * 128)
                    dls = dl_sb.at(0, 128, dp * TB, (dp + 1) * TB)
                    u32s = u32_sb.at(0, 128, dp * TB, (dp + 1) * TB)
                    ubfs = ubf_sb2.at(0, 128, dp * TB, (dp + 1) * TB)
                    zbfs = zbf_sb.at(0, 128, dp * TB, (dp + 1) * TB)
                    dus = du_sb.at(0, 128, dp * TB, (dp + 1) * TB)
                    szs = sz_sb.at(0, 128, dp * TB, (dp + 1) * TB)
                    kb.load("sync", dls, dl_sp.slc(b, dcs, (t0, t0 + TB)), tag=f"Bd{b}{tc}{dc}")
                    kb.load("sync", u32s, u32_sp.slc(b, dcs, (t0, t0 + TB)), tag=f"Bu{b}{tc}{dc}")
                    kb.load("sync", ubfs, ubf_sp.slc(b, dcs, (t0, t0 + TB)), tag=f"Bv{b}{tc}{dc}")
                    kb.load("sync", zbfs, z_sp.slc(b, dcs, (t0, t0 + TB)), tag=f"Bz{b}{tc}{dc}")
                    kb.tt("gpsimd", ALU.mult, dus, dls, u32s, tag=f"Bdu{b}{tc}{dc}")
                    kb.act(AF.Silu, szs, zbfs, tag=f"Bsz{b}{tc}{dc}")
                    pt = ps.bank(dc % 2, 0, TB)
                    for n in range(NST):
                        pp = n % 2
                        dA = dA_sb.at(0, 128, pp * TB, (pp + 1) * TB)
                        dBu = dBu_sb.at(0, 128, pp * TB, (pp + 1) * TB)
                        hh = h_sb.at(0, 128, pp * TB, (pp + 1) * TB)
                        yt = yt_sb.at(0, 128, pp * TB, (pp + 1) * TB)
                        kb.act(AF.Exp, dA, dls,
                               scale=Acoef.ap(0, 128, dc * 16 + n, dc * 16 + n + 1),
                               extra=[Acoef.reg()], tag=f"BdA{b}{tc}{dc}_{n}")
                        e1 = "gpsimd" if (n % 4 != 3) else "vector"
                        kb.tt(e1, ALU.mult, dBu, dus,
                              bbc.at(0, 128, n * TB, (n + 1) * TB), tag=f"BdB{b}{tc}{dc}_{n}")
                        hc = (hcar.ap(0, 128, dc * NST + n, dc * NST + n + 1),
                              hcar.reg(0, 128, dc * NST + n, dc * NST + n + 1))
                        if tc == 0:
                            kb.scan(hh, dA, dBu, 0.0, tag=f"Bsc{b}{tc}{dc}_{n}")
                        else:
                            kb.scan(hh, dA, dBu, hc[0], extra=[hc[1]],
                                    tag=f"Bsc{b}{tc}{dc}_{n}")
                        if tc < NTC - 1:
                            kb.copy("gpsimd", hc,
                                    (h_sb.ap(0, 128, (pp + 1) * TB - 1, (pp + 1) * TB),
                                     h_sb.reg(0, 128, (pp + 1) * TB - 1, (pp + 1) * TB)),
                                    tag=f"Bhc{b}{tc}{dc}_{n}")
                        e2 = "vector" if (n % 4 != 1) else "gpsimd"
                        kb.tt(e2, ALU.mult, yt, hh,
                              cbc.at(0, 128, n * TB, (n + 1) * TB), tag=f"Byt{b}{tc}{dc}_{n}")
                        kb.mm(pt, (kb.ident.ap(), kb.ident.reg()), yt,
                              start=(n == 0), stop=False, tag=f"Bya{b}{tc}{dc}_{n}")
                    kb.mm(pt, (diagD_bf.ap(0, 128, dc * 128, (dc + 1) * 128),
                               diagD_bf.reg(0, 128, dc * 128, (dc + 1) * 128)),
                          ubfs, start=False, stop=True, tag=f"BDu{b}{tc}{dc}")
                    kb.tt("vector", ALU.mult, ygat.at(0, 128, dc * TB, (dc + 1) * TB),
                          pt, szs, tag=f"Byg{b}{tc}{dc}")

                for dk in range(DK):
                    kb.load("sync", xr_sb.at(0, 128, dk * TB, (dk + 1) * TB),
                            x_in.slc(b, (dk * 128, (dk + 1) * 128), (t0, t0 + TB)),
                            tag=f"Cx{b}{tc}{dk}")
                for oc in range(DK):
                    pt = ps.bank(2 + oc % 2, 0, TB)
                    pairs = [((outw_bf.ap(0, 128, dc * 1024 + oc * 128,
                                          dc * 1024 + (oc + 1) * 128),
                               outw_bf.reg(0, 128, dc * 1024 + oc * 128,
                                           dc * 1024 + (oc + 1) * 128)),
                              ygat.at(0, 128, dc * TB, (dc + 1) * TB))
                             for dc in range(NDC)]
                    kb.mm_group(pt, pairs, tag=f"Cop{b}{tc}{oc}")
                    kb.load("scalar", xsr_sb.at(),
                            xs_sp.slc(b, (oc * 128, (oc + 1) * 128), (t0, t0 + TB)),
                            tag=f"Cxs{b}{tc}{oc}")
                    kb.tt("vector", ALU.add, o_sb.at(0, 128, oc * TB, (oc + 1) * TB),
                          pt, xsr_sb.at(), tag=f"Co{b}{tc}{oc}")
                ot = [o_sb.at(0, 128, dk * TB, (dk + 1) * TB) for dk in range(DK)]
                mub, rsb = kb.ln_stats(ot, TB, (4, 6), stage2, tag=f"Cln{b}{tc}")
                for dk in range(DK):
                    kb.tt("vector", ALU.add, tc1.at(), ot[dk], mub, tag=f"Ca{b}{tc}{dk}")
                    kb.tt("vector", ALU.mult, tc2.at(), tc1.at(), rsb, tag=f"Cm{b}{tc}{dk}")
                    kb.tt("vector", ALU.add, xr_sb.at(0, 128, dk * TB, (dk + 1) * TB),
                          xr_sb.at(0, 128, dk * TB, (dk + 1) * TB), tc2.at(),
                          tag=f"Cr{b}{tc}{dk}")
                    kb.store("gpsimd",
                             x_out.slc(b, (dk * 128, (dk + 1) * 128), (t0, t0 + TB)),
                             xr_sb.at(0, 128, dk * TB, (dk + 1) * TB), tag=f"Cst{b}{tc}{dk}")




def host_prep_l0f(inp):
    out = {}
    out["l0f_wqkv"] = _f32(inp["L0_ffn_wqkv"])
    out["l0f_wo"] = _f32(inp["L0_ffn_wo"])
    qs = np.asarray(inp["L0_ffn_q_scale"])
    ks = np.asarray(inp["L0_ffn_k_scale"])
    p = np.arange(128)
    out["l0f_qs"] = _f32(qs[:, p % 64].T)
    out["l0f_ks"] = _f32(ks[:, p % 64].T)
    lg = np.asarray(inp["L0_ffn_ln_g"])
    lb = np.asarray(inp["L0_ffn_ln_b"])
    out["l0f_lng"] = _f32(np.stack([lg[l].reshape(8, 128).T for l in range(2)], -1)
                          .reshape(128, 16))
    out["l0f_lnb"] = _f32(np.stack([lb[l].reshape(8, 128).T for l in range(2)], -1)
                          .reshape(128, 16))
    HID = 2730
    HP = 2816
    w1 = np.asarray(inp["L0_ffn_w1"])
    b1 = np.asarray(inp["L0_ffn_b1"])
    w1p = np.zeros((2, 1024, 2 * HP), dtype=np.float32)
    w1p[:, :, 0:HID] = w1[:, :, 0:HID]
    w1p[:, :, HP:HP + HID] = w1[:, :, HID:2 * HID]
    out["l0f_w1p"] = _f32(w1p)
    b1p = np.zeros((2, 2 * HP), dtype=np.float32)
    b1p[:, 0:HID] = b1[:, 0:HID]
    b1p[:, HP:HP + HID] = b1[:, HID:2 * HID]
    nb = np.zeros((128, 88), dtype=np.float32)
    for l in range(2):
        for oc in range(44):
            nb[:, l * 44 + oc] = b1p[l, oc * 128:(oc + 1) * 128]
    out["l0f_b1p"] = _f32(nb)
    w2 = np.asarray(inp["L0_ffn_w2"])
    w2p = np.zeros((2, HP, 1024), dtype=np.float32)
    w2p[:, 0:HID, :] = w2
    out["l0f_w2p"] = _f32(w2p)
    b2 = np.asarray(inp["L0_ffn_b2"])
    out["l0f_b2"] = _f32(np.stack([b2[l].reshape(8, 128).T for l in range(2)], -1)
                         .reshape(128, 16))
    out["cst_e2"] = _f32((np.arange(128)[:, None] // 64 == np.arange(2)[None, :]))
    out["cst_e2b"] = _f32((np.arange(2)[:, None] == np.arange(128)[None, :] // 64))
    R = np.zeros((128, 128), dtype=np.float32)
    for blk in range(2):
        o = blk * 64
        for dd in range(32):
            R[o + dd, o + dd + 32] = -1.0
            R[o + dd + 32, o + dd] = 1.0
    out["cst_rot"] = _f32(R.T)
    DIM_HEAD = 64
    WINDOW = 512
    half = np.arange(0, DIM_HEAD, 2, dtype=np.float64)
    inv_freq = 1.0 / 10000.0 ** (half / DIM_HEAD)
    t = np.arange(T, dtype=np.float64)
    freqs = t[:, None] * inv_freq[None, :]
    freqs = np.concatenate([freqs, freqs], -1)
    base = (half + 0.4 * DIM_HEAD) / (1.4 * DIM_HEAD)
    power = (t - T // 2) / float(WINDOW // 2)
    scale = base[None, :] ** power[:, None]
    scale = np.concatenate([scale, scale], -1)
    cos = np.cos(freqs)
    sin = np.sin(freqs)
    p64 = np.arange(128) % 64
    out["l0f_qcs"] = _f32((cos * scale * 8.0).T[p64])
    out["l0f_qss"] = _f32((sin * scale * 8.0).T[p64])
    out["l0f_kcs"] = _f32((cos / scale).T[p64])
    out["l0f_kss"] = _f32((sin / scale).T[p64])
    W_ = 512
    il = np.arange(W_)
    jl = np.arange(2 * W_)
    masks = np.zeros((2, W_, 2 * W_), dtype=np.float32)
    for wv in range(2):
        w = 1 if wv else 0
        ig = w * W_ + il
        jg = (w - 1) * W_ + jl
        valid = np.ones((W_, 2 * W_), dtype=bool)
        jgb = np.broadcast_to(jg[None, :], (W_, 2 * W_))
        igb = np.broadcast_to(ig[:, None], (W_, 2 * W_))
        bad = (igb < jgb) | (igb > jgb + W_) | (jgb < 0)
        masks[wv][bad] = -1e30
    out["l0f_mask"] = _f32(masks)
    return out


DBG = {}


def emit_l0f(kb, x_in, x_out, W):
    nc, st, em = kb.nc, kb.stack, kb.em
    ps = kb.ps
    NH = 8
    NW = 4
    WS = 512
    HPK = 22
    TC = 512
    NCH = T // TC
    n_layers = DBG.get("l0f_layers", 2)
    skip_attn = DBG.get("l0f_skip_attn", False)
    skip_ffn = DBG.get("l0f_skip_ffn", False)

    ffs = DR(nc, "l0f_ffs", [B, D, T], F32,
             kind="ExternalOutput" if DBG.get("l0f_tap") else "Internal")

    e2 = SB(nc, st, "f_e2", 2, BF16)
    e2b = SB(nc, st, "f_e2b", 128, BF16, parts=2)
    rotm = SB(nc, st, "f_rot", 128, BF16)
    qscol = SB(nc, st, "f_qs", 2, F32)
    kscol = SB(nc, st, "f_ks", 2, F32)
    lngc = SB(nc, st, "f_lng", 16, F32)
    lnbc = SB(nc, st, "f_lnb", 16, F32)
    b1c = SB(nc, st, "f_b1", 88, F32)
    b2c = SB(nc, st, "f_b2", 16, F32)
    with contextlib.ExitStack() as s2:
        wst = SB(nc, s2, "f_wst0", 128, F32)
        kb.load("sync", wst.at(0, 128, 0, 2), W["cst_e2"].slc((0, 128), (0, 2)), tag="e2")
        kb.copy("vector", e2.at(), wst.at(0, 128, 0, 2), tag="e2c")
        kb.load("sync", wst.at(0, 2, 0, 128), W["cst_e2b"].slc((0, 2), (0, 128)), tag="e2b")
        kb.copy("vector", e2b.at(0, 2, 0, 128), wst.at(0, 2, 0, 128), tag="e2bc")
        kb.load("sync", wst.at(), W["cst_rot"].slc((0, 128), (0, 128)), tag="rot")
        kb.copy("vector", rotm.at(), wst.at(), tag="rotc")
    kb.load("sync", qscol.at(), W["l0f_qs"].slc((0, 128), (0, 2)), tag="qs")
    kb.load("sync", kscol.at(), W["l0f_ks"].slc((0, 128), (0, 2)), tag="ks")
    kb.load("sync", lngc.at(), W["l0f_lng"].slc((0, 128), (0, 16)), tag="flng")
    kb.load("sync", lnbc.at(), W["l0f_lnb"].slc((0, 128), (0, 16)), tag="flnb")
    kb.load("sync", b1c.at(), W["l0f_b1p"].slc((0, 128), (0, 88)), tag="fb1")
    kb.load("sync", b2c.at(), W["l0f_b2"].slc((0, 128), (0, 16)), tag="fb2")

    with contextlib.ExitStack() as s0:
        x_sb = SB(nc, s0, "f0_x", DK * TC, F32)
        stage = SB(nc, s0, "f0_st", 2 * DK * TC, BF16)
        tt1 = SB(nc, s0, "f0_t1", TC, F32)
        for b in range(B):
            for c in range(NCH):
                t0 = c * TC
                for dk in range(DK):
                    kb.load("sync", x_sb.at(0, 128, dk * TC, (dk + 1) * TC),
                            x_in.slc(b, (dk * 128, (dk + 1) * 128), (t0, t0 + TC)),
                            tag=f"F0x{b}{c}{dk}")
                xt = [x_sb.at(0, 128, dk * TC, (dk + 1) * TC) for dk in range(DK)]
                mub, rsb = kb.ln_stats(xt, TC, (0, 2), stage, tag=f"F0l{b}{c}")
                for dk in range(DK):
                    kb.tt("vector", ALU.add, tt1.at(), xt[dk], mub, tag=f"F0a{b}{c}{dk}")
                    kb.tt("vector", ALU.mult, x_sb.at(0, 128, dk * TC, (dk + 1) * TC),
                          tt1.at(), rsb, tag=f"F0m{b}{c}{dk}")
                    kb.store("gpsimd", ffs.slc(b, (dk * 128, (dk + 1) * 128), (t0, t0 + TC)),
                             x_sb.at(0, 128, dk * TC, (dk + 1) * TC), tag=f"F0s{b}{c}{dk}")

    def pe_transpose(out_tv, in_tv, tag):
        (oap, oreg) = out_tv
        (iap, ireg) = in_tv

        def fn(e):
            return e.transpose(oap, iap, kb.ident.ap())
        em.op("tensor", fn, reads=[ireg, kb.ident.reg()], writes=[oreg], tag=tag)

    for l in range(n_layers):
        for b in range(B):
            if not skip_attn:
                with contextlib.ExitStack() as sq:
                    q_f = SB(nc, sq, "fq_q", 4 * T, BF16)
                    k_f = SB(nc, sq, "fq_k", 4 * (T + WS), BF16)
                    vt = SB(nc, sq, "fa_vt", 20 * 512, BF16)
                    for rc in range(4):
                        kb.memset("gpsimd", k_f.at(0, 128, rc * (T + WS), rc * (T + WS) + WS),
                                  0.0, tag=f"kp{l}{b}{rc}")
                    kb.memset("gpsimd", vt.at(0, 128, 0, 4 * 512), 0.0, tag=f"vtp{l}{b}")
                    with contextlib.ExitStack() as sv2:
                        v_f = SB(nc, sv2, "fq_v", 4 * (T + WS), BF16)
                        xsh = SB(nc, sv2, "fq_xsh", DK * 513, F32)
                        xshb = SB(nc, sv2, "fq_xshb", DK * 513, BF16)
                        wqst = SB(nc, sv2, "fq_wst", DK * 256, F32)
                        wq_bf = SB(nc, sv2, "fq_wb", DK * 1536, BF16)
                        qcs_t = SB(nc, sv2, "fq_qcs", T, F32)
                        qss_t = SB(nc, sv2, "fq_qss", T, F32)
                        kcs_t = SB(nc, sv2, "fq_kcs", T, F32)
                        kss_t = SB(nc, sv2, "fq_kss", T, F32)
                        sqs = SB(nc, sv2, "fq_sq", TC, BF16)
                        qraw = SB(nc, sv2, "fq_qraw", TC, F32)
                        nrm = SB(nc, sv2, "fq_nrm", 3 * TC, F32, parts=2)
                        nrmb = SB(nc, sv2, "fq_nrmb", TC, BF16, parts=2)
                        rbc = SB(nc, sv2, "fq_rbc", TC, F32)
                        qn32 = SB(nc, sv2, "fq_qn", TC, F32)
                        qnb = SB(nc, sv2, "fq_qnb", TC, BF16)
                        s1 = SB(nc, sv2, "fq_s1", TC, F32)
                        s2b = SB(nc, sv2, "fq_s2", TC, F32)
                        kb.memset("gpsimd", v_f.at(0, 128, 0, 4 * (T + WS)), 0.0,
                                  tag=f"vp{l}{b}")
                        kb.load("sync", qcs_t.at(), W["l0f_qcs"].slc((0, 128), (0, T)),
                                tag=f"tq{l}{b}")
                        kb.load("sync", qss_t.at(), W["l0f_qss"].slc((0, 128), (0, T)),
                                tag=f"tq2{l}{b}")
                        kb.load("sync", kcs_t.at(), W["l0f_kcs"].slc((0, 128), (0, T)),
                                tag=f"tk{l}{b}")
                        kb.load("sync", kss_t.at(), W["l0f_kss"].slc((0, 128), (0, T)),
                                tag=f"tk2{l}{b}")
                        for sl in range(6):
                            kb.load("sync", wqst.at(),
                                    W["l0f_wqkv"].wslice(l, D, sl * 256, (sl + 1) * 256),
                                    tag=f"Qw{l}{b}{sl}")
                            for dk in range(DK):
                                kb.copy("vector",
                                        wq_bf.at(0, 128, dk * 1536 + sl * 256,
                                                 dk * 1536 + (sl + 1) * 256),
                                        wqst.at(0, 128, dk * 256, (dk + 1) * 256),
                                        tag=f"Qwc{l}{b}{sl}{dk}")
                        for c in range(NCH):
                            t0 = c * TC
                            for dk in range(DK):
                                if t0 == 0:
                                    kb.memset("gpsimd", xsh.at(0, 128, dk * 513, dk * 513 + 1),
                                              0.0, tag=f"Qh{l}{b}{c}{dk}")
                                    kb.load("sync", xsh.at(0, 128, dk * 513 + 1, dk * 513 + 513),
                                            ffs.slc(b, (dk * 128, (dk + 1) * 128), (0, TC)),
                                            tag=f"Qx{l}{b}{c}{dk}")
                                else:
                                    kb.load("sync", xsh.at(0, 128, dk * 513, dk * 513 + 513),
                                            ffs.slc(b, (dk * 128, (dk + 1) * 128),
                                                    (t0 - 1, t0 + TC)),
                                            tag=f"Qx{l}{b}{c}{dk}")
                                kb.copy("vector", xshb.at(0, 128, dk * 513, (dk + 1) * 513),
                                        xsh.at(0, 128, dk * 513, (dk + 1) * 513),
                                        tag=f"Qc{l}{b}{c}{dk}")
                            for oc in range(12):
                                pt = ps.bank(4 + oc % 3, 0, TC)
                                pairs = []
                                for dk in range(DK):
                                    off = 1 if dk < 4 else 0
                                    pairs.append(
                                        ((wq_bf.ap(0, 128, dk * 1536 + oc * 128,
                                                   dk * 1536 + (oc + 1) * 128),
                                          wq_bf.reg(0, 128, dk * 1536 + oc * 128,
                                                    dk * 1536 + (oc + 1) * 128)),
                                         xshb.at(0, 128, dk * 513 + off, dk * 513 + off + TC)))
                                kb.mm_group(pt, pairs, tag=f"Qmm{l}{b}{c}{oc}")
                                if oc >= 8:
                                    rc = oc - 8
                                    kb.copy("vector",
                                            v_f.at(0, 128, rc * (T + WS) + WS + t0,
                                                   rc * (T + WS) + WS + t0 + TC),
                                            pt, tag=f"Qv{l}{b}{c}{oc}")
                                    continue
                                isq = oc < 4
                                rc = oc % 4
                                kb.copy("vector", qraw.at(), pt, tag=f"Qqr{l}{b}{c}{oc}")
                                kb.tt("gpsimd", ALU.mult, sqs.at(), qraw.at(), qraw.at(),
                                      tag=f"Qsq{l}{b}{c}{oc}")
                                ssp = ps.bank(7, 0, TC, 0, 2)
                                kb.mm_group(ssp, [((e2.ap(), e2.reg()), sqs.at())],
                                            tag=f"Qss{l}{b}{c}{oc}")
                                kb.act(AF.Sqrt, nrm.at(0, 2, 0, TC), ssp, tag=f"Qn1{l}{b}{c}{oc}")
                                kb.ts(nrm.at(0, 2, TC, 2 * TC), nrm.at(0, 2, 0, TC),
                                      1e-12, ALU.max, tag=f"Qn2{l}{b}{c}{oc}")
                                kb.recip(nrm.at(0, 2, 2 * TC, 3 * TC),
                                         nrm.at(0, 2, TC, 2 * TC), tag=f"Qn3{l}{b}{c}{oc}")
                                kb.copy("vector", nrmb.at(0, 2, 0, TC),
                                        nrm.at(0, 2, 2 * TC, 3 * TC), tag=f"Qn4{l}{b}{c}{oc}")
                                rbp = ps.bank(7, 0, TC)
                                kb.mm_group(rbp, [((e2b.ap(0, 2, 0, 128), e2b.reg(0, 2, 0, 128)),
                                                   nrmb.at(0, 2, 0, TC))],
                                            tag=f"Qrb{l}{b}{c}{oc}")
                                kb.copy("vector", rbc.at(), rbp, tag=f"Qrc{l}{b}{c}{oc}")
                                scol = qscol if isq else kscol
                                kb.stt(qn32.at(), pt, scol.ap(0, 128, l, l + 1), rbc.at(),
                                       ALU.mult, ALU.mult, extra=[scol.reg()],
                                       tag=f"Qsc{l}{b}{c}{oc}")
                                kb.copy("vector", qnb.at(), qn32.at(), tag=f"Qnb{l}{b}{c}{oc}")
                                rp = ps.bank(7, 0, TC)
                                kb.mm_group(rp, [((rotm.ap(), rotm.reg()), qnb.at())],
                                            tag=f"Qrt{l}{b}{c}{oc}")
                                cst = qcs_t if isq else kcs_t
                                snt = qss_t if isq else kss_t
                                kb.tt("vector", ALU.mult, s1.at(), qn32.at(),
                                      cst.at(0, 128, t0, t0 + TC), tag=f"Q1{l}{b}{c}{oc}")
                                kb.tt("vector", ALU.mult, s2b.at(), rp,
                                      snt.at(0, 128, t0, t0 + TC), tag=f"Q2{l}{b}{c}{oc}")
                                if isq:
                                    dstv = q_f.at(0, 128, rc * T + t0, rc * T + t0 + TC)
                                else:
                                    dstv = k_f.at(0, 128, rc * (T + WS) + WS + t0,
                                                  rc * (T + WS) + WS + t0 + TC)
                                kb.tt("vector", ALU.add, dstv, s1.at(), s2b.at(),
                                      tag=f"Q3{l}{b}{c}{oc}")
                        for rc in range(4):
                            for jc in range(4, 20):
                                tpb = (kb.ps.h[0:128, 7 * 512:7 * 512 + 64].bitcast(BF16),
                                       kb.ps.preg(7))
                                pe_transpose(tpb,
                                             (v_f.ap(0, 128, rc * (T + WS) + jc * 128,
                                                     rc * (T + WS) + (jc + 1) * 128),
                                              v_f.reg(0, 128, rc * (T + WS) + jc * 128,
                                                      rc * (T + WS) + (jc + 1) * 128)),
                                             tag=f"vt{l}{b}{rc}{jc}")
                                kb.copy("vector",
                                        vt.at(0, 128, jc * 512 + rc * 128,
                                              jc * 512 + (rc + 1) * 128),
                                        tpb, tag=f"vte{l}{b}{rc}{jc}")
                    if DBG.get("l0f_tapqkv") and l == 0 and b == 0:
                        qt_t = DR(nc, "tap_q", [128, 4 * T], F32, kind="ExternalOutput")
                        kt_t = DR(nc, "tap_k", [128, 4 * (T + WS)], F32, kind="ExternalOutput")
                        vt_t = DR(nc, "tap_vt", [128, 20 * 512], F32, kind="ExternalOutput")
                        with contextlib.ExitStack() as sT:
                            cvt = SB(nc, sT, "tapc", 4 * (T + WS), F32)
                            kb.copy("vector", cvt.at(0, 128, 0, 4 * T), q_f.at(), tag="tapq")
                            kb.store("gpsimd", qt_t.slc((0, 128), (0, 4 * T)),
                                     cvt.at(0, 128, 0, 4 * T), tag="tapqs")
                            kb.copy("vector", cvt.at(), k_f.at(), tag="tapk")
                            kb.store("gpsimd", kt_t.slc((0, 128), (0, 4 * (T + WS))),
                                     cvt.at(), tag="tapks")
                            kb.copy("vector", cvt.at(0, 128, 0, 20 * 512), vt.at(), tag="tapv")
                            kb.store("gpsimd", vt_t.slc((0, 128), (0, 20 * 512)),
                                     cvt.at(0, 128, 0, 20 * 512), tag="tapvs")
                    with contextlib.ExitStack() as sa:
                        masks = SB(nc, sa, "fa_msk", 2 * 4 * 1024, F32)
                        att = SB(nc, sa, "fa_att", 8 * 512, BF16)
                        smf = SB(nc, sa, "fa_sm", 1024, F32)
                        pb = SB(nc, sa, "fa_p", 1024, BF16)
                        pnb = SB(nc, sa, "fa_pn", 1024, BF16)
                        stat = SB(nc, sa, "fa_stt", 4, F32)
                        o_f = SB(nc, sa, "fa_o", 8 * T, BF16, parts=64)
                        for wv in range(2):
                            for ic in range(4):
                                kb.load("sync",
                                        masks.at(0, 128, (wv * 4 + ic) * 1024,
                                                 (wv * 4 + ic + 1) * 1024),
                                        W["l0f_mask"].slc(wv, (ic * 128, (ic + 1) * 128),
                                                          (0, 1024)),
                                        tag=f"Fm{l}{b}{wv}{ic}")
                        for h in range(NH):
                            rc = h // 2
                            p0 = 64 * (h % 2)
                            for w in range(NW):
                                wv = 1 if w else 0
                                for ic in range(4):
                                    for jh in range(2):
                                        sb_ = ps.bank(jh, 0, 512)
                                        kb.mm_group(
                                            sb_,
                                            [((q_f.ap(p0, p0 + 64,
                                                      rc * T + w * WS + ic * 128,
                                                      rc * T + w * WS + (ic + 1) * 128),
                                               q_f.reg(p0, p0 + 64,
                                                       rc * T + w * WS + ic * 128,
                                                       rc * T + w * WS + (ic + 1) * 128)),
                                              (k_f.ap(p0, p0 + 64,
                                                      rc * (T + WS) + w * WS + jh * 512,
                                                      rc * (T + WS) + w * WS + (jh + 1) * 512),
                                               k_f.reg(p0, p0 + 64,
                                                       rc * (T + WS) + w * WS + jh * 512,
                                                       rc * (T + WS) + w * WS + (jh + 1) * 512)))],
                                            tag=f"SC{l}{b}{h}{w}{ic}{jh}")
                                        kb.tt("vector", ALU.add,
                                              smf.at(0, 128, jh * 512, (jh + 1) * 512), sb_,
                                              masks.at(0, 128,
                                                       (wv * 4 + ic) * 1024 + jh * 512,
                                                       (wv * 4 + ic) * 1024 + (jh + 1) * 512),
                                              tag=f"SM{l}{b}{h}{w}{ic}{jh}")
                                    kb.reduce(ALU.max, stat.at(0, 128, 0, 1), smf.at(),
                                              negate=True, tag=f"SR{l}{b}{h}{w}{ic}")
                                    kb.act(AF.Exp, pb.at(), smf.at(),
                                           bias=stat.ap(0, 128, 0, 1),
                                           extra=[stat.reg(0, 128, 0, 1)],
                                           tag=f"SE{l}{b}{h}{w}{ic}")
                                    kb.reduce(ALU.add, stat.at(0, 128, 1, 2), pb.at(),
                                              tag=f"SS{l}{b}{h}{w}{ic}")
                                    kb.recip(stat.at(0, 128, 2, 3), stat.at(0, 128, 1, 2),
                                             tag=f"SV{l}{b}{h}{w}{ic}")
                                    kb.ts(pnb.at(), pb.at(), stat.ap(0, 128, 2, 3), ALU.mult,
                                          extra=[stat.reg(0, 128, 2, 3)],
                                          tag=f"SN{l}{b}{h}{w}{ic}")
                                    if (DBG.get("l0f_tapsc") and l == 0 and b == 0
                                            and h == 0 and w == 1 and ic == 0):
                                        sm_t = DR(nc, "tap_sm", [128, 1024], F32,
                                                  kind="ExternalOutput")
                                        pn_t = DR(nc, "tap_pn", [128, 1024], F32,
                                                  kind="ExternalOutput")
                                        kb.store("gpsimd", sm_t.slc((0, 128), (0, 1024)),
                                                 smf.at(), tag="tapsm")
                                        kb.copy("vector", smf.at(), pnb.at(), tag="tappncv")
                                        kb.store("gpsimd", pn_t.slc((0, 128), (0, 1024)),
                                                 smf.at(), tag="tappn")
                                    for jc in range(8):
                                        dstp = (kb.ps.h[0:128, 7 * 512 + jc * 64:
                                                        7 * 512 + (jc + 1) * 64].bitcast(BF16),
                                                kb.ps.preg(7))
                                        pe_transpose(dstp,
                                                     pnb.at(0, 128, jc * 128, (jc + 1) * 128),
                                                     tag=f"AT{l}{b}{h}{w}{ic}{jc}")
                                    dstap = bass.AP(att.h, ic * 128,
                                                    [[att.cols, 128], [512, 8], [1, 128]])
                                    srcap = kb.ps.h[0:128, 7 * 512:8 * 512].bitcast(BF16)

                                    def evfn(e, oap=dstap, iap=srcap):
                                        return e.tensor_copy(oap, iap)
                                    em.op("vector", evfn, reads=[kb.ps.preg(7)],
                                          writes=[att.reg(0, 128, 0, 8 * 512)],
                                          tag=f"AE{l}{b}{h}{w}{ic}")
                                avp = ps.bank(2 + (w % 2), 0, 512, 0, 64)
                                pairs = []
                                for jc in range(8):
                                    jcg = w * 4 + jc
                                    pairs.append(
                                        ((vt.ap(0, 128, jcg * 512 + rc * 128 + p0,
                                                jcg * 512 + rc * 128 + p0 + 64),
                                          vt.reg(0, 128, jcg * 512 + rc * 128 + p0,
                                                 jcg * 512 + rc * 128 + p0 + 64)),
                                         att.at(0, 128, jc * 512, (jc + 1) * 512)))
                                kb.mm_group(avp, pairs, tag=f"AV{l}{b}{h}{w}")
                                kb.copy("vector",
                                        o_f.at(0, 64, h * T + w * WS, h * T + (w + 1) * WS),
                                        avp, tag=f"AO{l}{b}{h}{w}")
                        if DBG.get("l0f_tapqkv") and l == 0 and b == 0:
                            of_t = DR(nc, "tap_o", [64, 8 * T], F32, kind="ExternalOutput")
                            with contextlib.ExitStack() as sT2:
                                cvt2 = SB(nc, sT2, "tapc2", 4 * T, F32, parts=64)
                                for hf in range(2):
                                    kb.copy("vector", cvt2.at(0, 64, 0, 4 * T),
                                            o_f.at(0, 64, hf * 4 * T, (hf + 1) * 4 * T), tag=f"tapo{hf}")
                                    kb.store("gpsimd", of_t.slc((0, 64), (hf * 4 * T, (hf + 1) * 4 * T)),
                                             cvt2.at(0, 64, 0, 4 * T), tag=f"tapos{hf}")
                        with contextlib.ExitStack() as so:
                            wo_bf = SB(nc, so, "fo_wo", 8 * 1024, BF16, parts=64)
                            wst = SB(nc, so, "fo_wst", 1024, F32, parts=64)
                            xsb = SB(nc, so, "fo_x", TC, F32)
                            for h in range(NH):
                                kb.load("sync", wst.at(0, 64, 0, 1024),
                                        W["l0f_wo"].slc(l, (h * 64, (h + 1) * 64), (0, 1024)),
                                        tag=f"WO{l}{b}{h}")
                                kb.copy("vector", wo_bf.at(0, 64, h * 1024, (h + 1) * 1024),
                                        wst.at(0, 64, 0, 1024), tag=f"WOc{l}{b}{h}")
                            for c in range(NCH):
                                t0 = c * TC
                                for oc in range(DK):
                                    pt = ps.bank(4 + oc % 2, 0, TC)
                                    pairs = [((wo_bf.ap(0, 64, h * 1024 + oc * 128,
                                                        h * 1024 + (oc + 1) * 128),
                                               wo_bf.reg(0, 64, h * 1024 + oc * 128,
                                                         h * 1024 + (oc + 1) * 128)),
                                              o_f.at(0, 64, h * T + t0, h * T + t0 + TC))
                                             for h in range(NH)]
                                    kb.mm_group(pt, pairs, tag=f"WOm{l}{b}{c}{oc}")
                                    kb.load("scalar", xsb.at(),
                                            ffs.slc(b, (oc * 128, (oc + 1) * 128),
                                                    (t0, t0 + TC)),
                                            tag=f"WOx{l}{b}{c}{oc}")
                                    kb.tt("vector", ALU.add, xsb.at(), pt, xsb.at(),
                                          tag=f"WOa{l}{b}{c}{oc}")
                                    kb.store("gpsimd",
                                             ffs.slc(b, (oc * 128, (oc + 1) * 128),
                                                     (t0, t0 + TC)),
                                             xsb.at(), tag=f"WOs{l}{b}{c}{oc}")

            if skip_ffn:
                continue
            with contextlib.ExitStack() as sf:
                xsh = SB(nc, sf, "ff_xsh", DK * 513, F32)
                htail = SB(nc, sf, "ff_htl", DK, F32)
                stage = SB(nc, sf, "ff_st", 2 * DK * TC, BF16)
                h2n = SB(nc, sf, "ff_h2n", DK * TC, BF16)
                gel = SB(nc, sf, "ff_gel", HPK * TC, BF16)
                w1st = SB(nc, sf, "ff_w1s", DK * 512, F32)
                w1b = SB(nc, sf, "ff_w1b", 2 * DK * 512, BF16)
                w2b = SB(nc, sf, "ff_w2b", HPK * 1024, BF16)
                w2st = SB(nc, sf, "ff_w2s", 1024, F32)
                tt1 = SB(nc, sf, "ff_t1", TC, F32)
                tt2 = SB(nc, sf, "ff_t2", TC, F32)
                for hk in range(HPK):
                    kb.load("sync", w2st.at(),
                            W["l0f_w2p"].slc(l, (hk * 128, (hk + 1) * 128), (0, 1024)),
                            tag=f"W2{l}{b}{hk}")
                    kb.copy("vector", w2b.at(0, 128, hk * 1024, (hk + 1) * 1024),
                            w2st.at(), tag=f"W2c{l}{b}{hk}")
                for c in range(NCH):
                    t0 = c * TC
                    for dk in range(DK):
                        if t0 == 0:
                            kb.memset("gpsimd", xsh.at(0, 128, dk * 513, dk * 513 + 1),
                                      0.0, tag=f"Gh{l}{b}{c}{dk}")
                        else:
                            kb.copy("gpsimd", xsh.at(0, 128, dk * 513, dk * 513 + 1),
                                    htail.at(0, 128, dk, dk + 1), tag=f"Gh{l}{b}{c}{dk}")
                        kb.load("sync", xsh.at(0, 128, dk * 513 + 1, dk * 513 + 513),
                                ffs.slc(b, (dk * 128, (dk + 1) * 128), (t0, t0 + TC)),
                                tag=f"Gx{l}{b}{c}{dk}")
                        if c < NCH - 1:
                            kb.copy("gpsimd", htail.at(0, 128, dk, dk + 1),
                                    xsh.at(0, 128, dk * 513 + 512, dk * 513 + 513),
                                    tag=f"Gt{l}{b}{c}{dk}")
                    sh = []
                    for dk in range(DK):
                        off = 1 if dk < 4 else 0
                        sh.append(xsh.at(0, 128, dk * 513 + off, dk * 513 + off + TC))
                    mub, rsb = kb.ln_stats(sh, TC, (0, 2), stage, tag=f"Gl{l}{b}{c}")
                    for dk in range(DK):
                        kb.tt("vector", ALU.add, tt1.at(), sh[dk], mub, tag=f"Ga{l}{b}{c}{dk}")
                        kb.tt("vector", ALU.mult, tt2.at(), tt1.at(), rsb,
                              tag=f"Gm{l}{b}{c}{dk}")
                        kb.ts(h2n.at(0, 128, dk * TC, (dk + 1) * TC), tt2.at(),
                              lngc.ap(0, 128, dk * 2 + l, dk * 2 + l + 1), ALU.mult,
                              lnbc.ap(0, 128, dk * 2 + l, dk * 2 + l + 1), ALU.add,
                              extra=[lngc.reg(), lnbc.reg()], tag=f"Gg{l}{b}{c}{dk}")
                    for gfirst in (True, False):
                        for oi in range(HPK):
                            oc = oi + HPK if gfirst else oi
                            wp = oi % 2
                            kb.load("sync", w1st.at(0, 128, 0, DK * 128),
                                    W["l0f_w1p"].wslice(l, D, oc * 128, (oc + 1) * 128),
                                    tag=f"G1{l}{b}{c}{oc}")
                            kb.copy("vector",
                                    w1b.at(0, 128, wp * DK * 128, wp * DK * 128 + DK * 128),
                                    w1st.at(0, 128, 0, DK * 128), tag=f"G1c{l}{b}{c}{oc}")
                            pt = ps.bank(4 + oi % 3, 0, TC)
                            pairs = [((w1b.ap(0, 128, wp * DK * 128 + dk * 128,
                                              wp * DK * 128 + (dk + 1) * 128),
                                       w1b.reg(0, 128, wp * DK * 128 + dk * 128,
                                               wp * DK * 128 + (dk + 1) * 128)),
                                      h2n.at(0, 128, dk * TC, (dk + 1) * TC))
                                     for dk in range(DK)]
                            kb.mm_group(pt, pairs, tag=f"G1m{l}{b}{c}{oc}")
                            bcol = b1c.ap(0, 128, l * 44 + oc, l * 44 + oc + 1)
                            breg = [b1c.reg()]
                            if gfirst:
                                kb.act(AF.Gelu, gel.at(0, 128, oi * TC, (oi + 1) * TC),
                                       pt, bias=bcol, extra=breg, tag=f"G1g{l}{b}{c}{oc}")
                            else:
                                kb.stt(gel.at(0, 128, oi * TC, (oi + 1) * TC), pt, bcol,
                                       gel.at(0, 128, oi * TC, (oi + 1) * TC),
                                       ALU.add, ALU.mult, extra=breg,
                                       tag=f"G1h{l}{b}{c}{oc}")
                    for oc in range(DK):
                        pt = ps.bank(4 + oc % 2, 0, TC)
                        pairs = [((w2b.ap(0, 128, hk * 1024 + oc * 128,
                                          hk * 1024 + (oc + 1) * 128),
                                   w2b.reg(0, 128, hk * 1024 + oc * 128,
                                           hk * 1024 + (oc + 1) * 128)),
                                  gel.at(0, 128, hk * TC, (hk + 1) * TC))
                                 for hk in range(HPK)]
                        kb.mm_group(pt, pairs, tag=f"G2m{l}{b}{c}{oc}")
                        kb.stt(tt1.at(), pt,
                               b2c.ap(0, 128, oc * 2 + l, oc * 2 + l + 1),
                               xsh.at(0, 128, oc * 513 + 1, oc * 513 + 1 + TC),
                               ALU.add, ALU.add, extra=[b2c.reg()],
                               tag=f"G2a{l}{b}{c}{oc}")
                        kb.store("gpsimd",
                                 ffs.slc(b, (oc * 128, (oc + 1) * 128), (t0, t0 + TC)),
                                 tt1.at(), tag=f"G2s{l}{b}{c}{oc}")

    with contextlib.ExitStack() as s9:
        f_sb = SB(nc, s9, "f9_f", DK * TC, F32)
        xrr = SB(nc, s9, "f9_x", DK * TC, F32)
        stage = SB(nc, s9, "f9_st", 2 * DK * TC, BF16)
        tt1 = SB(nc, s9, "f9_t1", TC, F32)
        tt2 = SB(nc, s9, "f9_t2", TC, F32)
        kb._lntap_row = tt2
        for b in range(B):
            for c in range(NCH):
                t0 = c * TC
                for dk in range(DK):
                    kb.load("sync", f_sb.at(0, 128, dk * TC, (dk + 1) * TC),
                            ffs.slc(b, (dk * 128, (dk + 1) * 128), (t0, t0 + TC)),
                            tag=f"E1{b}{c}{dk}")
                    kb.load("scalar", xrr.at(0, 128, dk * TC, (dk + 1) * TC),
                            x_in.slc(b, (dk * 128, (dk + 1) * 128), (t0, t0 + TC)),
                            tag=f"E2{b}{c}{dk}")
                ft = [f_sb.at(0, 128, dk * TC, (dk + 1) * TC) for dk in range(DK)]
                mub, rsb = kb.ln_stats(ft, TC, (0, 2), stage, tag=f"El{b}{c}")
                for dk in range(DK):
                    kb.tt("vector", ALU.add, tt1.at(), ft[dk], mub, tag=f"Ea{b}{c}{dk}")
                    kb.tt("vector", ALU.mult, tt2.at(), tt1.at(), rsb, tag=f"Em{b}{c}{dk}")
                    kb.tt("vector", ALU.add, xrr.at(0, 128, dk * TC, (dk + 1) * TC),
                          xrr.at(0, 128, dk * TC, (dk + 1) * TC), tt2.at(),
                          tag=f"Er{b}{c}{dk}")
                    kb.store("gpsimd",
                             x_out.slc(b, (dk * 128, (dk + 1) * 128), (t0, t0 + TC)),
                             xrr.at(0, 128, dk * TC, (dk + 1) * TC), tag=f"Es{b}{c}{dk}")




def host_prep_l1a(inp):
    out = {}
    out["l1a_wqkvT"] = _f32(np.asarray(inp["L1_attn_w_qkv"]).T)
    wc = np.asarray(inp["L1_attn_w_conv"])
    w2 = np.zeros((8, 3, 128, 128), dtype=np.float32)
    for ch in range(8):
        for tau in range(3):
            for a in range(2):
                g = 2 * ch + a
                w2[ch, tau, 64 * a:64 * a + 64, 64 * a:64 * a + 64] = \
                    wc[g * 64:(g + 1) * 64, :, tau].T
    out["l1a_wconv2"] = _f32(w2)
    bc = np.asarray(inp["L1_attn_b_conv"])
    out["l1a_bconv"] = _f32(bc.reshape(8, 128).T)
    out["l1a_nullk"] = _f32(np.asarray(inp["L1_attn_null_k"]).reshape(1024)
                            .reshape(8, 128).T)
    out["l1a_nullv"] = _f32(np.asarray(inp["L1_attn_null_v"]).reshape(1024)
                            .reshape(8, 128).T)
    out["l1a_woutT"] = _f32(np.asarray(inp["L1_attn_w_out"]).T)
    out["l1a_bout"] = _f32(np.asarray(inp["L1_attn_b_out"]).reshape(8, 128).T)
    i = np.arange(2048)[:, None]
    jc = np.arange(683)[None, :]
    msk = np.zeros((2048, 684), dtype=np.float32)
    msk[:, 1:] = np.where(i < np.minimum(3 * jc + 1, 2047), np.float32(-1e30), 0.0)
    out["l1a_mask"] = np.ascontiguousarray(msk, dtype=ml_dtypes.bfloat16)
    return out


def emit_l1a(kb, x_in, x_out, W):
    nc, st, em = kb.nc, kb.stack, kb.em
    ps = kb.ps
    NH = 16
    J = 684
    JP = 768
    NJC = 6
    SCALE = 1.0 / 32.0

    aout = DR(nc, "l1a_out", [B, D, T], F32,
              kind="ExternalOutput" if DBG.get("l1a_tap") else "Internal")

    wconv_bf = SB(nc, st, "a1_wc", 8 * 3 * 128, BF16)
    bconv = SB(nc, st, "a1_bc", 8, F32)
    nullk = SB(nc, st, "a1_nk", 8, F32)
    nullv = SB(nc, st, "a1_nv", 8, F32)
    bout = SB(nc, st, "a1_bo", 8, F32)
    with contextlib.ExitStack() as s2:
        wst = SB(nc, s2, "a1_wst", 8 * 3 * 128, F32)
        wc2 = W["l1a_wconv2"]
        wc_ap = bass.AP(wc2.t, 0, [[128, 128], [3 * 128 * 128, 8],
                                   [128 * 128, 3], [1, 128]])
        kb.load("sync", wst.at(), (wc_ap, ("D" + wc2.name, 0, 1, 0, wc2.size)),
                tag="a1wc")
        kb.copy("vector", wconv_bf.at(), wst.at(), tag="a1wcc")
    kb.load("sync", bconv.at(), W["l1a_bconv"].slc((0, 128), (0, 8)), tag="a1bc")
    kb.load("sync", nullk.at(), W["l1a_nullk"].slc((0, 128), (0, 8)), tag="a1nk")
    kb.load("sync", nullv.at(), W["l1a_nullv"].slc((0, 128), (0, 8)), tag="a1nv")
    kb.load("sync", bout.at(), W["l1a_bout"].slc((0, 128), (0, 8)), tag="a1bo")

    def pe_transpose(out_tv, in_tv, tag):
        (oap, oreg) = out_tv
        (iap, ireg) = in_tv

        def fn(e):
            return e.transpose(oap, iap, kb.ident.ap())
        em.op("tensor", fn, reads=[ireg, kb.ident.reg()], writes=[oreg], tag=tag)

    TC = 512
    NCH = T // TC
    for b in range(B):
        with contextlib.ExitStack() as sb_:
            xn_bf = SB(nc, sb_, "a1_xn", DK * T, BF16)
            q_f = SB(nc, sb_, "a1_q", 8 * T, BF16)
            kc_f = SB(nc, sb_, "a1_kc", 8 * JP, BF16)
            vt_f = SB(nc, sb_, "a1_vt", 8 * NJC * 128, BF16)
            with contextlib.ExitStack() as sq:
                kv_f = SB(nc, sq, "a1_kv", 8 * (T + 4), BF16)
                vc_f = SB(nc, sq, "a1_vc", 8 * JP, BF16)
                x_sb = SB(nc, sq, "a1_x", DK * TC, F32)
                stage = SB(nc, sq, "a1_st", 2 * DK * TC, BF16)
                tt1 = SB(nc, sq, "a1_t1", TC, F32)
                wqst = SB(nc, sq, "a1_wq", DK * 256, F32)
                qtmp = SB(nc, sq, "a1_qt", TC, BF16)
                wq_bf = SB(nc, sq, "a1_wqb", DK * 256, BF16)
                for ch in range(8):
                    kb.memset("gpsimd", kv_f.at(0, 128, ch * (T + 4), ch * (T + 4) + 1),
                              0.0, tag=f"akp{b}{ch}")
                for c in range(NCH):
                    t0 = c * TC
                    for dk in range(DK):
                        kb.load("sync", x_sb.at(0, 128, dk * TC, (dk + 1) * TC),
                                x_in.slc(b, (dk * 128, (dk + 1) * 128), (t0, t0 + TC)),
                                tag=f"Ax{b}{c}{dk}")
                    xt = [x_sb.at(0, 128, dk * TC, (dk + 1) * TC) for dk in range(DK)]
                    mub, rsb = kb.ln_stats(xt, TC, (0, 2), stage, tag=f"Aln{b}{c}")
                    for dk in range(DK):
                        kb.tt("vector", ALU.add, tt1.at(), xt[dk], mub, tag=f"Aa{b}{c}{dk}")
                        kb.tt("vector", ALU.mult,
                              xn_bf.at(0, 128, dk * T + t0, dk * T + t0 + TC),
                              tt1.at(), rsb, tag=f"Am{b}{c}{dk}")

                def qkv_part(sls, kind, tagp):
                    for sl in sls:
                        kb.load("sync", wqst.at(),
                                W["l1a_wqkvT"].wslice(None, D, sl * 256, (sl + 1) * 256),
                                tag=f"Aw{b}{tagp}{sl}")
                        for dk in range(DK):
                            kb.copy("vector",
                                    wq_bf.at(0, 128, dk * 256, (dk + 1) * 256),
                                    wqst.at(0, 128, dk * 256, (dk + 1) * 256),
                                    tag=f"Awc{b}{tagp}{sl}{dk}")
                        for oc4 in range(2):
                            oc = sl * 2 + oc4
                            ch = oc % 8
                            for tt_ in range(4):
                                pt = ps.bank(4 + (oc4 * 4 + tt_) % 4, 0, TC)
                                pairs = [((wq_bf.ap(0, 128, dk * 256 + oc4 * 128,
                                                    dk * 256 + (oc4 + 1) * 128),
                                           wq_bf.reg(0, 128, dk * 256 + oc4 * 128,
                                                     dk * 256 + (oc4 + 1) * 128)),
                                          xn_bf.at(0, 128, dk * T + tt_ * TC,
                                                   dk * T + (tt_ + 1) * TC))
                                         for dk in range(DK)]
                                kb.mm_group(pt, pairs, tag=f"Aq{b}{tagp}{oc}{tt_}")
                                if kind == 0:
                                    kb.ts(qtmp.at(), pt, SCALE, ALU.mult,
                                          tag=f"Aqe{b}{oc}{tt_}")
                                    kb.copy("gpsimd",
                                            q_f.at(0, 128, ch * T + tt_ * TC,
                                                   ch * T + (tt_ + 1) * TC),
                                            qtmp.at(), tag=f"Aqr{b}{oc}{tt_}")
                                else:
                                    kb.copy("vector",
                                            kv_f.at(0, 128, ch * (T + 4) + 1 + tt_ * TC,
                                                    ch * (T + 4) + 1 + (tt_ + 1) * TC),
                                            pt, tag=f"Ake{b}{tagp}{oc}{tt_}")

                def compress(dst_f, nullc, which):
                    for ch in range(8):
                        for jh in range(2):
                            j0 = jh * 512
                            nj = 512 if jh == 0 else 683 - 512
                            pt = ps.bank(jh, 0, nj)
                            pairs = []
                            for tau in range(3):
                                col0 = ch * (T + 4) + 3 * j0 + tau
                                rsl = kv_f.h[0:128, col0:col0 + 3 * nj]
                                rap = bass.AP(rsl.tensor, rsl.offset,
                                              [list(rsl.ap[0]), [3, nj]])
                                pairs.append(
                                    ((wconv_bf.ap(0, 128, (ch * 3 + tau) * 128,
                                                  (ch * 3 + tau + 1) * 128),
                                      wconv_bf.reg(0, 128, (ch * 3 + tau) * 128,
                                                   (ch * 3 + tau + 1) * 128)),
                                     (rap, kv_f.reg(0, 128, ch * (T + 4),
                                                    (ch + 1) * (T + 4)))))
                            kb.mm_group(pt, pairs, tag=f"Ac{b}{which}{ch}{jh}")
                            kb.ts((dst_f.ap(0, 128, ch * JP + 1 + j0,
                                            ch * JP + 1 + j0 + nj),
                                   dst_f.reg(0, 128, ch * JP + 1 + j0,
                                             ch * JP + 1 + j0 + nj)),
                                  pt, bconv.ap(0, 128, ch, ch + 1),
                                  ALU.add, extra=[bconv.reg()],
                                  tag=f"Ace{b}{which}{ch}{jh}")
                    for ch in range(8):
                        kb.copy("vector",
                                (dst_f.ap(0, 128, ch * JP, ch * JP + 1),
                                 dst_f.reg(0, 128, ch * JP, ch * JP + 1)),
                                (nullc.ap(0, 128, ch, ch + 1), nullc.reg()),
                                tag=f"Acn{b}{which}{ch}")
                        kb.memset("gpsimd",
                                  (dst_f.ap(0, 128, ch * JP + J, (ch + 1) * JP),
                                   dst_f.reg(0, 128, ch * JP + J, (ch + 1) * JP)),
                                  0.0, tag=f"Acz{b}{which}{ch}")

                qkv_part([4, 5, 6, 7], 1, "k")
                compress(kc_f, nullk, 0)
                qkv_part([8, 9, 10, 11], 1, "v")
                compress(vc_f, nullv, 1)
                qkv_part([0, 1, 2, 3], 0, "q")
                for ch in range(8):
                    for jc in range(NJC):
                        tpb = (kb.ps.h[0:128, 7 * 512:7 * 512 + 64].bitcast(BF16),
                               kb.ps.preg(7))
                        pe_transpose(tpb,
                                     (vc_f.ap(0, 128, ch * JP + jc * 128,
                                              ch * JP + (jc + 1) * 128),
                                      vc_f.reg(0, 128, ch * JP + jc * 128,
                                               ch * JP + (jc + 1) * 128)),
                                     tag=f"avt{b}{ch}{jc}")
                        kb.copy("vector",
                                vt_f.at(0, 128, (ch * NJC + jc) * 128,
                                        (ch * NJC + jc + 1) * 128),
                                tpb, tag=f"avte{b}{ch}{jc}")
            with contextlib.ExitStack() as sa:
                masks = SB(nc, sa, "a1_msk", 16 * J, BF16)
                att = SB(nc, sa, "a1_att", NJC * 512, BF16)
                smf = SB(nc, sa, "a1_sm", J, F32)
                pnb = SB(nc, sa, "a1_pn", JP, BF16)
                pbb = SB(nc, sa, "a1_pb", JP, BF16)
                stat = SB(nc, sa, "a1_stt", 4, F32)
                o_f = SB(nc, sa, "a1_o", NH * 512, BF16, parts=64)
                wout_bf = SB(nc, sa, "a1_wo", NH * 1024, BF16, parts=64)
                wst = SB(nc, sa, "a1_ws", 1024, F32, parts=64)
                ot1 = SB(nc, sa, "a1_ot", TC, F32)
                kb.memset("gpsimd", pnb.at(), 0.0, tag=f"apz{b}")
                kb.memset("gpsimd", pbb.at(), 0.0, tag=f"apz2{b}")
                for ic in range(16):
                    kb.load("sync", masks.at(0, 128, ic * J, (ic + 1) * J),
                            W["l1a_mask"].slc((ic * 128, (ic + 1) * 128), (0, J)),
                            tag=f"aml{b}{ic}")
                for h in range(NH):
                    kb.load("sync", wst.at(0, 64, 0, 1024),
                            W["l1a_woutT"].slc((h * 64, (h + 1) * 64), (0, 1024)),
                            tag=f"awo{b}{h}")
                    kb.copy("vector", wout_bf.at(0, 64, h * 1024, (h + 1) * 1024),
                            wst.at(0, 64, 0, 1024), tag=f"awoc{b}{h}")
                for ig in range(4):
                    for h in range(NH):
                        ch = h // 2
                        p0 = 64 * (h % 2)
                        for ic4 in range(4):
                            ic = ig * 4 + ic4
                            i0 = ic * 128
                            for jh in range(2):
                                j0 = jh * 512
                                nj = 512 if jh == 0 else J - 512
                                sb2 = ps.bank(jh, 0, nj)
                                kb.mm_group(
                                    sb2,
                                    [((q_f.ap(p0, p0 + 64, ch * T + i0, ch * T + i0 + 128),
                                       q_f.reg(p0, p0 + 64, ch * T + i0, ch * T + i0 + 128)),
                                      (kc_f.ap(p0, p0 + 64, ch * JP + j0, ch * JP + j0 + nj),
                                       kc_f.reg(p0, p0 + 64, ch * JP + j0, ch * JP + j0 + nj)))],
                                    tag=f"S{b}{h}{ic}{jh}")
                                kb.tt("vector", ALU.add,
                                      smf.at(0, 128, j0, j0 + nj), sb2,
                                      masks.at(0, 128, ic * J + j0, ic * J + j0 + nj),
                                      tag=f"Sm{b}{h}{ic}{jh}")
                            kb.reduce(ALU.max, stat.at(0, 128, 0, 1), smf.at(),
                                      negate=True, tag=f"Sr{b}{h}{ic}")
                            kb.act(AF.Exp, pbb.at(0, 128, 0, J), smf.at(),
                                   bias=stat.ap(0, 128, 0, 1),
                                   extra=[stat.reg(0, 128, 0, 1)], tag=f"Se{b}{h}{ic}")
                            kb.reduce(ALU.add, stat.at(0, 128, 1, 2),
                                      pbb.at(0, 128, 0, J), tag=f"Ss{b}{h}{ic}")
                            kb.recip(stat.at(0, 128, 2, 3), stat.at(0, 128, 1, 2),
                                     tag=f"Sv{b}{h}{ic}")
                            kb.ts(pnb.at(0, 128, 0, J), pbb.at(0, 128, 0, J),
                                  stat.ap(0, 128, 2, 3), ALU.mult,
                                  extra=[stat.reg(0, 128, 2, 3)], tag=f"Sn{b}{h}{ic}")
                            if DBG.get("l1a_tap") and b == 0 and h == 0 and ic == 0:
                                smt = DR(nc, "tap_sm1", [128, J], F32,
                                         kind="ExternalOutput")
                                pnt = DR(nc, "tap_pn1", [128, JP], BF16,
                                         kind="ExternalOutput")
                                qt_ = DR(nc, "tap_q1", [64, 128], BF16,
                                         kind="ExternalOutput")
                                kct = DR(nc, "tap_kc1", [64, JP], BF16,
                                         kind="ExternalOutput")
                                kb.store("gpsimd", smt.slc((0, 128), (0, J)),
                                         smf.at(), tag="t1sm")
                                kb.store("gpsimd", pnt.slc((0, 128), (0, JP)),
                                         pnb.at(), tag="t1pn")
                                kb.store("gpsimd", qt_.slc((0, 64), (0, 128)),
                                         (q_f.ap(0, 64, 0, 128),
                                          q_f.reg(0, 64, 0, 128)), tag="t1q")
                                kb.store("gpsimd", kct.slc((0, 64), (0, JP)),
                                         (kc_f.ap(0, 64, 0, JP),
                                          kc_f.reg(0, 64, 0, JP)), tag="t1kc")
                            for jc in range(NJC):
                                dstp = (kb.ps.h[0:128, 7 * 512 + jc * 64:
                                                7 * 512 + (jc + 1) * 64].bitcast(BF16),
                                        kb.ps.preg(7))
                                pe_transpose(dstp,
                                             pnb.at(0, 128, jc * 128, (jc + 1) * 128),
                                             tag=f"St{b}{h}{ic}{jc}")
                            dstap = bass.AP(att.h, ic4 * 128,
                                            [[att.cols, 128], [512, NJC], [1, 128]])
                            srcap = kb.ps.h[0:128, 7 * 512:7 * 512 + NJC * 64].bitcast(BF16)

                            def evfn(e, oap=dstap, iap=srcap):
                                return e.tensor_copy(oap, iap)
                            em.op("vector", evfn, reads=[kb.ps.preg(7)],
                                  writes=[att.reg(0, 128, 0, NJC * 512)],
                                  tag=f"Ste{b}{h}{ic}")
                        avp = ps.bank(2 + (h % 2), 0, 512, 0, 64)
                        pairs = []
                        for jc in range(NJC):
                            pairs.append(
                                ((vt_f.ap(0, 128, (ch * NJC + jc) * 128 + p0,
                                          (ch * NJC + jc) * 128 + p0 + 64),
                                  vt_f.reg(0, 128, (ch * NJC + jc) * 128 + p0,
                                           (ch * NJC + jc) * 128 + p0 + 64)),
                                 att.at(0, 128, jc * 512, (jc + 1) * 512)))
                        kb.mm_group(avp, pairs, tag=f"AV{b}{h}{ig}")
                        kb.copy("vector", o_f.at(0, 64, h * 512, (h + 1) * 512),
                                avp, tag=f"AO{b}{h}{ig}")
                    if DBG.get("l1a_tap") and b == 0 and ig == 0:
                        oft = DR(nc, "tap_of", [64, NH * 512], BF16,
                                 kind="ExternalOutput")
                        attt = DR(nc, "tap_att", [128, NJC * 512], BF16,
                                  kind="ExternalOutput")
                        kb.store("gpsimd", oft.slc((0, 64), (0, NH * 512)),
                                 o_f.at(0, 64, 0, NH * 512), tag="t2of")
                        kb.store("gpsimd", attt.slc((0, 128), (0, NJC * 512)),
                                 att.at(0, 128, 0, NJC * 512), tag="t2att")
                    t0 = ig * 512
                    for oc in range(DK):
                        pt = ps.bank(4 + oc % 2, 0, TC)
                        pairs = [((wout_bf.ap(0, 64, h2 * 1024 + oc * 128,
                                              h2 * 1024 + (oc + 1) * 128),
                                   wout_bf.reg(0, 64, h2 * 1024 + oc * 128,
                                               h2 * 1024 + (oc + 1) * 128)),
                                  o_f.at(0, 64, h2 * 512, (h2 + 1) * 512))
                                 for h2 in range(NH)]
                        kb.mm_group(pt, pairs, tag=f"AW{b}{ig}{oc}")
                        kb.ts(ot1.at(), pt, bout.ap(0, 128, oc, oc + 1), ALU.add,
                              extra=[bout.reg()], tag=f"AWb{b}{ig}{oc}")
                        kb.store("gpsimd",
                                 aout.slc(b, (oc * 128, (oc + 1) * 128), (t0, t0 + TC)),
                                 ot1.at(), tag=f"AWs{b}{ig}{oc}")

    with contextlib.ExitStack() as s9:
        f_sb = SB(nc, s9, "a9_f", DK * TC, F32)
        xrr = SB(nc, s9, "a9_x", DK * TC, F32)
        stage = SB(nc, s9, "a9_st", 2 * DK * TC, BF16)
        tt1 = SB(nc, s9, "a9_t1", TC, F32)
        tt2 = SB(nc, s9, "a9_t2", TC, F32)
        kb._lntap_row = tt2
        for b in range(B):
            for c in range(NCH):
                t0 = c * TC
                for dk in range(DK):
                    kb.load("sync", f_sb.at(0, 128, dk * TC, (dk + 1) * TC),
                            aout.slc(b, (dk * 128, (dk + 1) * 128), (t0, t0 + TC)),
                            tag=f"Z1{b}{c}{dk}")
                    kb.load("scalar", xrr.at(0, 128, dk * TC, (dk + 1) * TC),
                            x_in.slc(b, (dk * 128, (dk + 1) * 128), (t0, t0 + TC)),
                            tag=f"Z2{b}{c}{dk}")
                ft = [f_sb.at(0, 128, dk * TC, (dk + 1) * TC) for dk in range(DK)]
                mub, rsb = kb.ln_stats(ft, TC, (0, 2), stage, tag=f"Zl{b}{c}")
                for dk in range(DK):
                    kb.tt("vector", ALU.add, tt1.at(), ft[dk], mub, tag=f"Za{b}{c}{dk}")
                    kb.tt("vector", ALU.mult, tt2.at(), tt1.at(), rsb, tag=f"Zm{b}{c}{dk}")
                    kb.tt("vector", ALU.add, xrr.at(0, 128, dk * TC, (dk + 1) * TC),
                          xrr.at(0, 128, dk * TC, (dk + 1) * TC), tt2.at(),
                          tag=f"Zr{b}{c}{dk}")
                    kb.store("gpsimd",
                             x_out.slc(b, (dk * 128, (dk + 1) * 128), (t0, t0 + TC)),
                             xrr.at(0, 128, dk * TC, (dk + 1) * TC), tag=f"Zs{b}{c}{dk}")




def host_prep_l1f(inp):
    out = {}
    out["l1f_wgT"] = _f32(np.asarray(inp["L1_ffn_wg"]).T)
    out["l1f_bg"] = _f32(np.asarray(inp["L1_ffn_bg"]).reshape(1, 8))
    out["l1f_WgT"] = _f32(np.asarray(inp["L1_ffn_Wgate"]).transpose(0, 2, 1))
    out["l1f_WuT"] = _f32(np.asarray(inp["L1_ffn_Wup"]).transpose(0, 2, 1))
    out["l1f_WdT"] = _f32(np.asarray(inp["L1_ffn_Wdown"]).transpose(0, 2, 1))
    return out


def emit_l1f(kb, x_in, x_out, W):
    nc, st, em = kb.nc, kb.stack, kb.em
    ps = kb.ps
    NE = 8
    HID = 4096
    TG = 1024
    TC = 512
    NCH = T // TC

    moe = DR(nc, "l1f_moe", [B, D, T], F32)
    wrt = DR(nc, "l1f_wrt", [B, NE, T], F32,
             kind="ExternalOutput" if DBG.get("l1f_tap") else "Internal")
    xn_sp = DR(nc, "l1f_xn", [B, D, T], BF16)

    wg_bf = SB(nc, st, "m_wg", DK * 8, BF16)
    bg_bf = SB(nc, st, "m_bg", 8, BF16, parts=1)
    with contextlib.ExitStack() as s2:
        wst = SB(nc, s2, "m_wst0", DK * 8, F32)
        kb.load("sync", wst.at(), W["l1f_wgT"].wslice(None, D, 0, 8), tag="mwg")
        kb.copy("vector", wg_bf.at(), wst.at(), tag="mwgc")
        kb.load("sync", wst.at(0, 1, 0, 8), W["l1f_bg"].slc((0, 1), (0, 8)), tag="mbg")
        kb.copy("vector", bg_bf.at(0, 1, 0, 8), wst.at(0, 1, 0, 8), tag="mbgc")

    with contextlib.ExitStack() as s1:
        x_sb = SB(nc, s1, "m_x", DK * TC, F32)
        stage = SB(nc, s1, "m_st", 2 * DK * TC, BF16)
        tt1 = SB(nc, s1, "m_t1", TC, F32)
        xnc = SB(nc, s1, "m_xn", DK * TC, BF16)
        lsb = SB(nc, s1, "m_l", 8, F32)
        rsb_ = SB(nc, s1, "m_r", 16, F32)
        wsb = SB(nc, s1, "m_w", 8, F32)
        e1 = SB(nc, s1, "m_e1", 8, F32)
        e2 = SB(nc, s1, "m_e2", 8, F32)
        e3 = SB(nc, s1, "m_e3", 8, F32)
        wrow = SB(nc, s1, "m_wr", 128, F32, parts=8)
        for b in range(B):
            for c in range(NCH):
                t0 = c * TC
                for dk in range(DK):
                    kb.load("sync", x_sb.at(0, 128, dk * TC, (dk + 1) * TC),
                            x_in.slc(b, (dk * 128, (dk + 1) * 128), (t0, t0 + TC)),
                            tag=f"Mx{b}{c}{dk}")
                xt = [x_sb.at(0, 128, dk * TC, (dk + 1) * TC) for dk in range(DK)]
                mub, rsb = kb.ln_stats(xt, TC, (0, 2), stage, tag=f"Ml{b}{c}")
                for dk in range(DK):
                    kb.tt("vector", ALU.add, tt1.at(), xt[dk], mub, tag=f"Ma{b}{c}{dk}")
                    kb.tt("vector", ALU.mult, xnc.at(0, 128, dk * TC, (dk + 1) * TC),
                          tt1.at(), rsb, tag=f"Mm{b}{c}{dk}")
                    kb.store("gpsimd",
                             xn_sp.slc(b, (dk * 128, (dk + 1) * 128), (t0, t0 + TC)),
                             xnc.at(0, 128, dk * TC, (dk + 1) * TC), tag=f"Ms{b}{c}{dk}")
                for tt_ in range(4):
                    i0 = tt_ * 128
                    lp = ps.bank(4, 0, 8)
                    pairs = [((xnc.ap(0, 128, dk * TC + i0, dk * TC + i0 + 128),
                               xnc.reg(0, 128, dk * TC + i0, dk * TC + i0 + 128)),
                              wg_bf.at(0, 128, dk * 8, (dk + 1) * 8))
                             for dk in range(DK)]
                    pairs.append(((kb.ones_row.ap(0, 1, 0, 128), kb.ones_row.reg()),
                                  bg_bf.at(0, 1, 0, 8)))
                    kb.mm_group(lp, pairs, tag=f"Mr{b}{c}{tt_}")
                    kb.copy("vector", lsb.at(), lp, tag=f"Mre{b}{c}{tt_}")
                    if DBG.get("l1f_tap") and b == 0 and c == 0 and tt_ == 0:
                        lt_ = DR(nc, "tap_lg", [128, 48], F32, kind="ExternalOutput")
                        kb.store("gpsimd", lt_.slc((0, 128), (0, 8)), lsb.at(),
                                 tag="tlg")
                    kb.reduce(ALU.max, rsb_.at(0, 128, 0, 1), lsb.at(),
                              tag=f"Mm1{b}{c}{tt_}")
                    kb.reduce(ALU.max, rsb_.at(0, 128, 1, 2), lsb.at(), negate=True,
                              tag=f"Mn1{b}{c}{tt_}")
                    kb.ts(e1.at(), lsb.at(), rsb_.ap(0, 128, 0, 1), ALU.is_equal,
                          extra=[rsb_.reg(0, 128, 0, 1)], tag=f"Meq{b}{c}{tt_}")
                    kb.stt(e2.at(), e1.at(), -1e30, lsb.at(), ALU.mult, ALU.add,
                           tag=f"Ml2{b}{c}{tt_}")
                    kb.reduce(ALU.max, rsb_.at(0, 128, 2, 3), e2.at(),
                              tag=f"Mm2{b}{c}{tt_}")
                    kb.ts(e1.at(), lsb.at(), rsb_.ap(0, 128, 2, 3), ALU.is_ge,
                          extra=[rsb_.reg(0, 128, 2, 3)], tag=f"Mkp{b}{c}{tt_}")
                    kb.act(AF.Exp, e2.at(), lsb.at(), bias=rsb_.ap(0, 128, 1, 2),
                           extra=[rsb_.reg(0, 128, 1, 2)], tag=f"Mex{b}{c}{tt_}",
                           flush=True)
                    kb.tt("vector", ALU.mult, e3.at(), e1.at(), e2.at(),
                          tag=f"Mpk{b}{c}{tt_}")
                    kb.reduce(ALU.add, rsb_.at(0, 128, 3, 4), e3.at(),
                              tag=f"Msu{b}{c}{tt_}")
                    kb.recip(rsb_.at(0, 128, 4, 5), rsb_.at(0, 128, 3, 4),
                             tag=f"Mrc{b}{c}{tt_}")
                    kb.ts(wsb.at(), e3.at(), rsb_.ap(0, 128, 4, 5), ALU.mult,
                          extra=[rsb_.reg(0, 128, 4, 5)], tag=f"Mwn{b}{c}{tt_}")
                    if DBG.get("l1f_tap") and b == 0 and c == 0 and tt_ == 0:
                        kb.store("gpsimd", lt_.slc((0, 128), (8, 16)), e1.at(), tag="te1")
                        kb.store("gpsimd", lt_.slc((0, 128), (16, 24)), e2.at(), tag="te2")
                        kb.store("gpsimd", lt_.slc((0, 128), (24, 32)), e3.at(), tag="te3")
                        kb.store("gpsimd", lt_.slc((0, 128), (32, 40)), wsb.at(), tag="tw")
                        kb.store("gpsimd", lt_.slc((0, 128), (40, 45)),
                                 rsb_.at(0, 128, 0, 5), tag="tr")
                    tp = (kb.ps.h[0:8, 7 * 512:7 * 512 + 128], kb.ps.preg(7))

                    def tfn(e, oap=tp[0], iap=wsb.ap()):
                        return e.transpose(oap, iap, kb.ident_f.ap())
                    em.op("tensor", tfn, reads=[wsb.reg(), kb.ident_f.reg()],
                          writes=[kb.ps.preg(7)], tag=f"Mtp{b}{c}{tt_}")
                    kb.copy("vector", wrow.at(0, 8, 0, 128), tp, tag=f"Mte{b}{c}{tt_}")
                    wap = bass.AP(wrt.t, b * NE * T + t0 + i0, [[T, 8], [1, 128]])
                    kb.store("gpsimd",
                             (wap, ("D" + wrt.name, 0, 1, b * NE * T + t0 + i0,
                                    b * NE * T + (NE - 1) * T + t0 + i0 + 128)),
                             wrow.at(0, 8, 0, 128), tag=f"Mts{b}{c}{tt_}")

    with contextlib.ExitStack() as s3:
        xng = SB(nc, s3, "m_xng", DK * TG, BF16)
        hact = SB(nc, s3, "m_h", 32 * TG, BF16)
        wstage = SB(nc, s3, "m_ws", DK * 512, F32)
        wgb = SB(nc, s3, "m_wgb", DK * 512, BF16)
        wub = SB(nc, s3, "m_wub", DK * 512, BF16)
        sg = SB(nc, s3, "m_sg", TC, F32)
        wdst = SB(nc, s3, "m_wds", 32 * 128, F32)
        wdb = SB(nc, s3, "m_wdb", 2 * 32 * 128, BF16)
        wbc = SB(nc, s3, "m_wbc", TG, F32)
        oacc = SB(nc, s3, "m_oac", TC, F32)
        for b in range(B):
            for g in range(T // TG):
                t0 = g * TG
                for dk in range(DK):
                    kb.load("sync", xng.at(0, 128, dk * TG, (dk + 1) * TG),
                            xn_sp.slc(b, (dk * 128, (dk + 1) * 128), (t0, t0 + TG)),
                            tag=f"Gx{b}{g}{dk}")
                for e in range(NE):
                    kb.load("scalar", wbc.at(),
                            wrt.bcast(128, b, (e, e + 1), (t0, t0 + TG)),
                            tag=f"Gw{b}{g}{e}")
                    for hc in range(8):
                        kb.load("sync", wstage.at(),
                                W["l1f_WgT"].wslice(e, D, hc * 512, (hc + 1) * 512),
                                tag=f"Gg{b}{g}{e}{hc}")
                        kb.copy("vector", wgb.at(), wstage.at(), tag=f"Ggc{b}{g}{e}{hc}")
                        kb.load("sync", wstage.at(),
                                W["l1f_WuT"].wslice(e, D, hc * 512, (hc + 1) * 512),
                                tag=f"Gu{b}{g}{e}{hc}")
                        kb.copy("vector", wub.at(), wstage.at(), tag=f"Guc{b}{g}{e}{hc}")
                        for hs in range(4):
                            for tt_ in range(2):
                                bk = (hs * 2 + tt_) % 4
                                gp = ps.bank(bk, 0, TC)
                                up = ps.bank(4 + bk, 0, TC)
                                gpairs = [((wgb.ap(0, 128, dk * 512 + hs * 128,
                                                   dk * 512 + (hs + 1) * 128),
                                            wgb.reg(0, 128, dk * 512 + hs * 128,
                                                    dk * 512 + (hs + 1) * 128)),
                                           xng.at(0, 128, dk * TG + tt_ * TC,
                                                  dk * TG + (tt_ + 1) * TC))
                                          for dk in range(DK)]
                                kb.mm_group(gp, gpairs, tag=f"GG{b}{g}{e}{hc}{hs}{tt_}")
                                upairs = [((wub.ap(0, 128, dk * 512 + hs * 128,
                                                   dk * 512 + (hs + 1) * 128),
                                            wub.reg(0, 128, dk * 512 + hs * 128,
                                                    dk * 512 + (hs + 1) * 128)),
                                           xng.at(0, 128, dk * TG + tt_ * TC,
                                                  dk * TG + (tt_ + 1) * TC))
                                          for dk in range(DK)]
                                kb.mm_group(up, upairs, tag=f"GU{b}{g}{e}{hc}{hs}{tt_}")
                                kb.act(AF.Silu, (sg.ap(), sg.reg()), gp,
                                       tag=f"GS{b}{g}{e}{hc}{hs}{tt_}")
                                hrow = hc * 4 + hs
                                kb.tt("vector", ALU.mult,
                                      hact.at(0, 128, hrow * TG + tt_ * TC,
                                              hrow * TG + (tt_ + 1) * TC),
                                      up, sg.at(), tag=f"GH{b}{g}{e}{hc}{hs}{tt_}")
                    for oc in range(DK):
                        op_ = oc % 2
                        kb.load("sync", wdst.at(),
                                W["l1f_WdT"].wslice(e, HID, oc * 128, (oc + 1) * 128),
                                tag=f"Gd{b}{g}{e}{oc}")
                        kb.copy("vector", wdb.at(0, 128, op_ * 32 * 128,
                                                 (op_ + 1) * 32 * 128),
                                wdst.at(), tag=f"Gdc{b}{g}{e}{oc}")
                        for tt_ in range(2):
                            dp = ps.bank(oc % 4, 0, TC)
                            pairs = [((wdb.ap(0, 128, op_ * 32 * 128 + hk * 128,
                                              op_ * 32 * 128 + (hk + 1) * 128),
                                       wdb.reg(0, 128, op_ * 32 * 128 + hk * 128,
                                               op_ * 32 * 128 + (hk + 1) * 128)),
                                      hact.at(0, 128, hk * TG + tt_ * TC,
                                              hk * TG + (tt_ + 1) * TC))
                                     for hk in range(32)]
                            kb.mm_group(dp, pairs, tag=f"GD{b}{g}{e}{oc}{tt_}")
                            kb.tt("vector", ALU.mult, oacc.at(), dp,
                                  wbc.at(0, 128, tt_ * TC, (tt_ + 1) * TC),
                                  tag=f"GE{b}{g}{e}{oc}{tt_}")
                            kb.store("gpsimd",
                                     moe.slc(b, (oc * 128, (oc + 1) * 128),
                                             (t0 + tt_ * TC, t0 + (tt_ + 1) * TC)),
                                     oacc.at(), accum=(e > 0),
                                     tag=f"GT{b}{g}{e}{oc}{tt_}")

    with contextlib.ExitStack() as s9:
        f_sb = SB(nc, s9, "m9_f", DK * TC, F32)
        xrr = SB(nc, s9, "m9_x", DK * TC, F32)
        stage = SB(nc, s9, "m9_st", 2 * DK * TC, BF16)
        tt1 = SB(nc, s9, "m9_t1", TC, F32)
        tt2 = SB(nc, s9, "m9_t2", TC, F32)
        for b in range(B):
            for c in range(NCH):
                t0 = c * TC
                for dk in range(DK):
                    kb.load("sync", f_sb.at(0, 128, dk * TC, (dk + 1) * TC),
                            moe.slc(b, (dk * 128, (dk + 1) * 128), (t0, t0 + TC)),
                            tag=f"Y1{b}{c}{dk}")
                    kb.load("scalar", xrr.at(0, 128, dk * TC, (dk + 1) * TC),
                            x_in.slc(b, (dk * 128, (dk + 1) * 128), (t0, t0 + TC)),
                            tag=f"Y2{b}{c}{dk}")
                ft = [f_sb.at(0, 128, dk * TC, (dk + 1) * TC) for dk in range(DK)]
                mub, rsb = kb.ln_stats(ft, TC, (0, 2), stage, tag=f"Yl{b}{c}")
                for dk in range(DK):
                    kb.tt("vector", ALU.add, tt1.at(), ft[dk], mub, tag=f"Ya{b}{c}{dk}")
                    kb.tt("vector", ALU.mult, tt2.at(), tt1.at(), rsb, tag=f"Ym{b}{c}{dk}")
                    kb.tt("vector", ALU.add, xrr.at(0, 128, dk * TC, (dk + 1) * TC),
                          xrr.at(0, 128, dk * TC, (dk + 1) * TC), tt2.at(),
                          tag=f"Yr{b}{c}{dk}")
                    kb.store("gpsimd",
                             x_out.slc(b, (dk * 128, (dk + 1) * 128), (t0, t0 + TC)),
                             xrr.at(0, 128, dk * TC, (dk + 1) * TC), tag=f"Ys{b}{c}{dk}")



L0A_INPUTS = [("l0a_inwT", [D, 4096]), ("l0a_convw", [128, 64]),
              ("l0a_convb", [128, 16]), ("l0a_xprojT", [2048, 128]),
              ("l0a_dtwT", [64, 2048]), ("l0a_dtb", [128, 16]),
              ("l0a_Alog", [128, 256]), ("l0a_diagD", [128, 16 * 128]),
              ("l0a_outwT", [2048, 1024]), ("l0a_lng", [128, 8]),
              ("l0a_lnb", [128, 8])]

L0F_INPUTS = [("l0f_wqkv", [2, 1024, 1536]), ("l0f_wo", [2, 512, 1024]),
              ("l0f_qs", [128, 2]), ("l0f_ks", [128, 2]),
              ("l0f_lng", [128, 16]), ("l0f_lnb", [128, 16]),
              ("l0f_w1p", [2, 1024, 5632]), ("l0f_b1p", [128, 88]),
              ("l0f_w2p", [2, 2816, 1024]), ("l0f_b2", [128, 16]),
              ("cst_e2", [128, 2]), ("cst_e2b", [2, 128]), ("cst_rot", [128, 128]),
              ("l0f_qcs", [128, T]), ("l0f_qss", [128, T]),
              ("l0f_kcs", [128, T]), ("l0f_kss", [128, T]),
              ("l0f_mask", [2, 512, 1024])]

L1A_INPUTS = [("l1a_wqkvT", [D, 3072]), ("l1a_wconv2", [8, 3, 128, 128]),
              ("l1a_bconv", [128, 8]), ("l1a_nullk", [128, 8]),
              ("l1a_nullv", [128, 8]), ("l1a_woutT", [D, D]),
              ("l1a_bout", [128, 8]), ("l1a_mask", [T, 684], BF16)]

L1F_INPUTS = [("l1f_wgT", [D, 8]), ("l1f_bg", [1, 8]),
              ("l1f_WgT", [8, D, 4096]), ("l1f_WuT", [8, D, 4096]),
              ("l1f_WdT", [8, 4096, D])]

BLOCKS = {
    "l0a": (emit_l0a, L0A_INPUTS, host_prep_l0a),
    "l0f": (emit_l0f, L0F_INPUTS, host_prep_l0f),
    "l1a": (emit_l1a, L1A_INPUTS, host_prep_l1a),
    "l1f": (emit_l1f, L1F_INPUTS, host_prep_l1f),
}


def _build(block_names):
    nc = bass.Bass("TRN2", target_bir_lowering=False)
    stack = contextlib.ExitStack()
    em = Em(nc)
    x_in = DR(nc, "x_fm", [B, D, T], F32, kind="ExternalInput")
    x_out = DR(nc, "xres_out", [B, D, T], F32, kind="ExternalOutput")
    W = {}
    for bn in block_names:
        for ent in BLOCKS[bn][1]:
            n, s = ent[0], ent[1]
            dt = ent[2] if len(ent) > 2 else F32
            if n not in W:
                W[n] = DR(nc, n, s, dt, kind="ExternalInput")
    kb = Kb(nc, stack, em)
    cur_in = x_in
    for i, bn in enumerate(block_names):
        cur_out = x_out if i == len(block_names) - 1 else DR(
            nc, f"xmid_{i}", [B, D, T], F32)
        bst = contextlib.ExitStack()
        kb.stack = bst
        BLOCKS[bn][0](kb, cur_in, cur_out, W)
        bst.close()
        kb.stack = stack
        cur_in = cur_out
    em.finalize()
    stack.close()
    nc.finalize()
    return nc


PIPELINE = [["l0a", "l0f", "l1a", "l1f"]]


def kernel(**inputs):
    x = np.asarray(inputs["L0_attn_x"])
    Wh = {}
    seen = set()
    for grp in PIPELINE:
        for bn in grp:
            if bn not in seen:
                Wh.update(BLOCKS[bn][2](inputs))
                seen.add(bn)
    xc = [np.ascontiguousarray(x[c * B:(c + 1) * B].transpose(0, 2, 1),
                               dtype=np.float32) for c in range(NCORES)]
    for grp in PIPELINE:
        nc = _build(grp)
        need = set()
        for bn in grp:
            need.update(ent[0] for ent in BLOCKS[bn][1])
        in_maps = []
        for c in range(NCORES):
            m = {"x_fm": xc[c]}
            for n in need:
                m[n] = Wh[n]
            in_maps.append(m)
        res = run_bass_kernel_spmd(nc, in_maps, core_ids=list(range(NCORES)))
        xc = [np.asarray(res.results[c]["xres_out"]) for c in range(NCORES)]
    outs = [o.transpose(0, 2, 1) for o in xc]
    return np.ascontiguousarray(np.concatenate(outs, axis=0), dtype=np.float32)
